# version 02
# baseline (speedup 1.0000x reference)
import math
import contextlib
import numpy as np
import ml_dtypes

import concourse.bass as bass
import concourse.mybir as mybir
from concourse.bass_utils import run_bass_kernel_spmd

F32 = mybir.dt.float32
BF16 = mybir.dt.bfloat16
U8 = mybir.dt.uint8
AF = mybir.ActivationFunctionType
ALU = mybir.AluOpType
AX = mybir.AxisListType
NPBF16 = ml_dtypes.bfloat16

D = 1024
S = 2048
NB = 2
TOK = NB * S
NCH = D // 128
F = 4096
FCH = F // 128
H = 16
HD = 64
NT = S // 512
MAGIC = 12582912.0
INV2PI = float(1.0 / (2 * np.pi))
TWOPI = float(2 * np.pi)
PI = float(np.pi)
SIGC = 1.5957691216057308

ENGINES = ("sync", "gpsimd", "vector", "scalar", "tensor")
FFN_DEBUG = 0


class Sem:
    __slots__ = ("h", "n", "name")
    def __init__(self, h, name):
        self.h = h; self.n = 0; self.name = name
    def bump(self, k):
        self.n += k
        return self.n


class Prog:
    def __init__(self, nc, stack):
        self.nc = nc
        self.stack = stack
        self.q = {e: [] for e in ENGINES}
        self._sems = {}

    def sem(self, name):
        if name not in self._sems:
            h = self.stack.enter_context(self.nc.semaphore(name))
            self._sems[name] = Sem(h, name)
        return self._sems[name]

    def on(self, eng, fn, waits=(), inc=None):
        def thunk(e, fn=fn, waits=tuple(waits), inc=inc):
            for s, v in waits:
                e.wait_ge(s.h, v)
            r = fn(e)
            if inc is not None:
                s, a = inc
                r.then_inc(s.h, a)
        self.q[eng].append(thunk)

    def dma(self, eng, out, in_, waits=()):
        s = self.sem(eng + "_dma")
        tgt = s.bump(16)
        self.on(eng, lambda e: e.dma_start(out=out, in_=in_), waits, (s, 16))
        return s, tgt

    def barrier(self):
        b = self.sem("bar")
        dma_waits = [(s, s.n) for nm, s in list(self._sems.items())
                     if nm.endswith("_dma")]
        tgt = b.n + len(ENGINES)
        for eng in ENGINES:
            def thunk(e, dma_waits=tuple(dma_waits), b=b, tgt=tgt):
                for s, v in dma_waits:
                    e.wait_ge(s.h, v)
                e.sem_inc(b.h, 1)
                e.wait_ge(b.h, tgt)
            self.q[eng].append(thunk)
        b.n = tgt

    def finalize(self):
        with self.nc.Block() as blk:
            @blk.sync
            def _(e):
                for t in self.q["sync"]:
                    t(e)
            @blk.gpsimd
            def _(e):
                for t in self.q["gpsimd"]:
                    t(e)
            @blk.vector
            def _(e):
                for t in self.q["vector"]:
                    t(e)
            @blk.scalar
            def _(e):
                for t in self.q["scalar"]:
                    t(e)
            @blk.tensor
            def _(e):
                for t in self.q["tensor"]:
                    t(e)



def _pp(vec):
    return np.ascontiguousarray(np.asarray(vec, np.float32).reshape(-1, 128).T)


def prep_host(inputs):
    f32 = lambda a: np.ascontiguousarray(np.asarray(a, np.float32))
    sh = {}
    sh["l0a_wq"] = f32(inputs["L0_attn_Wq"])
    sh["l0a_wk"] = f32(inputs["L0_attn_Wk"])
    sh["l0a_wv"] = f32(inputs["L0_attn_Wv"])
    sh["l0a_wg"] = f32(inputs["L0_attn_Wg"])
    sh["l0a_wa"] = f32(inputs["L0_attn_Wa"])
    sh["l0a_wo"] = f32(inputs["L0_attn_Wo"])
    w1 = f32(inputs["L0_attn_W1"])
    w2 = f32(inputs["L0_attn_W2"])
    sh["l0a_w1m"] = f32(w1.reshape(D, FCH, 128).transpose(1, 0, 2))
    sh["l0a_w2m"] = f32(w2.reshape(F, NCH, 128).transpose(1, 0, 2))
    for nm, src in (("gns", "gn_scale"), ("gnb", "gn_bias"), ("ln1s", "ln1_scale"),
                    ("ln1b", "ln1_bias"), ("ln2s", "ln2_scale"), ("ln2b", "ln2_bias"),
                    ("b2", "b2")):
        sh[f"l0a_{nm}"] = _pp(inputs[f"L0_attn_{src}"])
    sh["l0a_b1"] = _pp(inputs["L0_attn_b1"])
    for pre, src in (("l0f", "L0_ffn"), ("l1f", "L1_ffn")):
        w1t = f32(np.asarray(inputs[f"{src}_w1_w"]).T)
        w2t = f32(np.asarray(inputs[f"{src}_w2_w"]).T)
        h1t = f32(np.asarray(inputs[f"{src}_h1_w"]).T)
        h2t = f32(np.asarray(inputs[f"{src}_h2_w"]).T)
        w3t = f32(np.asarray(inputs[f"{src}_w3_w"]).T)
        sh[f"{pre}_w1m"] = f32(w1t.reshape(D, FCH, 128).transpose(1, 0, 2))
        sh[f"{pre}_w2m"] = f32(w2t.reshape(D, FCH, 128).transpose(1, 0, 2))
        sh[f"{pre}_h1m"] = f32(h1t.reshape(F, FCH, 128).transpose(1, 0, 2))
        sh[f"{pre}_h2m"] = f32(h2t.reshape(F, FCH, 128).transpose(1, 0, 2))
        sh[f"{pre}_w3m"] = f32(w3t.reshape(F, NCH, 128).transpose(1, 0, 2))
        sh[f"{pre}_s1"] = _pp(inputs[f"{src}_w1_s"])
        sh[f"{pre}_s2"] = _pp(inputs[f"{src}_w2_s"])
        sh[f"{pre}_sh1"] = _pp(inputs[f"{src}_h1_s"])
        sh[f"{pre}_sh2"] = _pp(inputs[f"{src}_h2_s"])
        sh[f"{pre}_s3"] = _pp(inputs[f"{src}_w3_s"])
    for nm in ("wq", "wk", "wv", "wg"):
        wt = f32(np.asarray(inputs[f"L1_attn_{nm}"]).T)
        sh[f"l1a_{nm}h"] = f32(wt.reshape(D, H, HD).transpose(1, 0, 2))
        b = np.asarray(inputs[f"L1_attn_b{nm[1]}"], np.float32)
        sh[f"l1a_b{nm[1]}h"] = f32(b.reshape(H, HD))
    sh["l1a_wot"] = f32(np.asarray(inputs["L1_attn_wo"]).T)
    sh["l1a_wdt"] = f32(np.asarray(inputs["L1_attn_wd"]).T)
    sh["l1a_bo"] = _pp(inputs["L1_attn_bo"])
    sh["l1a_bd_row"] = f32(np.asarray(inputs["L1_attn_bd"]).reshape(1, H))
    sh["l1a_gamma"] = _pp(inputs["L1_attn_gamma"])
    sh["l1a_beta"] = _pp(inputs["L1_attn_beta"])
    sh["l1a_gamma_t"] = f32(np.asarray(inputs["L1_attn_gamma"]).reshape(NCH, 128))
    sh["l1a_beta_t"] = f32(np.asarray(inputs["L1_attn_beta"]).reshape(NCH, 128))
    bfa = lambda a: np.ascontiguousarray(np.asarray(a, np.float32).astype(NPBF16))
    sh["c_id128"] = bfa(np.eye(128))
    sh["c_ones1x128"] = bfa(np.ones((1, 128)))
    sh["c_onescol"] = bfa(np.ones((128, 1)))
    sh["c_ones1x128_f32"] = np.ones((1, 128), np.float32)
    sh["c_iota_row"] = np.arange(S, dtype=np.float32).reshape(1, S)
    sh["c_iota2d"] = np.ascontiguousarray(
        np.arange(128, dtype=np.float32)[:, None]
        + 128.0 * np.arange(16, dtype=np.float32)[None, :])
    inv_freq = (1.0 / 10000.0 ** (np.arange(0, HD, 2, dtype=np.float32) / HD)
                ).astype(np.float32)
    freqs = np.arange(S, dtype=np.float32)[:, None] * inv_freq[None, :]
    emb = np.concatenate([freqs, freqs], axis=-1)
    sh["c_ropecos"] = f32(np.cos(emb).T)
    sh["c_ropesin"] = f32(np.sin(emb).T)
    gsel = np.zeros((128, 4), np.float32)
    for gg in range(4):
        gsel[gg * 32:(gg + 1) * 32, gg] = 1.0
    sh["c_gsel"] = f32(gsel)
    sh["c_gexp"] = f32(gsel.T.copy())
    sh["c_diagmask"] = (np.arange(128)[:, None] > np.arange(128)[None, :]
                        ).astype(np.uint8)
    pm = np.zeros((HD, HD), np.float32)
    for i in range(HD // 2):
        pm[2 * i + 1, 2 * i] = -1.0
        pm[2 * i, 2 * i + 1] = 1.0
    sh["c_rope_perm"] = bfa(pm)
    sh["c_onescol_f32"] = np.ones((128, 1), np.float32)
    kr = np.zeros((4, S), np.float32)
    kr[0:3, :] = 1.0
    sh["c_kaugrows"] = bfa(kr)
    e3 = np.zeros((1, 12), np.float32)
    e3[0, 0] = 1.0; e3[0, 5] = 1.0; e3[0, 10] = 1.0
    sh["c_eye4"] = bfa(e3)
    cp = np.zeros((128, 8), np.float32)
    cp[:, 0] = np.pi / 2
    cp[:, 1] = 0.5
    cp[:, 2] = -5.0 * math.log(2.0)
    sh["c_pool"] = f32(cp)

    x = np.asarray(inputs["L0_attn_x"], np.float32)
    maps = []
    for cid in range(8):
        m = dict(sh)
        xc = x[2 * cid:2 * cid + 2]
        xt = np.ascontiguousarray(xc.transpose(2, 0, 1).reshape(D, TOK))
        m["x0"] = np.ascontiguousarray(xt.reshape(NCH, 128, TOK))
        maps.append(m)
    return maps



class Kern:
    def __init__(self, nc, P, stack):
        self.nc = nc
        self.P = P
        self.stack = stack
        self.g = {}
        self.c = {}

    def sb(self, stk, name, shape, dt=F32):
        return stk.enter_context(self.nc.sbuf_tensor("sb_" + name, shape, dt))

    def ps8(self, stk, tag):
        return [stk.enter_context(self.nc.psum_tensor(f"ps_{tag}_{i}", [128, 512], F32))
                for i in range(8)]


DRAM_SPECS = {
    "x0": ([NCH, 128, TOK], F32),
    "out": ([NCH, 128, TOK], F32),
    "xres": ([NCH, 128, TOK], F32),
    "xres2": ([NCH, 128, TOK], F32),
    "xln": ([NCH, 128, TOK], BF16),
    "xln2": ([NCH, 128, TOK], BF16),
    "hscr": ([FCH, 128, S], BF16),
    "hscr2": ([FCH, 128, S], BF16),
}

WEIGHT_SPECS = {
    "l0a_wq": [D, D], "l0a_wk": [D, D], "l0a_wv": [D, D], "l0a_wg": [D, D],
    "l0a_wa": [D, 2 * D], "l0a_wo": [D, D],
    "l0a_w1m": [FCH, D, 128], "l0a_w2m": [NCH, F, 128],
    "l0a_gns": [128, NCH], "l0a_gnb": [128, NCH],
    "l0a_ln1s": [128, NCH], "l0a_ln1b": [128, NCH],
    "l0a_ln2s": [128, NCH], "l0a_ln2b": [128, NCH],
    "l0a_b1": [128, FCH], "l0a_b2": [128, NCH],
    "l1a_wot": [D, D], "l1a_wdt": [D, H],
    "l1a_bo": [128, NCH], "l1a_bd_row": [1, H],
    "l1a_gamma": [128, NCH], "l1a_beta": [128, NCH],
    "l1a_gamma_t": [NCH, 128], "l1a_beta_t": [NCH, 128],
    "c_ones1x128_f32": [1, 128], "c_onescol_f32": [128, 1],
    "c_iota_row": [1, S], "c_iota2d": [128, 16],
    "c_ropecos": [HD, S], "c_ropesin": [HD, S],
    "c_gsel": [128, 4], "c_gexp": [4, 128], "c_pool": [128, 8],
}
for _pre in ("l0f", "l1f"):
    WEIGHT_SPECS[f"{_pre}_w1m"] = [FCH, D, 128]
    WEIGHT_SPECS[f"{_pre}_w2m"] = [FCH, D, 128]
    WEIGHT_SPECS[f"{_pre}_h1m"] = [FCH, F, 128]
    WEIGHT_SPECS[f"{_pre}_h2m"] = [FCH, F, 128]
    WEIGHT_SPECS[f"{_pre}_w3m"] = [NCH, F, 128]
    for _s in ("s1", "s2", "sh1", "sh2"):
        WEIGHT_SPECS[f"{_pre}_{_s}"] = [128, FCH]
    WEIGHT_SPECS[f"{_pre}_s3"] = [128, NCH]
for _nm in ("wq", "wk", "wv", "wg"):
    WEIGHT_SPECS[f"l1a_{_nm}h"] = [H, D, HD]
    WEIGHT_SPECS[f"l1a_b{_nm[1]}h"] = [H, HD]
BF16_CONSTS = {"c_id128": [128, 128], "c_ones1x128": [1, 128],
               "c_onescol": [128, 1], "c_rope_perm": [HD, HD],
               "c_kaugrows": [4, S], "c_eye4": [1, 12]}


def declare_io(K, stage_ext=None):
    nc = K.nc
    ext = stage_ext or {}
    for name, (shape, dt) in DRAM_SPECS.items():
        kind = ("Internal" if name not in ("x0", "out") else
                ("ExternalInput" if name == "x0" else "ExternalOutput"))
        if name in ext:
            kind = "ExternalInput" if ext[name] == "in" else "ExternalOutput"
        K.g[name] = nc.dram_tensor(name, shape, dt, kind=kind).ap()
    for name, shape in WEIGHT_SPECS.items():
        K.g[name] = nc.dram_tensor(name, shape, F32, kind="ExternalInput").ap()
    for name, shape in BF16_CONSTS.items():
        K.g[name] = nc.dram_tensor(name, shape, BF16, kind="ExternalInput").ap()
    K.g["c_diagmask"] = nc.dram_tensor("c_diagmask", [128, 128], U8,
                                       kind="ExternalInput").ap()


def load_consts(K, stk):
    P = K.P
    c = K.c
    for nm, shape, dt in (
            ("c_id128", [128, 128], BF16), ("c_ones1x128", [1, 128], BF16),
            ("c_onescol", [128, 1], BF16), ("c_ones1x128_f32", [1, 128], F32),
            ("c_onescol_f32", [128, 1], F32), ("c_iota2d", [128, 16], F32),
            ("c_gsel", [128, 4], F32), ("c_gexp", [4, 128], F32),
            ("c_pool", [128, 8], F32), ("c_diagmask", [128, 128], U8)):
        c[nm] = K.sb(stk, "sb" + nm, shape, dt)
        P.dma("sync", c[nm][:], K.g[nm])
    P.barrier()



class Boundary:

    def __init__(self, K, stk, tag):
        self.K = K
        self.tag = tag
        self.Bbf = K.sb(stk, f"{tag}_bbf", [128, 8 * 512], BF16)
        self.Bsq = K.sb(stk, f"{tag}_bsq", [128, 8 * 512], BF16)
        self.T1 = K.sb(stk, f"{tag}_t1", [128, 8 * 512], F32)
        self.Xr = K.sb(stk, f"{tag}_xr", [128, 8 * 512], F32)
        self.rows = K.sb(stk, f"{tag}_rows", [1, 512 * 4], F32)
        self.war_bbf = None

    def row(self, i):
        return self.rows[0:1, 512 * i:512 * (i + 1)]

    def ln_rows(self, pSa, pSb, nfeat, eps, after_te):
        K, P = self.K, self.K.P
        sV, sS = P.sem("v"), P.sem("s")
        mu, r, nmr, va = self.row(0), self.row(1), self.row(2), self.row(3)
        inv = 1.0 / nfeat
        tV = sV.bump(1)
        P.on("vector", lambda e: (
            e.tensor_scalar(out=mu, in0=pSa, scalar1=inv, scalar2=None, op0=ALU.mult),
            e.tensor_scalar(out=va, in0=pSb, scalar1=inv, scalar2=None, op0=ALU.mult),
            e.tensor_tensor(out=nmr, in0=mu, in1=mu, op=ALU.mult),
            e.tensor_tensor(out=va, in0=va, in1=nmr, op=ALU.subtract),
            e.tensor_scalar(out=va, in0=va, scalar1=eps, scalar2=None, op0=ALU.add),
        )[-1], waits=[after_te], inc=(sV, 1))
        tS = sS.bump(1)
        P.on("scalar", lambda e: (
            e.activation(out=r, in_=va, func=AF.Ln),
            e.activation(out=r, in_=r, func=AF.Exp, scale=-0.5),
        )[-1], waits=[(sV, tV)], inc=(sS, 1))
        tV2 = sV.bump(1)
        P.on("vector", lambda e: (
            e.tensor_tensor(out=nmr, in0=mu, in1=r, op=ALU.mult),
            e.tensor_scalar(out=nmr, in0=nmr, scalar1=-1.0, scalar2=None,
                            op0=ALU.mult),
        )[-1], waits=[(sS, tS)], inc=(sV, 1))
        return tV2

    def stats(self, src_tiles, ps, after):
        K, P = self.K, self.K.P
        sP, sS, sT = P.sem("p"), P.sem("s"), P.sem("t")
        guard = [after]
        if self.war_bbf is not None:
            guard.append(self.war_bbf)
        n = len(src_tiles)
        tP = sP.bump(1)
        def cast_all(e):
            r = None
            for m in range(n):
                r = e.tensor_copy(out=self.Bbf[:, 512 * m:512 * (m + 1)],
                                  in_=src_tiles[m])
            return r
        P.on("gpsimd", cast_all, waits=guard, inc=(sP, 1))
        tS = sS.bump(1)
        def sq_all(e):
            r = None
            for m in range(n):
                r = e.activation(out=self.Bsq[:, 512 * m:512 * (m + 1)],
                                 in_=src_tiles[m], func=AF.Square)
            return r
        P.on("scalar", sq_all, waits=guard, inc=(sS, 1))
        oc = K.c["c_onescol"]
        tT = sT.bump(1)
        def sums(e):
            r = None
            for m in range(n):
                e.matmul(ps[0][0:1, :], oc[:], self.Bbf[:, 512 * m:512 * (m + 1)],
                         start=(m == 0), stop=(m == n - 1))
            for m in range(n):
                r = e.matmul(ps[1][0:1, :], oc[:], self.Bsq[:, 512 * m:512 * (m + 1)],
                             start=(m == 0), stop=(m == n - 1))
            return r
        P.on("tensor", sums, waits=[(sP, tP), (sS, tS)], inc=(sT, 1))
        return (sT, tT)

    def bcast(self, ps_c1, ps_c2, after_rows):
        K, P = self.K, self.K.P
        sT = P.sem("t")
        o1 = K.c["c_ones1x128_f32"]
        tT = sT.bump(1)
        P.on("tensor", lambda e: (
            e.matmul(ps_c1[:], o1[:], self.row(1), start=True, stop=True),
            e.matmul(ps_c2[:], o1[:], self.row(2), start=True, stop=True),
        )[-1], waits=[(P.sem("v"), after_rows)], inc=(sT, 1))
        return (sT, tT)

    def run_tt(self, b, tt, Bsrc, after_B, ps, resid_in, res_out, ln_out,
               affine=None, final=False, eps_pre=1e-6):
        K, P = self.K, self.K.P
        g = K.g
        tok0 = b * S + tt * 512
        sV = P.sem("v")
        sAd = P.sem("scalar_dma")
        cur = Bsrc
        cur_guard = after_B
        if affine is not None:
            st = self.stats(cur, ps, cur_guard)
            rv = self.ln_rows(ps[0][0:1, :], ps[1][0:1, :], D, eps_pre, st)
            bt = self.bcast(ps[2], ps[3], rv)
            spp, bpp = affine
            tV = sV.bump(1)
            def apply_aff(e, cur=cur):
                r = None
                for m in range(8):
                    t1 = self.T1[:, 512 * m:512 * (m + 1)]
                    e.tensor_tensor(out=t1, in0=cur[m], in1=ps[2][:], op=ALU.mult)
                    e.tensor_tensor(out=t1, in0=t1, in1=ps[3][:], op=ALU.add)
                    r = e.tensor_scalar(out=t1, in0=t1, scalar1=spp[:, m:m + 1],
                                        scalar2=bpp[:, m:m + 1], op0=ALU.mult,
                                        op1=ALU.add)
                return r
            P.on("vector", apply_aff, waits=[cur_guard, bt], inc=(sV, 1))
            cur = [self.T1[:, 512 * m:512 * (m + 1)] for m in range(8)]
            cur_guard = (sV, tV)
        st = self.stats(cur, ps, cur_guard)
        rv = self.ln_rows(ps[0][0:1, :], ps[1][0:1, :], D, 1e-5, st)
        bt = self.bcast(ps[4], ps[5], rv)
        warx = [(sV, sV.n)]
        for m in range(8):
            P.dma("scalar", self.Xr[:, 512 * m:512 * (m + 1)],
                  g[resid_in][m, :, tok0:tok0 + 512], waits=warx)
        ld_tgt = sAd.n
        tV = sV.bump(1)
        def apply_res(e, cur=cur):
            r = None
            for m in range(8):
                dst = self.T1[:, 512 * m:512 * (m + 1)]
                e.tensor_tensor(out=dst, in0=cur[m], in1=ps[4][:], op=ALU.mult)
                e.tensor_tensor(out=dst, in0=dst, in1=ps[5][:], op=ALU.add)
                r = e.tensor_tensor(out=dst, in0=dst,
                                    in1=self.Xr[:, 512 * m:512 * (m + 1)],
                                    op=ALU.add)
            return r
        P.on("vector", apply_res, waits=[cur_guard, bt, (sAd, ld_tgt)], inc=(sV, 1))
        xnew = [self.T1[:, 512 * m:512 * (m + 1)] for m in range(8)]
        out_name = "out" if final else res_out
        for m in range(8):
            P.dma("scalar", g[out_name][m, :, tok0:tok0 + 512], xnew[m],
                  waits=[(sV, tV)])
        if final:
            return
        st = self.stats(xnew, ps, (sV, tV))
        rv = self.ln_rows(ps[0][0:1, :], ps[1][0:1, :], D, 1e-5, st)
        bt = self.bcast(ps[6], ps[7], rv)
        tV2 = sV.bump(1)
        def apply_ln(e):
            r = None
            for m in range(8):
                xr = self.Xr[:, 512 * m:512 * (m + 1)]
                e.tensor_tensor(out=xr, in0=xnew[m], in1=ps[6][:], op=ALU.mult)
                r = e.tensor_tensor(out=self.Bbf[:, 512 * m:512 * (m + 1)],
                                    in0=xr, in1=ps[7][:], op=ALU.add)
            return r
        P.on("vector", apply_ln,
             waits=[(sV, tV), bt] + ([self.war_bbf] if self.war_bbf else []),
             inc=(sV, 1))
        for m in range(8):
            P.dma("scalar", g[ln_out][m, :, tok0:tok0 + 512],
                  self.Bbf[:, 512 * m:512 * (m + 1)], waits=[(sV, tV2)])
        self.war_bbf = (P.sem("scalar_dma"), P.sem("scalar_dma").n)



def phase_ln0(K):
    P = K.P
    g = K.g
    with contextlib.ExitStack() as stk:
        ps = K.ps8(stk, "ln0")
        bd = Boundary(K, stk, "ln0")
        X = K.sb(stk, "ln0_x", [128, 8 * 512], F32)
        sA = P.sem("sync_dma")
        sV = P.sem("v")
        for b in range(NB):
            for tt in range(NT):
                tok0 = b * S + tt * 512
                war = [(sV, sV.n)]
                for m in range(8):
                    P.dma("sync", X[:, 512 * m:512 * (m + 1)],
                          g["x0"][m, :, tok0:tok0 + 512], waits=war)
                ldt = sA.n
                xt = [X[:, 512 * m:512 * (m + 1)] for m in range(8)]
                st = bd.stats(xt, ps, (sA, ldt))
                rv = bd.ln_rows(ps[0][0:1, :], ps[1][0:1, :], D, 1e-5, st)
                bt = bd.bcast(ps[2], ps[3], rv)
                tV = sV.bump(1)
                def apply_ln(e, xt=xt):
                    r = None
                    for m in range(8):
                        t1 = bd.T1[:, 512 * m:512 * (m + 1)]
                        e.tensor_tensor(out=t1, in0=xt[m], in1=ps[2][:], op=ALU.mult)
                        r = e.tensor_tensor(out=bd.Bsq[:, 512 * m:512 * (m + 1)],
                                            in0=t1, in1=ps[3][:], op=ALU.add)
                    return r
                P.on("vector", apply_ln, waits=[bt], inc=(sV, 1))
                for m in range(8):
                    P.dma("sync", g["xln"][m, :, tok0:tok0 + 512],
                          bd.Bsq[:, 512 * m:512 * (m + 1)], waits=[(sV, tV)])
        P.barrier()



def phase_ffn(K, pre, resid_in, xln_in, res_out, ln_out, final=False):
    P = K.P
    g = K.g
    with contextlib.ExitStack() as svk:
        svec = {}
        for snm, w in (("s1", FCH), ("s2", FCH), ("sh1", FCH), ("sh2", FCH),
                       ("s3", NCH)):
            svec[snm] = K.sb(svk, f"{pre}_{snm}", [128, w], F32)
            P.dma("sync", svec[snm][:], g[f"{pre}_{snm}"])
        sW = P.sem("gpsimd_dma")
        sA = P.sem("sync_dma")
        sT = P.sem("t")
        sV = P.sem("v")
        sS = P.sem("s")
        war = {}
        P.barrier()

        for b in range(NB):
            tok0 = b * S
            with contextlib.ExitStack() as hstk:
                hid = K.sb(hstk, f"{pre}_hid{b}", [128, FCH * S], BF16)
                self_dummy = None
                stk = contextlib.ExitStack()
                ps = K.ps8(stk, f"{pre}a{b}")
                xlnb = K.sb(stk, f"{pre}_xln{b}", [128, NCH * S], BF16)
                wa = [K.sb(stk, f"{pre}_wa{i}{b}", [128, NCH * 128], BF16)
                      for i in range(4)]
                sil = K.sb(stk, f"{pre}_sil{b}", [128, 4 * 512], F32)
                for m in range(8):
                    P.dma("sync", xlnb[:, m * S:(m + 1) * S],
                          g[xln_in][m, :, tok0:tok0 + S], waits=[(sT, sT.n)])
                xln_tgt = sA.n
                for m in range(FCH):
                    wb1, wb2 = wa[(m % 2) * 2], wa[(m % 2) * 2 + 1]
                    for wi, wnm in ((0, "w1m"), (1, "w2m")):
                        wk_key = ("wa", b, (m % 2) * 2 + wi)
                        P.dma("gpsimd",
                              (wb1 if wi == 0 else wb2)[:]
                              .rearrange("p (k j) -> p k j", k=NCH),
                              g[f"{pre}_{wnm}"][m].rearrange("(k p) j -> p k j", p=128),
                              waits=[war[wk_key]] if wk_key in war else ())
                    w_tgt = sW.n
                    def mm1(e, wb1=wb1):
                        r = None
                        for tt in range(4):
                            for k in range(NCH):
                                r = e.matmul(
                                    ps[tt][:], wb1[:, k * 128:(k + 1) * 128],
                                    xlnb[:, k * S + tt * 512:k * S + (tt + 1) * 512],
                                    start=(k == 0), stop=(k == NCH - 1))
                        return r
                    t_mm1 = sT.bump(1)
                    P.on("tensor", mm1,
                         waits=[(sW, w_tgt), (sA, xln_tgt), (sV, sV.n), (sS, sS.n)],
                         inc=(sT, 1))
                    war[("wa", b, (m % 2) * 2)] = (sT, t_mm1)
                    t_sil = sS.bump(1)
                    def ev1(e, m=m):
                        r = None
                        for tt in range(4):
                            r = e.activation(out=sil[:, tt * 512:(tt + 1) * 512],
                                             in_=ps[tt][:], func=AF.Silu,
                                             scale=svec["s1"][:, m:m + 1])
                        return r
                    P.on("scalar", ev1, waits=[(sT, t_mm1)], inc=(sS, 1))
                    def mm2(e, wb2=wb2):
                        r = None
                        for tt in range(4):
                            for k in range(NCH):
                                r = e.matmul(
                                    ps[4 + tt][:], wb2[:, k * 128:(k + 1) * 128],
                                    xlnb[:, k * S + tt * 512:k * S + (tt + 1) * 512],
                                    start=(k == 0), stop=(k == NCH - 1))
                        return r
                    t_mm2 = sT.bump(1)
                    P.on("tensor", mm2, waits=[(sV, sV.n)], inc=(sT, 1))
                    war[("wa", b, (m % 2) * 2 + 1)] = (sT, t_mm2)
                    t_u = sV.bump(1)
                    def ev2(e, m=m):
                        r = None
                        for tt in range(4):
                            r = e.scalar_tensor_tensor(
                                out=hid[:, m * S + tt * 512:m * S + (tt + 1) * 512],
                                in0=ps[4 + tt][:], scalar=svec["s2"][:, m:m + 1],
                                in1=sil[:, tt * 512:(tt + 1) * 512],
                                op0=ALU.mult, op1=ALU.mult)
                        return r
                    P.on("vector", ev2, waits=[(sT, t_mm2), (sS, t_sil)],
                         inc=(sV, 1))
                if FFN_DEBUG == 1:
                    for kk in range(FCH):
                        P.dma("sync", g["hscr2"][kk], hid[:, kk * S:(kk + 1) * S],
                              waits=[(sV, sV.n)] if kk == 0 else ())
                P.barrier()
                stk.close()
                stk = contextlib.ExitStack()
                ps = K.ps8(stk, f"{pre}bc{b}")
                wtile = [K.sb(stk, f"{pre}_wt{i}{b}", [128, FCH * 128], BF16)
                         for i in range(2)]
                vstg = [K.sb(stk, f"{pre}_vs{i}{b}", [128, S], BF16)
                        for i in range(2)]
                stages_bc = (("B", "h1m", "sh1", None, "hscr"),) if FFN_DEBUG == 1 else (
                    ("B", "h1m", "sh1", None, "hscr"),
                    ("C", "h2m", "sh2", "hscr", "hscr2"))
                for stage, wnm, snm, src_dram, dst_dram in stages_bc:
                    if src_dram is not None:
                        for kk in range(FCH):
                            P.dma("sync", hid[:, kk * S:(kk + 1) * S],
                                  g[src_dram][kk],
                                  waits=[(sT, sT.n), (sA, sA.n)] if kk == 0 else ())
                    hid_tgt = sA.n
                    for m in range(FCH):
                        wb = wtile[m % 2]
                        wt_key = ("wt", b, m % 2)
                        P.dma("gpsimd", wb[:].rearrange("p (k j) -> p k j", k=FCH),
                              g[f"{pre}_{wnm}"][m].rearrange("(k p) j -> p k j", p=128),
                              waits=[war[wt_key]] if wt_key in war else ())
                        w_tgt = sW.n
                        def mmh(e, wb=wb, m=m):
                            r = None
                            for k in range(FCH):
                                for tt in range(4):
                                    r = e.matmul(
                                        ps[(m % 2) * 4 + tt][:],
                                        wb[:, k * 128:(k + 1) * 128],
                                        hid[:, k * S + tt * 512:k * S + (tt + 1) * 512],
                                        start=(k == 0), stop=(k == FCH - 1))
                            return r
                        t_mm = sT.bump(1)
                        P.on("tensor", mmh,
                             waits=[(sW, w_tgt), (sA, hid_tgt), (sS, sS.n)],
                             inc=(sT, 1))
                        war[("wt", b, m % 2)] = (sT, t_mm)
                        t_g = sS.bump(1)
                        vs = vstg[m % 2]
                        def evg(e, m=m, vs=vs, snm=snm):
                            r = None
                            for tt in range(4):
                                r = e.activation(
                                    out=vs[:, tt * 512:(tt + 1) * 512],
                                    in_=ps[(m % 2) * 4 + tt][:], func=AF.Gelu,
                                    scale=svec[snm][:, m:m + 1])
                            return r
                        P.on("scalar", evg, waits=[(sT, t_mm), (sA, sA.n)],
                             inc=(sS, 1))
                        P.dma("sync", g[dst_dram][m], vs[:], waits=[(sS, t_g)])
                P.barrier()
                stk.close()
            if FFN_DEBUG:
                continue
            with contextlib.ExitStack() as stk:
                ps = K.ps8(stk, f"{pre}d{b}")
                w3sb = K.sb(stk, f"{pre}_w3sb{b}", [128, NCH * FCH * 128], BF16)
                wslab = [K.sb(stk, f"{pre}_ws{i}{b}", [128, 512], BF16)
                         for i in range(2)]
                Bout = K.sb(stk, f"{pre}_bo{b}", [128, 8 * 512], F32)
                bd = Boundary(K, stk, f"{pre}_bd{b}")
                for m in range(NCH):
                    P.dma("gpsimd",
                          w3sb[:].rearrange("p (k m j) -> p k m j", k=FCH, m=NCH)
                          [:, :, m, :],
                          g[f"{pre}_w3m"][m].rearrange("(k p) j -> p k j", p=128))
                w3_tgt = sW.n
                for tt in range(4):
                    for k in range(FCH):
                        wsl = wslab[k % 2]
                        ws_key = ("wsl", pre, b, k % 2)
                        P.dma("sync", wsl[:],
                              g["hscr2"][k][:, tt * 512:(tt + 1) * 512],
                              waits=[war[ws_key]] if ws_key in war else ())
                        sl_tgt = sA.n
                        waits = [(sA, sl_tgt), (sW, w3_tgt)]
                        if k == 0:
                            waits.append((sV, sV.n))
                        def mmd(e, wsl=wsl, k=k):
                            r = None
                            for m in range(NCH):
                                r = e.matmul(
                                    ps[m][:],
                                    w3sb[:, (k * NCH + m) * 128:(k * NCH + m + 1) * 128],
                                    wsl[:], start=(k == 0), stop=(k == FCH - 1))
                            return r
                        t_mm = sT.bump(1)
                        P.on("tensor", mmd, waits=waits, inc=(sT, 1))
                        war[ws_key] = (sT, t_mm)
                    t_ev = sV.bump(1)
                    def evd(e):
                        r = None
                        for m in range(NCH):
                            r = e.tensor_scalar(
                                out=Bout[:, m * 512:(m + 1) * 512], in0=ps[m][:],
                                scalar1=svec["s3"][:, m:m + 1], scalar2=None,
                                op0=ALU.mult)
                        return r
                    P.on("vector", evd, waits=[(sT, sT.n)], inc=(sV, 1))
                    Bt = [Bout[:, m * 512:(m + 1) * 512] for m in range(NCH)]
                    bd.run_tt(b, tt, Bt, (sV, t_ev), ps, resid_in, res_out,
                              ln_out, affine=None, final=final)
                P.barrier()
    P.barrier()


def phase_l0attn(K, resid_in, xln_in, res_out, ln_out):
    P = K.P
    g = K.g
    c = K.c
    sW = P.sem("gpsimd_dma")
    sA = P.sem("sync_dma")
    sT = P.sem("t")
    sV = P.sem("v")
    sS = P.sem("s")
    war = {}
    GEPS = 1e-6
    GINV = 1.0 / (S * 32)

    with contextlib.ExitStack() as pstk:
        vec = {}
        for nm in ("gns", "gnb", "ln1s", "ln1b", "ln2s", "ln2b", "b2"):
            vec[nm] = K.sb(pstk, f"l0a_{nm}v", [128, NCH], F32)
            P.dma("sync", vec[nm][:], g[f"l0a_{nm}"])
        vec["b1"] = K.sb(pstk, "l0a_b1v", [128, FCH], F32)
        P.dma("sync", vec["b1"][:], g["l0a_b1"])
        P.barrier()

        for b in range(NB):
            tok0 = b * S
            with contextlib.ExitStack() as bstk:
                zb = K.sb(bstk, f"l0a_z{b}", [128, NCH * S], BF16)
                with contextlib.ExitStack() as stk:
                    ps = K.ps8(stk, f"l0ap{b}")
                    xlnb = K.sb(stk, f"l0a_xln{b}", [128, NCH * S], BF16)
                    T = {nm: K.sb(stk, f"l0a_{nm}{b}", [128, S], F32)
                         for nm in ("tq", "tk", "tv", "tar", "tai", "tg",
                                    "sc1", "sc2", "sc3")}
                    wsl = [K.sb(stk, f"l0a_ws{i}{b}", [128, NCH * 128], BF16)
                           for i in range(12)]
                    Sg = K.sb(stk, f"l0a_sg{b}", [128, 8], F32)
                    Sg2 = K.sb(stk, f"l0a_sg2{b}", [8, 8], F32)
                    lnbf = K.sb(stk, f"l0a_lbf{b}", [128, 8 * 512], BF16)
                    lnsq = K.sb(stk, f"l0a_lsq{b}", [128, 8 * 512], BF16)
                    rows = K.sb(stk, f"l0a_rows{b}", [1, 4 * 512], F32)
                    for m in range(8):
                        P.dma("sync", xlnb[:, m * S:(m + 1) * S],
                              g[xln_in][m, :, tok0:tok0 + S],
                              waits=[(sT, sT.n)])
                    xln_tgt = sA.n
                    pool_pi2 = c["c_pool"][:, 0:1]
                    pool_half = c["c_pool"][:, 1:2]

                    for ch in range(NCH):
                        wset = wsl[(ch % 2) * 6:(ch % 2) * 6 + 6]
                        srcs = [("l0a_wq", ch * 128), ("l0a_wk", ch * 128),
                                ("l0a_wv", ch * 128), ("l0a_wg", ch * 128),
                                ("l0a_wa", ch * 128), ("l0a_wa", D + ch * 128)]
                        wq_key = ("l0aw", b, ch % 2)
                        for wi, (wn, off) in enumerate(srcs):
                            P.dma("gpsimd",
                                  wset[wi][:].rearrange("p (k j) -> p k j", k=NCH),
                                  g[wn][:, off:off + 128]
                                  .rearrange("(k p) j -> p k j", p=128),
                                  waits=([war[wq_key]] if wq_key in war else ())
                                  if wi == 0 else ())
                        w_tgt = sW.n
                        quants = [("tq", 0, "dve"), ("tk", 1, "dve"),
                                  ("tv", 2, "dve"), ("tar", 4, "act"),
                                  ("tai", 5, "act"), ("tg", 3, "silu")]
                        for qi, (dnm, wi, ev) in enumerate(quants):
                            bank = (qi % 2) * 4
                            def mmq(e, wi=wi, bank=bank, wset=wset):
                                r = None
                                for tt in range(4):
                                    for k in range(NCH):
                                        r = e.matmul(
                                            ps[bank + tt][:],
                                            wset[wi][:, k * 128:(k + 1) * 128],
                                            xlnb[:, k * S + tt * 512:
                                                 k * S + (tt + 1) * 512],
                                            start=(k == 0), stop=(k == NCH - 1))
                                return r
                            t_mm = sT.bump(1)
                            P.on("tensor", mmq,
                                 waits=[(sW, w_tgt), (sA, xln_tgt),
                                        (sV, sV.n), (sS, sS.n)],
                                 inc=(sT, 1))
                            war[("l0aw", b, ch % 2)] = (sT, t_mm)
                            if ev == "dve":
                                t_e = sV.bump(1)
                                def evq(e, dnm=dnm, bank=bank):
                                    r = None
                                    for tt in range(4):
                                        r = e.tensor_copy(
                                            out=T[dnm][:, tt * 512:(tt + 1) * 512],
                                            in_=ps[bank + tt][:])
                                    return r
                                P.on("vector", evq, waits=[(sT, t_mm)], inc=(sV, 1))
                            else:
                                fn = AF.Silu if ev == "silu" else AF.Identity
                                t_e = sS.bump(1)
                                def evq(e, dnm=dnm, bank=bank, fn=fn):
                                    r = None
                                    for tt in range(4):
                                        r = e.activation(
                                            out=T[dnm][:, tt * 512:(tt + 1) * 512],
                                            in_=ps[bank + tt][:], func=fn)
                                    return r
                                P.on("scalar", evq, waits=[(sT, t_mm), (sV, sV.n)],
                                     inc=(sS, 1))
                        evac_v = sV.n
                        evac_s = sS.n

                        def dve(fn, waits=()):
                            t = sV.bump(1)
                            P.on("vector", fn, waits=waits, inc=(sV, 1))
                            return t

                        def act(fn, waits=()):
                            t = sS.bump(1)
                            P.on("scalar", fn, waits=waits, inc=(sS, 1))
                            return t

                        tq, tk, tv = T["tq"][:], T["tk"][:], T["tv"][:]
                        tar, tai, tg = T["tar"][:], T["tai"][:], T["tg"][:]
                        s1, s2, s3 = T["sc1"][:], T["sc2"][:], T["sc3"][:]
                        dve(lambda e: (
                            e.tensor_tensor(out=tk, in0=tk, in1=tv, op=ALU.mult),
                            e.tensor_tensor(out=s1, in0=tar, in1=tar, op=ALU.mult),
                            e.tensor_tensor(out=s2, in0=tai, in1=tai, op=ALU.mult),
                            e.tensor_tensor(out=s1, in0=s1, in1=s2, op=ALU.add),
                        )[-1], waits=[(sS, evac_s)])
                        act(lambda e: (
                            e.activation(out=tv, in_=s1, func=AF.Sqrt),
                            e.activation(out=tv, in_=tv, func=AF.Sigmoid),
                            e.activation(out=s2, in_=tar, func=AF.Sign),
                            e.activation(out=s2, in_=s2, func=AF.Sign, bias=pool_half),
                            e.activation(out=s1, in_=tar, func=AF.Abs),
                        )[-1], waits=[(sV, sV.n)])
                        dve(lambda e: (
                            e.tensor_scalar(out=s1, in0=s1, scalar1=1e-30,
                                            scalar2=None, op0=ALU.max),
                        )[-1], waits=[(sS, sS.n)])
                        act(lambda e: (
                            e.activation(out=s1, in_=s1, func=AF.Ln),
                            e.activation(out=s1, in_=s1, func=AF.Exp, scale=-1.0),
                        )[-1], waits=[(sV, sV.n)])
                        dve(lambda e: (
                            e.tensor_tensor(out=s1, in0=s1, in1=s2, op=ALU.mult),
                            e.tensor_tensor(out=s1, in0=tai, in1=s1, op=ALU.mult),
                        )[-1], waits=[(sS, sS.n)])
                        act(lambda e: (
                            e.activation(out=s1, in_=s1, func=AF.Arctan),
                            e.activation(out=tar, in_=tai, func=AF.Sign),
                        )[-1], waits=[(sV, sV.n)])
                        dve(lambda e: (
                            e.tensor_scalar(out=s3, in0=s2, scalar1=-PI / 2,
                                            scalar2=PI / 2, op0=ALU.mult, op1=ALU.add),
                            e.tensor_tensor(out=s3, in0=s3, in1=tar, op=ALU.mult),
                            e.tensor_tensor(out=s1, in0=s1, in1=s3, op=ALU.add),
                            e.tensor_tensor_scan(out=tai, data0=s1, data1=s1,
                                                 initial=0.0, op0=ALU.add,
                                                 op1=ALU.bypass),
                            e.tensor_scalar(out=s2, in0=tai, scalar1=INV2PI,
                                            scalar2=None, op0=ALU.mult),
                            e.tensor_scalar(out=s3, in0=s2, scalar1=MAGIC,
                                            scalar2=MAGIC, op0=ALU.add,
                                            op1=ALU.subtract),
                            e.scalar_tensor_tensor(out=s3, in0=s3, scalar=-TWOPI,
                                                   in1=tai, op0=ALU.mult,
                                                   op1=ALU.add),
                            e.tensor_scalar(out=s2, in0=s2, scalar1=0.25,
                                            scalar2=None, op0=ALU.add),
                            e.tensor_scalar(out=s2, in0=s2, scalar1=MAGIC,
                                            scalar2=MAGIC, op0=ALU.add,
                                            op1=ALU.subtract),
                            e.scalar_tensor_tensor(out=s2, in0=s2, scalar=-TWOPI,
                                                   in1=tai, op0=ALU.mult,
                                                   op1=ALU.add),
                        )[-1], waits=[(sS, sS.n)])
                        act(lambda e: (
                            e.activation(out=s3, in_=s3, func=AF.Sin),
                            e.activation(out=s2, in_=s2, func=AF.Sin, bias=pool_pi2),
                        )[-1], waits=[(sV, sV.n)])
                        t_chain = dve(lambda e: (
                            e.tensor_tensor(out=tar, in0=s2, in1=tk, op=ALU.mult),
                            e.tensor_tensor(out=tai, in0=s3, in1=tk, op=ALU.mult),
                            e.tensor_tensor_scan(out=tk, data0=tv, data1=tar,
                                                 initial=0.0, op0=ALU.mult,
                                                 op1=ALU.add),
                            e.tensor_tensor_scan(out=s1, data0=tv, data1=tai,
                                                 initial=0.0, op0=ALU.mult,
                                                 op1=ALU.add),
                            e.tensor_tensor(out=tar, in0=s2, in1=tk, op=ALU.mult),
                            e.tensor_tensor(out=tai, in0=s3, in1=s1, op=ALU.mult),
                            e.tensor_tensor(out=tar, in0=tar, in1=tai, op=ALU.add),
                            e.tensor_tensor(out=tar, in0=tq, in1=tar, op=ALU.mult),
                            e.tensor_tensor(out=tai, in0=s3, in1=tk, op=ALU.mult),
                            e.tensor_tensor(out=tk, in0=s2, in1=s1, op=ALU.mult),
                            e.tensor_tensor(out=tai, in0=tai, in1=tk, op=ALU.subtract),
                            e.tensor_tensor(out=tai, in0=tq, in1=tai, op=ALU.mult),
                        )[-1], waits=[(sS, sS.n)])
                        dve(lambda e: (
                            e.tensor_reduce(out=Sg[:, 0:1], in_=tar, axis=AX.X,
                                            op=ALU.add),
                            e.drain(),
                            e.tensor_reduce(out=Sg[:, 1:2], in_=tai, axis=AX.X,
                                            op=ALU.add),
                            e.drain(),
                        )[-1])
                        act(lambda e: (
                            e.activation(out=s2, in_=tar, func=AF.Square,
                                         accum_out=Sg[:, 2:3]),
                            e.drain(),
                            e.activation(out=s3, in_=tai, func=AF.Square,
                                         accum_out=Sg[:, 3:4]),
                            e.drain(),
                        )[-1], waits=[(sV, sV.n)])
                        t_sg = dve(lambda e: (
                            e.tensor_tensor(out=Sg[:, 2:3], in0=Sg[:, 2:3],
                                            in1=Sg[:, 3:4], op=ALU.add),
                            e.drain(),
                        )[-1], waits=[(sS, sS.n)])
                        t_g1 = sT.bump(1)
                        P.on("tensor", lambda e: e.matmul(
                            ps[0][0:4, 0:3], c["c_gsel"][:], Sg[:, 0:3],
                            start=True, stop=True),
                            waits=[(sV, t_sg)], inc=(sT, 1))
                        dve(lambda e: (
                            e.tensor_scalar(out=Sg2[0:4, 2:5], in0=ps[0][0:4, 0:3],
                                            scalar1=GINV, scalar2=None, op0=ALU.mult),
                            e.drain(),
                            e.tensor_tensor(out=Sg2[0:4, 5:6], in0=Sg2[0:4, 2:3],
                                            in1=Sg2[0:4, 2:3], op=ALU.mult),
                            e.tensor_tensor(out=Sg2[0:4, 6:7], in0=Sg2[0:4, 3:4],
                                            in1=Sg2[0:4, 3:4], op=ALU.mult),
                            e.drain(),
                            e.tensor_tensor(out=Sg2[0:4, 5:6], in0=Sg2[0:4, 5:6],
                                            in1=Sg2[0:4, 6:7], op=ALU.add),
                            e.drain(),
                            e.tensor_tensor(out=Sg2[0:4, 5:6], in0=Sg2[0:4, 4:5],
                                            in1=Sg2[0:4, 5:6], op=ALU.subtract),
                            e.drain(),
                            e.tensor_scalar(out=Sg2[0:4, 5:6], in0=Sg2[0:4, 5:6],
                                            scalar1=GEPS, scalar2=None, op0=ALU.add),
                            e.drain(),
                        )[-1], waits=[(sT, t_g1)])
                        act(lambda e: (
                            e.activation(out=Sg2[0:4, 1:2], in_=Sg2[0:4, 5:6],
                                         func=AF.Ln),
                            e.drain(),
                            e.activation(out=Sg2[0:4, 1:2], in_=Sg2[0:4, 1:2],
                                         func=AF.Exp, scale=-0.5),
                            e.drain(),
                        )[-1], waits=[(sV, sV.n)])
                        t_sm = dve(lambda e: (
                            e.tensor_copy(out=Sg2[0:4, 0:1], in_=Sg2[0:4, 2:3]),
                            e.drain(),
                        )[-1], waits=[(sS, sS.n)])
                        t_g2 = sT.bump(1)
                        P.on("tensor", lambda e: e.matmul(
                            ps[1][:, 0:2], c["c_gexp"][:], Sg2[0:4, 0:2],
                            start=True, stop=True),
                            waits=[(sV, t_sm)], inc=(sT, 1))
                        t_ab = dve(lambda e, ch=ch: (
                            e.tensor_tensor(out=Sg[:, 4:5], in0=ps[1][:, 1:2],
                                            in1=vec["gns"][:, ch:ch + 1], op=ALU.mult),
                            e.drain(),
                            e.tensor_tensor(out=Sg[:, 5:6], in0=ps[1][:, 0:1],
                                            in1=Sg[:, 4:5], op=ALU.mult),
                            e.drain(),
                            e.scalar_tensor_tensor(out=Sg[:, 5:6], in0=Sg[:, 5:6],
                                                   scalar=-1.0,
                                                   in1=vec["gnb"][:, ch:ch + 1],
                                                   op0=ALU.mult, op1=ALU.add),
                            e.drain(),
                            e.tensor_scalar(out=s2, in0=tar, scalar1=Sg[:, 4:5],
                                            scalar2=Sg[:, 5:6], op0=ALU.mult,
                                            op1=ALU.add),
                        )[-1], waits=[(sT, t_g2)])
                        dve(lambda e, ch=ch: e.tensor_tensor(
                            out=zb[:, ch * S:(ch + 1) * S], in0=s2, in1=tg,
                            op=ALU.mult))

                    P.barrier()
                    for mo in range(8):
                        P.dma("gpsimd",
                              wsl[mo][:].rearrange("p (k j) -> p k j", k=NCH),
                              g["l0a_wo"][:, mo * 128:(mo + 1) * 128]
                              .rearrange("(k p) j -> p k j", p=128))
                    wo_tgt = sW.n
                    oc = c["c_onescol"]
                    o1f = c["c_ones1x128_f32"]
                    for tt in range(4):
                        def mmo(e, tt=tt):
                            r = None
                            for mo in range(8):
                                for k in range(NCH):
                                    e.matmul(
                                        ps[mo][:],
                                        wsl[mo][:, k * 128:(k + 1) * 128],
                                        zb[:, k * S + tt * 512:k * S + (tt + 1) * 512],
                                        start=(k == 0), stop=False)
                                r = e.matmul(
                                    ps[mo][:], c["c_id128"][:],
                                    xlnb[:, mo * S + tt * 512:mo * S + (tt + 1) * 512],
                                    start=False, stop=True)
                            return r
                        t_mm = sT.bump(1)
                        P.on("tensor", mmo,
                             waits=[(sW, wo_tgt), (sV, sV.n), (sS, sS.n)],
                             inc=(sT, 1))
                        xi = [T["tq"][:, m * 512:(m + 1) * 512] for m in range(4)] + \
                             [T["tk"][:, m * 512:(m + 1) * 512] for m in range(4)]
                        t_ev = sV.bump(1)
                        def evo(e, xi=xi):
                            r = None
                            for m in range(8):
                                r = e.tensor_copy(out=xi[m], in_=ps[m][:])
                            return r
                        P.on("vector", evo, waits=[(sT, t_mm)], inc=(sV, 1))
                        t_p = P.sem("p").bump(1)
                        def castf(e, xi=xi):
                            r = None
                            for m in range(8):
                                r = e.tensor_copy(
                                    out=lnbf[:, m * 512:(m + 1) * 512], in_=xi[m])
                            return r
                        P.on("gpsimd", castf, waits=[(sV, t_ev)], inc=(P.sem("p"), 1))
                        t_sq = sS.bump(1)
                        def sqf(e, xi=xi):
                            r = None
                            for m in range(8):
                                r = e.activation(
                                    out=lnsq[:, m * 512:(m + 1) * 512], in_=xi[m],
                                    func=AF.Square)
                            return r
                        P.on("scalar", sqf, waits=[(sV, t_ev)], inc=(sS, 1))
                        t_sum = sT.bump(1)
                        def sums(e):
                            r = None
                            for m in range(8):
                                e.matmul(ps[0][0:1, :], oc[:],
                                         lnbf[:, m * 512:(m + 1) * 512],
                                         start=(m == 0), stop=(m == 7))
                            for m in range(8):
                                r = e.matmul(ps[1][0:1, :], oc[:],
                                             lnsq[:, m * 512:(m + 1) * 512],
                                             start=(m == 0), stop=(m == 7))
                            return r
                        P.on("tensor", sums, waits=[(P.sem("p"), t_p), (sS, t_sq)],
                             inc=(sT, 1))
                        mu, rr, nmr, va = (rows[0:1, i * 512:(i + 1) * 512]
                                           for i in range(4))
                        t_r1 = sV.bump(1)
                        P.on("vector", lambda e, mu=mu, rr=rr, nmr=nmr, va=va: (
                            e.tensor_scalar(out=mu, in0=ps[0][0:1, :],
                                            scalar1=1.0 / D, scalar2=None,
                                            op0=ALU.mult),
                            e.tensor_scalar(out=va, in0=ps[1][0:1, :],
                                            scalar1=1.0 / D, scalar2=None,
                                            op0=ALU.mult),
                            e.tensor_tensor(out=nmr, in0=mu, in1=mu, op=ALU.mult),
                            e.tensor_tensor(out=va, in0=va, in1=nmr, op=ALU.subtract),
                            e.tensor_scalar(out=va, in0=va, scalar1=GEPS,
                                            scalar2=None, op0=ALU.add),
                        )[-1], waits=[(sT, t_sum)], inc=(sV, 1))
                        t_r2 = sS.bump(1)
                        P.on("scalar", lambda e, rr=rr, va=va: (
                            e.activation(out=rr, in_=va, func=AF.Ln),
                            e.activation(out=rr, in_=rr, func=AF.Exp, scale=-0.5),
                        )[-1], waits=[(sV, t_r1)], inc=(sS, 1))
                        t_r3 = sV.bump(1)
                        P.on("vector", lambda e, mu=mu, rr=rr, nmr=nmr: (
                            e.tensor_tensor(out=nmr, in0=mu, in1=rr, op=ALU.mult),
                            e.tensor_scalar(out=nmr, in0=nmr, scalar1=-1.0,
                                            scalar2=None, op0=ALU.mult),
                        )[-1], waits=[(sS, t_r2)], inc=(sV, 1))
                        t_bc = sT.bump(1)
                        P.on("tensor", lambda e, rr=rr, nmr=nmr: (
                            e.matmul(ps[2][:], o1f[:], rr, start=True, stop=True),
                            e.matmul(ps[3][:], o1f[:], nmr, start=True, stop=True),
                        )[-1], waits=[(sV, t_r3)], inc=(sT, 1))
                        t_ap = sV.bump(1)
                        def apl(e, xi=xi, tt=tt):
                            r = None
                            for m in range(8):
                                sc = T["sc1"][:, m % 4 * 512:(m % 4 + 1) * 512]
                                e.tensor_tensor(out=sc, in0=xi[m], in1=ps[2][:],
                                                op=ALU.mult)
                                e.tensor_tensor(out=sc, in0=sc, in1=ps[3][:],
                                                op=ALU.add)
                                r = e.tensor_scalar(
                                    out=zb[:, m * S + tt * 512:m * S + (tt + 1) * 512],
                                    in0=sc, scalar1=vec["ln1s"][:, m:m + 1],
                                    scalar2=vec["ln1b"][:, m:m + 1],
                                    op0=ALU.mult, op1=ALU.add)
                            return r
                        P.on("vector", apl, waits=[(sT, t_bc)], inc=(sV, 1))
                    P.barrier()
                with contextlib.ExitStack() as stk:
                    ps = K.ps8(stk, f"l0am1{b}")
                    wa = [K.sb(stk, f"l0a_w1t{i}{b}", [128, NCH * 128], BF16)
                          for i in range(2)]
                    vstg = [K.sb(stk, f"l0a_vs{i}{b}", [128, S], BF16)
                            for i in range(2)]
                    for m in range(FCH):
                        wb = wa[m % 2]
                        m1_key = ("l0am1", b, m % 2)
                        P.dma("gpsimd", wb[:].rearrange("p (k j) -> p k j", k=NCH),
                              g["l0a_w1m"][m].rearrange("(k p) j -> p k j", p=128),
                              waits=[war[m1_key]] if m1_key in war else ())
                        w_tgt = sW.n
                        def mmh(e, wb=wb, m=m):
                            r = None
                            for tt in range(4):
                                for k in range(NCH):
                                    r = e.matmul(
                                        ps[(m % 2) * 4 + tt][:],
                                        wb[:, k * 128:(k + 1) * 128],
                                        zb[:, k * S + tt * 512:k * S + (tt + 1) * 512],
                                        start=(k == 0), stop=(k == NCH - 1))
                            return r
                        t_mm = sT.bump(1)
                        P.on("tensor", mmh, waits=[(sW, w_tgt), (sS, sS.n)],
                             inc=(sT, 1))
                        war[("l0am1", b, m % 2)] = (sT, t_mm)
                        t_gl = sS.bump(1)
                        vs = vstg[m % 2]
                        def evh(e, m=m, vs=vs):
                            r = None
                            for tt in range(4):
                                r = e.activation(
                                    out=vs[:, tt * 512:(tt + 1) * 512],
                                    in_=ps[(m % 2) * 4 + tt][:],
                                    func=AF.Gelu_apprx_tanh,
                                    bias=vec["b1"][:, m:m + 1])
                            return r
                        P.on("scalar", evh, waits=[(sT, t_mm), (sA, sA.n)],
                             inc=(sS, 1))
                        P.dma("sync", g["hscr"][m], vs[:], waits=[(sS, t_gl)])
                    P.barrier()
                with contextlib.ExitStack() as stk:
                    ps = K.ps8(stk, f"l0am2{b}")
                    w2sb = K.sb(stk, f"l0a_w2sb{b}", [128, NCH * FCH * 128], BF16)
                    wslab = [K.sb(stk, f"l0a_wsl{i}{b}", [128, 512], BF16)
                             for i in range(2)]
                    Bout = K.sb(stk, f"l0a_bo{b}", [128, 8 * 512], F32)
                    bd = Boundary(K, stk, f"l0a_bd{b}")
                    for m in range(NCH):
                        P.dma("gpsimd",
                              w2sb[:].rearrange("p (k m j) -> p k m j", k=FCH, m=NCH)
                              [:, :, m, :],
                              g["l0a_w2m"][m].rearrange("(k p) j -> p k j", p=128))
                    w2_tgt = sW.n
                    for tt in range(4):
                        for k in range(FCH):
                            wsl2 = wslab[k % 2]
                            m2_key = ("l0am2", b, k % 2)
                            P.dma("sync", wsl2[:],
                                  g["hscr"][k][:, tt * 512:(tt + 1) * 512],
                                  waits=[war[m2_key]] if m2_key in war else ())
                            sl_tgt = sA.n
                            waits = [(sA, sl_tgt), (sW, w2_tgt)]
                            if k == 0:
                                waits.append((sV, sV.n))
                            def mmd(e, wsl2=wsl2, k=k):
                                r = None
                                for m in range(NCH):
                                    r = e.matmul(
                                        ps[m][:],
                                        w2sb[:, (k * NCH + m) * 128:
                                             (k * NCH + m + 1) * 128],
                                        wsl2[:], start=(k == 0), stop=False)
                                return r
                            t_mm = sT.bump(1)
                            P.on("tensor", mmd, waits=waits, inc=(sT, 1))
                            war[("l0am2", b, k % 2)] = (sT, t_mm)
                        def mmr(e, tt=tt):
                            r = None
                            for m in range(NCH):
                                r = e.matmul(
                                    ps[m][:], c["c_id128"][:],
                                    zb[:, m * S + tt * 512:m * S + (tt + 1) * 512],
                                    start=False, stop=True)
                            return r
                        t_mm = sT.bump(1)
                        P.on("tensor", mmr, inc=(sT, 1))
                        t_ev = sV.bump(1)
                        def evd(e):
                            r = None
                            for m in range(NCH):
                                r = e.tensor_scalar(
                                    out=Bout[:, m * 512:(m + 1) * 512],
                                    in0=ps[m][:], scalar1=vec["b2"][:, m:m + 1],
                                    scalar2=None, op0=ALU.add)
                            return r
                        P.on("vector", evd, waits=[(sT, t_mm)], inc=(sV, 1))
                        Bt = [Bout[:, m * 512:(m + 1) * 512] for m in range(NCH)]
                        bd.run_tt(b, tt, Bt, (sV, t_ev), ps, resid_in, res_out,
                                  ln_out, affine=(vec["ln2s"], vec["ln2b"]),
                                  final=False, eps_pre=GEPS)
                    P.barrier()
    P.barrier()


def phase_l1attn(K, resid_in, xln_in, res_out, ln_out):
    P = K.P
    g = K.g
    c = K.c
    sW = P.sem("gpsimd_dma")
    sA = P.sem("sync_dma")
    sT = P.sem("t")
    sV = P.sem("v")
    sS = P.sem("s")
    sPo = P.sem("p")
    war = {}

    with contextlib.ExitStack() as pstk:
        rc = {}
        for nm, shape, dt in (("c_iota_row", [1, S], F32),
                              ("c_ropecos", [HD, S], F32),
                              ("c_ropesin", [HD, S], F32),
                              ("c_rope_perm", [HD, HD], BF16),
                              ("c_eye4", [1, 12], BF16)):
            rc[nm] = K.sb(pstk, "l1" + nm, shape, dt)
            P.dma("sync", rc[nm][:], g[nm])
        vec = {}
        for nm in ("bo", "gamma", "beta"):
            vec[nm] = K.sb(pstk, f"l1a_{nm}v", [128, NCH], F32)
            P.dma("sync", vec[nm][:], g[f"l1a_{nm}"])
        bdrow = K.sb(pstk, "l1a_bdrow", [1, H], F32)
        P.dma("sync", bdrow[:], g["l1a_bd_row"])
        wdsl = K.sb(pstk, "l1a_wdsl", [128, NCH * H], F32)
        P.dma("sync", wdsl[:].rearrange("p (k j) -> p k j", k=NCH),
              g["l1a_wdt"].rearrange("(k p) j -> p k j", p=128))
        P.barrier()

        for b in range(NB):
            tok0 = b * S
            with contextlib.ExitStack() as bstk:
                zb = K.sb(bstk, f"l1a_z{b}", [128, NCH * S], BF16)
                with contextlib.ExitStack() as stk:
                    ps = K.ps8(stk, f"l1h{b}")
                    xlnb = K.sb(stk, f"l1a_xln{b}", [128, NCH * S], BF16)
                    ATj = K.sb(stk, f"l1a_at{b}", [128, 16 * 512], BF16)
                    Qaug = K.sb(stk, f"l1a_qa{b}", [68, S], BF16)
                    Kaug = K.sb(stk, f"l1a_ka{b}", [68, S], BF16)
                    Tt1 = K.sb(stk, f"l1a_t1{b}", [64, S], F32)
                    Tt2 = K.sb(stk, f"l1a_t2{b}", [64, S], F32)
                    SqB = K.sb(stk, f"l1a_sqb{b}", [64, S], BF16)
                    gsil = K.sb(stk, f"l1a_gs{b}", [64, S], BF16)
                    ogh = K.sb(stk, f"l1a_og{b}", [64, S], F32)
                    Vsb = K.sb(stk, f"l1a_v{b}", [128, 16 * 65], BF16)
                    whd = {nm: K.sb(stk, f"l1a_w{nm}{b}", [128, NCH * HD], BF16)
                           for nm in ("q", "k", "v", "g")}
                    bh = {nm: K.sb(stk, f"l1a_b{nm}{b}", [64, 1], F32)
                          for nm in ("q", "k", "v", "g")}
                    VRf = K.sb(stk, f"l1a_vrf{b}", [1, HD], F32)
                    VRb = K.sb(stk, f"l1a_vrb{b}", [1, HD], BF16)
                    mrow = K.sb(stk, f"l1a_mq{b}", [1, S], F32)
                    srow = K.sb(stk, f"l1a_sr{b}", [1, S], F32)
                    Rb0 = K.sb(stk, f"l1a_rb0{b}", [1, S], BF16)
                    Rb1 = K.sb(stk, f"l1a_rb1{b}", [1, S], BF16)
                    Rb2 = K.sb(stk, f"l1a_rb2{b}", [1, S], BF16)
                    dup = K.sb(stk, f"l1a_du{b}", [128, 128], BF16)
                    dlo = K.sb(stk, f"l1a_dl{b}", [128, 128], BF16)
                    Bh = K.sb(stk, f"l1a_bh{b}", [128, 16], F32)
                    ldc = K.sb(stk, f"l1a_ldc{b}", [128, 1], F32)
                    sm = K.sb(stk, f"l1a_sm{b}", [128, 8], F32)
                    smr = K.sb(stk, f"l1a_smr{b}", [1, 576], F32)
                    mb2 = K.sb(stk, f"l1a_mb2{b}", [64, 4], F32)
                    gbh = K.sb(stk, f"l1a_gbh{b}", [64, 2], F32)

                    def dve(fn, waits=()):
                        t = sV.bump(1)
                        P.on("vector", fn, waits=waits, inc=(sV, 1))
                        return t

                    def act(fn, waits=()):
                        t = sS.bump(1)
                        P.on("scalar", fn, waits=waits, inc=(sS, 1))
                        return t

                    def ten(fn, waits=()):
                        t = sT.bump(1)
                        P.on("tensor", fn, waits=waits, inc=(sT, 1))
                        return t

                    for m in range(8):
                        P.dma("sync", xlnb[:, m * S:(m + 1) * S],
                              g[xln_in][m, :, tok0:tok0 + S], waits=[(sT, sT.n)])
                    xln_tgt = sA.n
                    P.dma("sync", Kaug[64:68, :], g["c_kaugrows"],
                          waits=[(sT, sT.n)])
                    kaug_tgt = sA.n
                    t_ones = sPo.bump(1)
                    P.on("gpsimd", lambda e: e.memset(Vsb[:], 1.0),
                         waits=[(sT, sT.n)], inc=(sPo, 1))

                    def xmean(e):
                        r = None
                        for m in range(8):
                            r = e.tensor_reduce(out=sm[:, m:m + 1],
                                                in_=xlnb[:, m * S:(m + 1) * S],
                                                axis=AX.X, op=ALU.add)
                            r = e.drain()
                        return r
                    t_xm = dve(xmean, waits=[(sA, xln_tgt)])
                    def pdmm(e):
                        r = None
                        for m in range(8):
                            r = e.matmul(ps[0][0:1, 0:H], sm[:, m:m + 1],
                                         wdsl[:, m * H:(m + 1) * H],
                                         start=(m == 0), stop=(m == 7))
                        return r
                    t_pd1 = ten(pdmm, waits=[(sV, t_xm)])
                    ldr = smr[0:1, 0:H]
                    lds = smr[0:1, H:2 * H]
                    t_pd2 = dve(lambda e: (
                        e.tensor_scalar(out=ldr, in0=ps[0][0:1, 0:H],
                                        scalar1=1.0 / S, scalar2=None, op0=ALU.mult),
                        e.drain(),
                        e.tensor_tensor(out=ldr, in0=ldr, in1=bdrow[:], op=ALU.add),
                        e.drain(),
                    )[-1], waits=[(sT, t_pd1)])
                    t_pd3 = act(lambda e: (
                        e.activation(out=ldr, in_=ldr, func=AF.Sigmoid),
                        e.drain(),
                        e.activation(out=lds, in_=ldr, func=AF.Exp,
                                     scale=float(-8.0 * math.log(2.0)),
                                     bias=c["c_pool"][0:1, 2:3]),
                        e.drain(),
                    )[-1], waits=[(sV, t_pd2)])
                    t_pd4 = dve(lambda e: (
                        e.tensor_scalar(out=lds, in0=lds, scalar1=-1.0,
                                        scalar2=1.0, op0=ALU.mult, op1=ALU.add),
                        e.drain(),
                    )[-1], waits=[(sS, t_pd3)])
                    t_pd5 = act(lambda e: (
                        e.activation(out=ldr, in_=lds, func=AF.Ln),
                        e.drain(),
                    )[-1], waits=[(sV, t_pd4)])

                    for h in range(H):
                        ch_h = h // 2
                        pb_h = (h % 2) * 64
                        for nm in ("q", "k", "v", "g"):
                            wh_key = ("whd", b, nm)
                            P.dma("gpsimd",
                                  whd[nm][:].rearrange("p (k j) -> p k j", k=NCH),
                                  g[f"l1a_w{nm}h"][h]
                                  .rearrange("(k p) j -> p k j", p=128),
                                  waits=[war[wh_key]] if wh_key in war else ())
                            P.dma("sync", bh[nm][:],
                                  g[f"l1a_b{nm}h"][h].rearrange("(j o) -> j o", o=1),
                                  waits=[(sS, sS.n)])
                        P.dma("sync", VRf[:], g["l1a_bvh"][h].rearrange("(o j) -> o j", o=1),
                              waits=[(sPo, sPo.n)])
                        P.dma("sync", gbh[:, 0:1],
                              g["l1a_gamma_t"][ch_h, pb_h:pb_h + 64]
                              .rearrange("(j o) -> j o", o=1),
                              waits=[(sV, sV.n)])
                        P.dma("sync", gbh[:, 1:2],
                              g["l1a_beta_t"][ch_h, pb_h:pb_h + 64]
                              .rearrange("(j o) -> j o", o=1))
                        gb_tgt = P.sem("sync_dma").n
                        w_tgt = sW.n
                        b_tgt = sA.n
                        t_vr = sPo.bump(1)
                        P.on("gpsimd", lambda e: e.tensor_copy(out=VRb[:], in_=VRf[:]),
                             waits=[(sA, b_tgt)], inc=(sPo, 1))
                        t_ld1 = ten(lambda e: e.matmul(
                            ps[6][:, 0:1], c["c_ones1x128_f32"][:],
                            smr[0:1, 0:1] if False else ldr[0:1, h:h + 1],
                            start=True, stop=True),
                            waits=[(sS, t_pd5), (sV, sV.n)])
                        t_ld2 = dve(lambda e: (
                            e.tensor_copy(out=ldc[:], in_=ps[6][:, 0:1]),
                            e.drain(),
                            e.tensor_scalar(out=Bh[:], in0=c["c_iota2d"][:],
                                            scalar1=ldc[:], scalar2=None,
                                            op0=ALU.mult),
                            e.drain(),
                        )[-1], waits=[(sT, t_ld1)])
                        for tt in range(4):
                            t_mm = ten(lambda e, tt=tt: [
                                e.matmul(ps[4 + (tt % 2)][0:64, :],
                                         whd["g"][:, k * HD:(k + 1) * HD],
                                         xlnb[:, k * S + tt * 512:k * S + (tt + 1) * 512],
                                         start=(k == 0), stop=(k == NCH - 1))
                                for k in range(NCH)][-1],
                                waits=[(sW, w_tgt), (sA, xln_tgt),
                                       (sS, sS.n), (sV, sV.n)])
                            war[("whd", b, "g")] = (sT, t_mm)
                            act(lambda e, tt=tt: e.activation(
                                out=gsil[:, tt * 512:(tt + 1) * 512],
                                in_=ps[4 + (tt % 2)][0:64, :], func=AF.Silu,
                                bias=bh["g"][:]),
                                waits=[(sT, t_mm), (sA, b_tgt)])
                        for nm, aug in (("q", Qaug), ("k", Kaug)):
                            for tt in range(4):
                                t_qk = ten(lambda e, nm=nm, tt=tt: [
                                    e.matmul(ps[tt][0:64, :],
                                             whd[nm][:, k * HD:(k + 1) * HD],
                                             xlnb[:, k * S + tt * 512:
                                                  k * S + (tt + 1) * 512],
                                             start=(k == 0), stop=(k == NCH - 1))
                                    for k in range(NCH)][-1],
                                    waits=[(sW, w_tgt), (sA, xln_tgt),
                                           (sS, sS.n), (sV, sV.n)])
                                war[("whd", b, nm)] = (sT, t_qk)
                            t_ev = act(lambda e, nm=nm, aug=aug: [
                                e.activation(out=aug[0:64, tt * 512:(tt + 1) * 512],
                                             in_=ps[tt][0:64, :], func=AF.Identity,
                                             bias=bh[nm][:])
                                for tt in range(4)][-1],
                                waits=[(sT, sT.n), (sA, b_tgt)])
                            t_rt = ten(lambda e, aug=aug: [
                                e.matmul(ps[4 + tt][0:64, :], rc["c_rope_perm"][:],
                                         aug[0:64, tt * 512:(tt + 1) * 512],
                                         start=True, stop=True)
                                for tt in range(4)][-1],
                                waits=[(sS, t_ev), (sV, sV.n)])
                            dve(lambda e, aug=aug: (
                                e.tensor_tensor(out=Tt1[:], in0=aug[0:64, :],
                                                in1=rc["c_ropecos"][:], op=ALU.mult),
                            )[-1], waits=[(sS, t_ev)])
                            dve(lambda e, aug=aug: ([
                                e.tensor_tensor(
                                    out=Tt2[:, tt * 512:(tt + 1) * 512],
                                    in0=ps[4 + tt][0:64, :],
                                    in1=rc["c_ropesin"][:, tt * 512:(tt + 1) * 512],
                                    op=ALU.mult)
                                for tt in range(4)],
                                e.tensor_tensor(out=aug[0:64, :], in0=Tt1[:],
                                                in1=Tt2[:], op=ALU.add),
                            )[-1], waits=[(sT, t_rt)])
                        for tc in range(16):
                            bank = 4 + (tc % 4)
                            t_mm = ten(lambda e, tc=tc, bank=bank: (
                                [e.matmul(ps[bank][:, 0:HD],
                                          xlnb[:, k * S + tc * 128:k * S + tc * 128 + 128],
                                          whd["v"][:, k * HD:(k + 1) * HD],
                                          start=(k == 0), stop=False)
                                 for k in range(NCH)],
                                e.matmul(ps[bank][:, 0:HD], c["c_ones1x128"][:],
                                         VRb[:], start=False, stop=True),
                            )[-1], waits=[(sW, w_tgt), (sPo, t_vr), (sV, sV.n)])
                            war[("whd", b, "v")] = (sT, t_mm)
                            dve(lambda e, tc=tc, bank=bank: e.tensor_copy(
                                out=Vsb[:, tc * 65:tc * 65 + HD],
                                in_=ps[bank][:, 0:HD]),
                                waits=[(sT, t_mm)])
                        t_sq = act(lambda e: (
                            e.activation(out=Tt1[:], in_=Qaug[0:64, :],
                                         func=AF.Square),
                        )[-1], waits=[(sV, sV.n)])
                        t_cb = sPo.bump(1)
                        P.on("gpsimd", lambda e: e.tensor_copy(out=SqB[:], in_=Tt1[:]),
                             waits=[(sS, t_sq)], inc=(sPo, 1))
                        t_qn = ten(lambda e: [
                            e.matmul(ps[tt][0:1, :], c["c_onescol"][0:64, :],
                                     SqB[:, tt * 512:(tt + 1) * 512],
                                     start=True, stop=True)
                            for tt in range(4)][-1],
                            waits=[(sPo, t_cb)])
                        t_qs = act(lambda e: [
                            e.activation(out=mrow[0:1, tt * 512:(tt + 1) * 512],
                                         in_=ps[tt][0:1, :], func=AF.Sqrt)
                            for tt in range(4)][-1],
                            waits=[(sT, t_qn)])
                        t_sq2 = act(lambda e: (
                            e.activation(out=Tt1[:], in_=Kaug[0:64, :],
                                         func=AF.Square),
                        )[-1])
                        t_cb2 = sPo.bump(1)
                        P.on("gpsimd", lambda e: e.tensor_copy(out=SqB[:], in_=Tt1[:]),
                             waits=[(sS, t_sq2)], inc=(sPo, 1))
                        t_kn = ten(lambda e: [
                            e.matmul(ps[4 + tt][0:1, :], c["c_onescol"][0:64, :],
                                     SqB[:, tt * 512:(tt + 1) * 512],
                                     start=True, stop=True)
                            for tt in range(4)][-1],
                            waits=[(sPo, t_cb2)])
                        t_kmax = dve(lambda e: ([
                            e.tensor_copy(out=srow[0:1, tt * 512:(tt + 1) * 512],
                                          in_=ps[4 + tt][0:1, :])
                            for tt in range(4)],
                            e.drain(),
                            e.tensor_reduce(out=smr[0:1, 32:33], in_=srow[:],
                                            axis=AX.X, op=ALU.max),
                            e.drain(),
                        )[-1], waits=[(sT, t_kn)])
                        t_ks = act(lambda e: (
                            e.activation(out=smr[0:1, 32:33], in_=smr[0:1, 32:33],
                                         func=AF.Sqrt),
                            e.drain(),
                        )[-1], waits=[(sV, t_kmax)])
                        t_rows = dve(lambda e: (
                            e.tensor_scalar(out=mrow[:], in0=mrow[:],
                                            scalar1=smr[0:1, 32:33], scalar2=-1.0,
                                            op0=ALU.mult, op1=ALU.mult),
                            e.tensor_copy(out=Rb0[:], in_=mrow[:]),
                            e.tensor_scalar(out=srow[:], in0=rc["c_iota_row"][:],
                                            scalar1=ldc[0:1, 0:1], scalar2=-8.0,
                                            op0=ALU.mult, op1=ALU.mult),
                            e.tensor_copy(out=Rb1[:], in_=srow[:]),
                            e.drain(),
                            e.tensor_tensor(out=mrow[:], in0=srow[:], in1=Rb1[:],
                                            op=ALU.subtract),
                            e.tensor_copy(out=Rb2[:], in_=mrow[:]),
                            e.drain(),
                        )[-1], waits=[(sS, t_ks), (sV, t_ld2)])
                        t_aug = None
                        for jj in range(4):
                            t_stk = ten(lambda e, jj=jj: (
                                e.matmul(ps[7][0:4, :], rc["c_eye4"][0:1, 0:4],
                                         Rb0[0:1, jj * 512:(jj + 1) * 512],
                                         start=True, stop=False),
                                e.matmul(ps[7][0:4, :], rc["c_eye4"][0:1, 4:8],
                                         Rb1[0:1, jj * 512:(jj + 1) * 512],
                                         start=False, stop=False),
                                e.matmul(ps[7][0:4, :], rc["c_eye4"][0:1, 8:12],
                                         Rb2[0:1, jj * 512:(jj + 1) * 512],
                                         start=False, stop=True),
                            )[-1], waits=[(sV, t_aug if t_aug else t_rows)])
                            t_aug = dve(lambda e, jj=jj: e.tensor_copy(
                                out=Qaug[64:68, jj * 512:(jj + 1) * 512],
                                in_=ps[7][0:4, :]), waits=[(sT, t_stk)])
                        for j in range(4):
                            for cc in range(16):
                                bank = (0, 1, 2, 6, 7)[cc % 5]
                                if cc < 4 * j:
                                    t_mm = ten(lambda e, cc=cc, j=j, bank=bank:
                                        e.matmul(ps[bank][:],
                                                 Kaug[0:65, cc * 128:(cc + 1) * 128],
                                                 Qaug[0:65, j * 512:(j + 1) * 512],
                                                 start=True, stop=True),
                                        waits=[(sV, t_aug), (sS, sS.n)])
                                    act(lambda e, cc=cc, bank=bank: e.activation(
                                        out=ATj[:, cc * 512:(cc + 1) * 512],
                                        in_=ps[bank][:], func=AF.Exp, scale=0.125),
                                        waits=[(sT, t_mm)])
                                elif cc >= 4 * (j + 1):
                                    t_mm = ten(lambda e, cc=cc, j=j, bank=bank:
                                        e.matmul(ps[bank][:],
                                                 Kaug[0:68, cc * 128:(cc + 1) * 128],
                                                 Qaug[0:68, j * 512:(j + 1) * 512],
                                                 start=True, stop=True),
                                        waits=[(sV, t_aug), (sS, sS.n)])
                                    act(lambda e, cc=cc, bank=bank: e.activation(
                                        out=ATj[:, cc * 512:(cc + 1) * 512],
                                        in_=ps[bank][:], func=AF.Exp, scale=0.125,
                                        bias=Bh[:, cc:cc + 1]),
                                        waits=[(sT, t_mm)])
                                else:
                                    for ssub in range(4):
                                        jq = 4 * j + ssub
                                        q0 = j * 512 + ssub * 128
                                        if jq < cc:
                                            t_mm = ten(lambda e, cc=cc, q0=q0, bank=bank:
                                                e.matmul(ps[bank][:, 0:128],
                                                         Kaug[0:68, cc * 128:(cc + 1) * 128],
                                                         Qaug[0:68, q0:q0 + 128],
                                                         start=True, stop=True),
                                                waits=[(sV, t_aug), (sS, sS.n)])
                                            act(lambda e, cc=cc, q0=q0, bank=bank, j=j:
                                                e.activation(
                                                    out=ATj[:, cc * 512 + (q0 - j * 512):
                                                            cc * 512 + (q0 - j * 512) + 128],
                                                    in_=ps[bank][:, 0:128],
                                                    func=AF.Exp, scale=0.125,
                                                    bias=Bh[:, cc:cc + 1]),
                                                waits=[(sT, t_mm)])
                                        elif jq > cc:
                                            t_mm = ten(lambda e, cc=cc, q0=q0, bank=bank:
                                                e.matmul(ps[bank][:, 0:128],
                                                         Kaug[0:65, cc * 128:(cc + 1) * 128],
                                                         Qaug[0:65, q0:q0 + 128],
                                                         start=True, stop=True),
                                                waits=[(sV, t_aug), (sS, sS.n)])
                                            act(lambda e, cc=cc, q0=q0, bank=bank, j=j:
                                                e.activation(
                                                    out=ATj[:, cc * 512 + (q0 - j * 512):
                                                            cc * 512 + (q0 - j * 512) + 128],
                                                    in_=ps[bank][:, 0:128],
                                                    func=AF.Exp, scale=0.125),
                                                waits=[(sT, t_mm)])
                                        else:
                                            t_mm = ten(lambda e, cc=cc, q0=q0:
                                                (e.matmul(ps[3][:, 0:128],
                                                          Kaug[0:68, cc * 128:(cc + 1) * 128],
                                                          Qaug[0:68, q0:q0 + 128],
                                                          start=True, stop=True),
                                                 e.matmul(ps[3][:, 128:256],
                                                          Kaug[0:65, cc * 128:(cc + 1) * 128],
                                                          Qaug[0:65, q0:q0 + 128],
                                                          start=True, stop=True),
                                                 )[-1],
                                                waits=[(sV, sV.n), (sS, sS.n)])
                                            t_ex = act(lambda e, cc=cc: (
                                                e.activation(out=dup[:],
                                                             in_=ps[3][:, 0:128],
                                                             func=AF.Exp, scale=0.125,
                                                             bias=Bh[:, cc:cc + 1]),
                                                e.activation(out=dlo[:],
                                                             in_=ps[3][:, 128:256],
                                                             func=AF.Exp, scale=0.125),
                                            )[-1], waits=[(sT, t_mm)])
                                            dve(lambda e, cc=cc, q0=q0, j=j:
                                                e.select(
                                                    out=ATj[:, cc * 512 + (q0 - j * 512):
                                                            cc * 512 + (q0 - j * 512) + 128],
                                                    mask=c["c_diagmask"][:],
                                                    on_true=dup[:], on_false=dlo[:]),
                                                waits=[(sS, t_ex)])
                            t_av = ten(lambda e, j=j: [
                                e.matmul(ps[4][0:65, :], Vsb[:, tc * 65:(tc + 1) * 65],
                                         ATj[:, tc * 512:(tc + 1) * 512],
                                         start=(tc == 0), stop=(tc == 15))
                                for tc in range(16)][-1],
                                waits=[(sS, sS.n), (sV, sV.n)])
                            nr = smr[0:1, 64:576]
                            t_nr = act(lambda e: (
                                e.activation(out=nr, in_=ps[4][64:65, :], func=AF.Ln),
                                e.activation(out=nr, in_=nr, func=AF.Exp, scale=-1.0),
                            )[-1], waits=[(sT, t_av)])
                            t_nb = ten(lambda e: e.matmul(
                                ps[5][0:64, :], c["c_ones1x128_f32"][0:1, 0:64], nr,
                                start=True, stop=True),
                                waits=[(sS, t_nr)])
                            dve(lambda e, j=j: (
                                e.tensor_copy(out=Tt1[:, 0:512], in_=ps[4][0:64, :]),
                                e.tensor_tensor(out=ogh[:, j * 512:(j + 1) * 512],
                                                in0=Tt1[:, 0:512], in1=ps[5][0:64, :],
                                                op=ALU.mult),
                            )[-1], waits=[(sT, t_nb)])
                        t_st = dve(lambda e: (
                            e.tensor_reduce(out=sm[0:64, 0:1], in_=ogh[:],
                                            axis=AX.X, op=ALU.add),
                            e.drain(),
                        )[-1])
                        t_st2 = act(lambda e: (
                            e.activation(out=Tt1[:], in_=ogh[:], func=AF.Square,
                                         accum_out=sm[0:64, 1:2]),
                            e.drain(),
                        )[-1], waits=[(sV, t_st)])
                        t_st3 = ten(lambda e: e.matmul(
                            ps[6][0:1, 0:2], c["c_onescol_f32"][0:64, :],
                            sm[0:64, 0:2], start=True, stop=True),
                            waits=[(sS, t_st2)])
                        t_st4 = dve(lambda e: (
                            e.tensor_scalar(out=smr[0:1, 34:36], in0=ps[6][0:1, 0:2],
                                            scalar1=1.0 / (S * HD), scalar2=None,
                                            op0=ALU.mult),
                            e.drain(),
                            e.tensor_tensor(out=smr[0:1, 36:37], in0=smr[0:1, 34:35],
                                            in1=smr[0:1, 34:35], op=ALU.mult),
                            e.drain(),
                            e.tensor_tensor(out=smr[0:1, 35:36], in0=smr[0:1, 35:36],
                                            in1=smr[0:1, 36:37], op=ALU.subtract),
                            e.drain(),
                            e.tensor_scalar(out=smr[0:1, 35:36], in0=smr[0:1, 35:36],
                                            scalar1=1e-5, scalar2=None, op0=ALU.add),
                            e.drain(),
                        )[-1], waits=[(sT, t_st3)])
                        t_st5 = act(lambda e: (
                            e.activation(out=smr[0:1, 35:36], in_=smr[0:1, 35:36],
                                         func=AF.Ln),
                            e.drain(),
                            e.activation(out=smr[0:1, 35:36], in_=smr[0:1, 35:36],
                                         func=AF.Exp, scale=-0.5),
                            e.drain(),
                        )[-1], waits=[(sV, t_st4)])
                        t_bc2 = ten(lambda e: e.matmul(
                            ps[7][0:64, 0:2], c["c_ones1x128_f32"][0:1, 0:64],
                            smr[0:1, 34:36], start=True, stop=True),
                            waits=[(sS, t_st5)])
                        t_ab2 = dve(lambda e, ch_h=ch_h, pb_h=pb_h: (
                            e.tensor_copy(out=mb2[:, 0:2], in_=ps[7][0:64, 0:2]),
                            e.drain(),
                            e.tensor_tensor(out=mb2[:, 2:3], in0=mb2[:, 1:2],
                                            in1=gbh[:, 0:1], op=ALU.mult),
                            e.drain(),
                            e.tensor_tensor(out=mb2[:, 3:4], in0=mb2[:, 0:1],
                                            in1=mb2[:, 2:3], op=ALU.mult),
                            e.drain(),
                            e.scalar_tensor_tensor(out=mb2[:, 3:4], in0=mb2[:, 3:4],
                                                   scalar=-1.0,
                                                   in1=gbh[:, 1:2],
                                                   op0=ALU.mult, op1=ALU.add),
                            e.drain(),
                            e.tensor_scalar(out=Tt1[:], in0=ogh[:],
                                            scalar1=mb2[:, 2:3], scalar2=mb2[:, 3:4],
                                            op0=ALU.mult, op1=ALU.add),
                            e.tensor_tensor(
                                out=zb[pb_h:pb_h + 64, ch_h * S:(ch_h + 1) * S],
                                in0=Tt1[:], in1=gsil[:], op=ALU.mult),
                        )[-1], waits=[(sT, t_bc2), (P.sem("sync_dma"), gb_tgt)])
                P.barrier()
                with contextlib.ExitStack() as stk:
                    ps = K.ps8(stk, f"l1wo{b}")
                    wsl = [K.sb(stk, f"l1a_wo{i}{b}", [128, NCH * 128], BF16)
                           for i in range(8)]
                    Bout = K.sb(stk, f"l1a_bo2{b}", [128, 8 * 512], F32)
                    bd = Boundary(K, stk, f"l1a_bd{b}")
                    for mo in range(8):
                        P.dma("gpsimd",
                              wsl[mo][:].rearrange("p (k j) -> p k j", k=NCH),
                              g["l1a_wot"][:, mo * 128:(mo + 1) * 128]
                              .rearrange("(k p) j -> p k j", p=128))
                    wo_tgt = sW.n
                    for tt in range(4):
                        def mmo(e, tt=tt):
                            r = None
                            for mo in range(8):
                                for k in range(NCH):
                                    r = e.matmul(
                                        ps[mo][:],
                                        wsl[mo][:, k * 128:(k + 1) * 128],
                                        zb[:, k * S + tt * 512:k * S + (tt + 1) * 512],
                                        start=(k == 0), stop=(k == NCH - 1))
                            return r
                        t_mm = sT.bump(1)
                        P.on("tensor", mmo,
                             waits=[(sW, wo_tgt), (sV, sV.n), (sS, sS.n)],
                             inc=(sT, 1))
                        t_ev = sV.bump(1)
                        def evo(e):
                            r = None
                            for m in range(NCH):
                                r = e.tensor_scalar(
                                    out=Bout[:, m * 512:(m + 1) * 512],
                                    in0=ps[m][:], scalar1=vec["bo"][:, m:m + 1],
                                    scalar2=None, op0=ALU.add)
                            return r
                        P.on("vector", evo, waits=[(sT, t_mm)], inc=(sV, 1))
                        Bt = [Bout[:, m * 512:(m + 1) * 512] for m in range(NCH)]
                        bd.run_tt(b, tt, Bt, (sV, t_ev), ps, resid_in, res_out,
                                  ln_out, affine=None, final=False)
                    P.barrier()
    P.barrier()



def build_program(phases, stage_ext=None):
    nc = bass.Bass("TRN2", target_bir_lowering=False)
    stack = contextlib.ExitStack()
    P = Prog(nc, stack)
    K = Kern(nc, P, stack)
    declare_io(K, stage_ext)
    with contextlib.ExitStack() as cstk:
        load_consts(K, cstk)
        for ph in phases:
            if ph == "ln0":
                phase_ln0(K)
            elif ph == "l0a":
                phase_l0attn(K, "x0", "xln", "xres", "xln")
            elif ph == "l0f":
                phase_ffn(K, "l0f", "xres", "xln", "xres2", "xln2")
            elif ph == "l1a":
                phase_l1attn(K, "xres2", "xln2", "xres", "xln")
            elif ph == "l1f":
                phase_ffn(K, "l1f", "xres", "xln", None, None, final=True)
            else:
                raise ValueError(ph)
        P.finalize()
    stack.close()
    return nc


ALL_PHASES = ["ln0", "l0a", "l0f", "l1a", "l1f"]


def kernel(**inputs):
    maps = prep_host(inputs)
    nc = build_program(ALL_PHASES)
    res = run_bass_kernel_spmd(nc, maps, core_ids=list(range(8)))
    out = np.empty((16, S, D), np.float32)
    for cid in range(8):
        o = np.asarray(res.results[cid]["out"], np.float32)
        out[2 * cid:2 * cid + 2] = o.reshape(D, NB, S).transpose(1, 2, 0)
    return out


# version 03
# speedup vs baseline: 1.0077x; 1.0077x over previous
import math
import contextlib
import numpy as np
import ml_dtypes

import concourse.bass as bass
import concourse.mybir as mybir
from concourse.bass_utils import run_bass_kernel_spmd

F32 = mybir.dt.float32
BF16 = mybir.dt.bfloat16
U8 = mybir.dt.uint8
AF = mybir.ActivationFunctionType
ALU = mybir.AluOpType
AX = mybir.AxisListType
NPBF16 = ml_dtypes.bfloat16

D = 1024
S = 2048
NB = 2
TOK = NB * S
NCH = D // 128
F = 4096
FCH = F // 128
H = 16
HD = 64
NT = S // 512
MAGIC = 12582912.0
INV2PI = float(1.0 / (2 * np.pi))
TWOPI = float(2 * np.pi)
PI = float(np.pi)
SIGC = 1.5957691216057308

ENGINES = ("sync", "gpsimd", "vector", "scalar", "tensor")
FFN_DEBUG = 0


class Sem:
    __slots__ = ("h", "n", "name")
    def __init__(self, h, name):
        self.h = h; self.n = 0; self.name = name
    def bump(self, k):
        self.n += k
        return self.n


class Prog:
    def __init__(self, nc, stack):
        self.nc = nc
        self.stack = stack
        self.q = {e: [] for e in ENGINES}
        self._sems = {}

    def sem(self, name):
        if name not in self._sems:
            h = self.stack.enter_context(self.nc.semaphore(name))
            self._sems[name] = Sem(h, name)
        return self._sems[name]

    def on(self, eng, fn, waits=(), inc=None):
        def thunk(e, fn=fn, waits=tuple(waits), inc=inc):
            for s, v in waits:
                e.wait_ge(s.h, v)
            r = fn(e)
            if inc is not None:
                s, a = inc
                r.then_inc(s.h, a)
        self.q[eng].append(thunk)

    def dma(self, eng, out, in_, waits=()):
        s = self.sem(eng + "_dma")
        tgt = s.bump(16)
        self.on(eng, lambda e: e.dma_start(out=out, in_=in_), waits, (s, 16))
        return s, tgt

    def barrier(self):
        b = self.sem("bar")
        dma_waits = [(s, s.n) for nm, s in list(self._sems.items())
                     if nm.endswith("_dma")]
        tgt = b.n + len(ENGINES)
        for eng in ENGINES:
            def thunk(e, dma_waits=tuple(dma_waits), b=b, tgt=tgt):
                for s, v in dma_waits:
                    e.wait_ge(s.h, v)
                e.sem_inc(b.h, 1)
                e.wait_ge(b.h, tgt)
            self.q[eng].append(thunk)
        b.n = tgt

    def finalize(self):
        with self.nc.Block() as blk:
            @blk.sync
            def _(e):
                for t in self.q["sync"]:
                    t(e)
            @blk.gpsimd
            def _(e):
                for t in self.q["gpsimd"]:
                    t(e)
            @blk.vector
            def _(e):
                for t in self.q["vector"]:
                    t(e)
            @blk.scalar
            def _(e):
                for t in self.q["scalar"]:
                    t(e)
            @blk.tensor
            def _(e):
                for t in self.q["tensor"]:
                    t(e)



def _pp(vec):
    return np.ascontiguousarray(np.asarray(vec, np.float32).reshape(-1, 128).T)


def prep_host(inputs):
    f32 = lambda a: np.ascontiguousarray(np.asarray(a, np.float32))
    sh = {}
    sh["l0a_wq"] = f32(inputs["L0_attn_Wq"])
    sh["l0a_wk"] = f32(inputs["L0_attn_Wk"])
    sh["l0a_wv"] = f32(inputs["L0_attn_Wv"])
    sh["l0a_wg"] = f32(inputs["L0_attn_Wg"])
    sh["l0a_wa"] = f32(inputs["L0_attn_Wa"])
    sh["l0a_wo"] = f32(inputs["L0_attn_Wo"])
    w1 = f32(inputs["L0_attn_W1"])
    w2 = f32(inputs["L0_attn_W2"])
    sh["l0a_w1m"] = f32(w1.reshape(D, FCH, 128).transpose(1, 0, 2))
    sh["l0a_w2m"] = f32(w2.reshape(F, NCH, 128).transpose(1, 0, 2))
    for nm, src in (("gns", "gn_scale"), ("gnb", "gn_bias"), ("ln1s", "ln1_scale"),
                    ("ln1b", "ln1_bias"), ("ln2s", "ln2_scale"), ("ln2b", "ln2_bias"),
                    ("b2", "b2")):
        sh[f"l0a_{nm}"] = _pp(inputs[f"L0_attn_{src}"])
    sh["l0a_b1"] = _pp(inputs["L0_attn_b1"])
    for pre, src in (("l0f", "L0_ffn"), ("l1f", "L1_ffn")):
        w1t = f32(np.asarray(inputs[f"{src}_w1_w"]).T)
        w2t = f32(np.asarray(inputs[f"{src}_w2_w"]).T)
        h1t = f32(np.asarray(inputs[f"{src}_h1_w"]).T)
        h2t = f32(np.asarray(inputs[f"{src}_h2_w"]).T)
        w3t = f32(np.asarray(inputs[f"{src}_w3_w"]).T)
        sh[f"{pre}_w1m"] = f32(w1t.reshape(D, FCH, 128).transpose(1, 0, 2))
        sh[f"{pre}_w2m"] = f32(w2t.reshape(D, FCH, 128).transpose(1, 0, 2))
        sh[f"{pre}_h1m"] = f32(h1t.reshape(F, FCH, 128).transpose(1, 0, 2))
        sh[f"{pre}_h2m"] = f32(h2t.reshape(F, FCH, 128).transpose(1, 0, 2))
        sh[f"{pre}_w3m"] = f32(w3t.reshape(F, NCH, 128).transpose(1, 0, 2))
        sh[f"{pre}_s1"] = _pp(inputs[f"{src}_w1_s"])
        sh[f"{pre}_s2"] = _pp(inputs[f"{src}_w2_s"])
        sh[f"{pre}_sh1"] = _pp(inputs[f"{src}_h1_s"])
        sh[f"{pre}_sh2"] = _pp(inputs[f"{src}_h2_s"])
        sh[f"{pre}_s3"] = _pp(inputs[f"{src}_w3_s"])
    for nm in ("wq", "wk", "wv", "wg"):
        wt = f32(np.asarray(inputs[f"L1_attn_{nm}"]).T)
        sh[f"l1a_{nm}h"] = f32(wt.reshape(D, H, HD).transpose(1, 0, 2))
        b = np.asarray(inputs[f"L1_attn_b{nm[1]}"], np.float32)
        sh[f"l1a_b{nm[1]}h"] = f32(b.reshape(H, HD))
    sh["l1a_wot"] = f32(np.asarray(inputs["L1_attn_wo"]).T)
    sh["l1a_wdt"] = f32(np.asarray(inputs["L1_attn_wd"]).T)
    sh["l1a_bo"] = _pp(inputs["L1_attn_bo"])
    sh["l1a_bd_row"] = f32(np.asarray(inputs["L1_attn_bd"]).reshape(1, H))
    sh["l1a_gamma"] = _pp(inputs["L1_attn_gamma"])
    sh["l1a_beta"] = _pp(inputs["L1_attn_beta"])
    sh["l1a_gamma_t"] = f32(np.asarray(inputs["L1_attn_gamma"]).reshape(NCH, 128))
    sh["l1a_beta_t"] = f32(np.asarray(inputs["L1_attn_beta"]).reshape(NCH, 128))
    bfa = lambda a: np.ascontiguousarray(np.asarray(a, np.float32).astype(NPBF16))
    sh["c_id128"] = bfa(np.eye(128))
    sh["c_ones1x128"] = bfa(np.ones((1, 128)))
    sh["c_onescol"] = bfa(np.ones((128, 1)))
    sh["c_ones1x128_f32"] = np.ones((1, 128), np.float32)
    sh["c_iota_row"] = np.arange(S, dtype=np.float32).reshape(1, S)
    sh["c_iota2d"] = np.ascontiguousarray(
        np.arange(128, dtype=np.float32)[:, None]
        + 128.0 * np.arange(16, dtype=np.float32)[None, :])
    inv_freq = (1.0 / 10000.0 ** (np.arange(0, HD, 2, dtype=np.float32) / HD)
                ).astype(np.float32)
    freqs = np.arange(S, dtype=np.float32)[:, None] * inv_freq[None, :]
    emb = np.concatenate([freqs, freqs], axis=-1)
    sh["c_ropecos"] = f32(np.cos(emb).T)
    sh["c_ropesin"] = f32(np.sin(emb).T)
    gsel = np.zeros((128, 4), np.float32)
    for gg in range(4):
        gsel[gg * 32:(gg + 1) * 32, gg] = 1.0
    sh["c_gsel"] = f32(gsel)
    sh["c_gexp"] = f32(gsel.T.copy())
    sh["c_diagmask"] = (np.arange(128)[:, None] > np.arange(128)[None, :]
                        ).astype(np.uint8)
    pm = np.zeros((HD, HD), np.float32)
    for i in range(HD // 2):
        pm[2 * i + 1, 2 * i] = -1.0
        pm[2 * i, 2 * i + 1] = 1.0
    sh["c_rope_perm"] = bfa(pm)
    sh["c_onescol_f32"] = np.ones((128, 1), np.float32)
    kr = np.zeros((4, S), np.float32)
    kr[0:3, :] = 1.0
    sh["c_kaugrows"] = bfa(kr)
    e3 = np.zeros((1, 12), np.float32)
    e3[0, 0] = 1.0; e3[0, 5] = 1.0; e3[0, 10] = 1.0
    sh["c_eye4"] = bfa(e3)
    cp = np.zeros((128, 8), np.float32)
    cp[:, 0] = np.pi / 2
    cp[:, 1] = 0.5
    cp[:, 2] = -5.0 * math.log(2.0)
    sh["c_pool"] = f32(cp)

    x = np.asarray(inputs["L0_attn_x"], np.float32)
    maps = []
    for cid in range(8):
        m = dict(sh)
        xc = x[2 * cid:2 * cid + 2]
        xt = np.ascontiguousarray(xc.transpose(2, 0, 1).reshape(D, TOK))
        m["x0"] = np.ascontiguousarray(xt.reshape(NCH, 128, TOK))
        maps.append(m)
    return maps



class Kern:
    def __init__(self, nc, P, stack):
        self.nc = nc
        self.P = P
        self.stack = stack
        self.g = {}
        self.c = {}

    def sb(self, stk, name, shape, dt=F32):
        return stk.enter_context(self.nc.sbuf_tensor("sb_" + name, shape, dt))

    def ps8(self, stk, tag):
        return [stk.enter_context(self.nc.psum_tensor(f"ps_{tag}_{i}", [128, 512], F32))
                for i in range(8)]


DRAM_SPECS = {
    "x0": ([NCH, 128, TOK], F32),
    "out": ([NCH, 128, TOK], F32),
    "xres": ([NCH, 128, TOK], F32),
    "xres2": ([NCH, 128, TOK], F32),
    "xln": ([NCH, 128, TOK], BF16),
    "xln2": ([NCH, 128, TOK], BF16),
    "hscr": ([FCH, 128, S], BF16),
    "hscr2": ([FCH, 128, S], BF16),
}

WEIGHT_SPECS = {
    "l0a_wq": [D, D], "l0a_wk": [D, D], "l0a_wv": [D, D], "l0a_wg": [D, D],
    "l0a_wa": [D, 2 * D], "l0a_wo": [D, D],
    "l0a_w1m": [FCH, D, 128], "l0a_w2m": [NCH, F, 128],
    "l0a_gns": [128, NCH], "l0a_gnb": [128, NCH],
    "l0a_ln1s": [128, NCH], "l0a_ln1b": [128, NCH],
    "l0a_ln2s": [128, NCH], "l0a_ln2b": [128, NCH],
    "l0a_b1": [128, FCH], "l0a_b2": [128, NCH],
    "l1a_wot": [D, D], "l1a_wdt": [D, H],
    "l1a_bo": [128, NCH], "l1a_bd_row": [1, H],
    "l1a_gamma": [128, NCH], "l1a_beta": [128, NCH],
    "l1a_gamma_t": [NCH, 128], "l1a_beta_t": [NCH, 128],
    "c_ones1x128_f32": [1, 128], "c_onescol_f32": [128, 1],
    "c_iota_row": [1, S], "c_iota2d": [128, 16],
    "c_ropecos": [HD, S], "c_ropesin": [HD, S],
    "c_gsel": [128, 4], "c_gexp": [4, 128], "c_pool": [128, 8],
}
for _pre in ("l0f", "l1f"):
    WEIGHT_SPECS[f"{_pre}_w1m"] = [FCH, D, 128]
    WEIGHT_SPECS[f"{_pre}_w2m"] = [FCH, D, 128]
    WEIGHT_SPECS[f"{_pre}_h1m"] = [FCH, F, 128]
    WEIGHT_SPECS[f"{_pre}_h2m"] = [FCH, F, 128]
    WEIGHT_SPECS[f"{_pre}_w3m"] = [NCH, F, 128]
    for _s in ("s1", "s2", "sh1", "sh2"):
        WEIGHT_SPECS[f"{_pre}_{_s}"] = [128, FCH]
    WEIGHT_SPECS[f"{_pre}_s3"] = [128, NCH]
for _nm in ("wq", "wk", "wv", "wg"):
    WEIGHT_SPECS[f"l1a_{_nm}h"] = [H, D, HD]
    WEIGHT_SPECS[f"l1a_b{_nm[1]}h"] = [H, HD]
BF16_CONSTS = {"c_id128": [128, 128], "c_ones1x128": [1, 128],
               "c_onescol": [128, 1], "c_rope_perm": [HD, HD],
               "c_kaugrows": [4, S], "c_eye4": [1, 12]}


def declare_io(K, stage_ext=None):
    nc = K.nc
    ext = stage_ext or {}
    for name, (shape, dt) in DRAM_SPECS.items():
        kind = ("Internal" if name not in ("x0", "out") else
                ("ExternalInput" if name == "x0" else "ExternalOutput"))
        if name in ext:
            kind = "ExternalInput" if ext[name] == "in" else "ExternalOutput"
        K.g[name] = nc.dram_tensor(name, shape, dt, kind=kind).ap()
    for name, shape in WEIGHT_SPECS.items():
        K.g[name] = nc.dram_tensor(name, shape, F32, kind="ExternalInput").ap()
    for name, shape in BF16_CONSTS.items():
        K.g[name] = nc.dram_tensor(name, shape, BF16, kind="ExternalInput").ap()
    K.g["c_diagmask"] = nc.dram_tensor("c_diagmask", [128, 128], U8,
                                       kind="ExternalInput").ap()


def load_consts(K, stk):
    P = K.P
    c = K.c
    for nm, shape, dt in (
            ("c_id128", [128, 128], BF16), ("c_ones1x128", [1, 128], BF16),
            ("c_onescol", [128, 1], BF16), ("c_ones1x128_f32", [1, 128], F32),
            ("c_onescol_f32", [128, 1], F32), ("c_iota2d", [128, 16], F32),
            ("c_gsel", [128, 4], F32), ("c_gexp", [4, 128], F32),
            ("c_pool", [128, 8], F32), ("c_diagmask", [128, 128], U8)):
        c[nm] = K.sb(stk, "sb" + nm, shape, dt)
        P.dma("sync", c[nm][:], K.g[nm])
    P.barrier()



class TDefer:

    def __init__(self, P):
        self.P = P
        self.sem = P.sem("t3")
        self.pend = []

    def ten(self, fn, waits):
        t = self.sem.bump(1)
        def emit(fn=fn, waits=tuple(waits)):
            self.P.on("tensor", fn, waits=waits, inc=(self.sem, 1))
        self.pend.append(emit)
        return (self.sem, t)

    def flush_one(self):
        if self.pend:
            self.pend.pop(0)()

    def flush(self):
        while self.pend:
            self.pend.pop(0)()


class Boundary:

    def __init__(self, K, stk, tag, banks):
        self.K = K
        self.tag = tag
        self.pb = banks
        self.Bbf = K.sb(stk, f"{tag}_bbf", [128, 8 * 512], BF16)
        self.Bsq = K.sb(stk, f"{tag}_bsq", [128, 8 * 512], BF16)
        self.T1 = K.sb(stk, f"{tag}_t1", [128, 8 * 512], F32)
        self.Xr = K.sb(stk, f"{tag}_xr", [128, 8 * 512], F32)
        self.rows = K.sb(stk, f"{tag}_rows", [1, 512 * 4], F32)
        self.war_bbf = None

    def row(self, i):
        return self.rows[0:1, 512 * i:512 * (i + 1)]

    def ln_rows(self, nfeat, eps, after_te):
        K, P = self.K, self.K.P
        sV, sS = P.sem("v"), P.sem("s")
        pSa, pSb = self.pb[0][0:1, :], self.pb[1][0:1, :]
        mu, r, nmr, va = self.row(0), self.row(1), self.row(2), self.row(3)
        inv = 1.0 / nfeat
        tV = sV.bump(1)
        P.on("vector", lambda e: (
            e.tensor_scalar(out=mu, in0=pSa, scalar1=inv, scalar2=None, op0=ALU.mult),
            e.tensor_scalar(out=va, in0=pSb, scalar1=inv, scalar2=None, op0=ALU.mult),
            e.tensor_tensor(out=nmr, in0=mu, in1=mu, op=ALU.mult),
            e.tensor_tensor(out=va, in0=va, in1=nmr, op=ALU.subtract),
            e.tensor_scalar(out=va, in0=va, scalar1=eps, scalar2=None, op0=ALU.add),
        )[-1], waits=[after_te], inc=(sV, 1))
        tS = sS.bump(1)
        P.on("scalar", lambda e: (
            e.activation(out=r, in_=va, func=AF.Ln),
            e.activation(out=r, in_=r, func=AF.Exp, scale=-0.5),
        )[-1], waits=[(sV, tV)], inc=(sS, 1))
        tV2 = sV.bump(1)
        P.on("vector", lambda e: (
            e.tensor_tensor(out=nmr, in0=mu, in1=r, op=ALU.mult),
            e.tensor_scalar(out=nmr, in0=nmr, scalar1=-1.0, scalar2=None,
                            op0=ALU.mult),
        )[-1], waits=[(sS, tS)], inc=(sV, 1))
        return tV2

    def stats(self, src_tiles, after):
        K, P = self.K, self.K.P
        sP, sS = P.sem("p"), P.sem("s")
        guard = [after]
        if self.war_bbf is not None:
            guard.append(self.war_bbf)
        n = len(src_tiles)
        tP = sP.bump(1)
        def cast_all(e):
            r = None
            for m in range(n):
                r = e.tensor_copy(out=self.Bbf[:, 512 * m:512 * (m + 1)],
                                  in_=src_tiles[m])
            return r
        P.on("gpsimd", cast_all, waits=guard, inc=(sP, 1))
        tS = sS.bump(1)
        def sq_all(e):
            r = None
            for m in range(n):
                r = e.activation(out=self.Bsq[:, 512 * m:512 * (m + 1)],
                                 in_=src_tiles[m], func=AF.Square)
            return r
        P.on("scalar", sq_all, waits=guard, inc=(sS, 1))
        oc = K.c["c_onescol"]
        pSa, pSb = self.pb[0][0:1, :], self.pb[1][0:1, :]
        def sums(e):
            r = None
            for m in range(n):
                e.matmul(pSa, oc[:], self.Bbf[:, 512 * m:512 * (m + 1)],
                         start=(m == 0), stop=(m == n - 1))
            for m in range(n):
                r = e.matmul(pSb, oc[:], self.Bsq[:, 512 * m:512 * (m + 1)],
                             start=(m == 0), stop=(m == n - 1))
            return r
        return self.K.td.ten(sums, [(sP, tP), (sS, tS)])

    def bcast(self, after_rows):
        K, P = self.K, self.K.P
        o1 = K.c["c_ones1x128_f32"]
        c1, c2, r1, r2 = self.pb[2][:], self.pb[3][:], self.row(1), self.row(2)
        def bc(e):
            e.matmul(c1, o1[:], r1, start=True, stop=True)
            return e.matmul(c2, o1[:], r2, start=True, stop=True)
        return self.K.td.ten(bc, [(P.sem("v"), after_rows)])

    def run_tt(self, b, tt, Bsrc, after_B, resid_in, res_out, ln_out,
               affine=None, final=False, eps_pre=1e-6):
        K, P = self.K, self.K.P
        g = K.g
        tok0 = b * S + tt * 512
        sV = P.sem("v")
        sAd = P.sem("scalar_dma")
        C1, C2 = self.pb[2][:], self.pb[3][:]
        cur = Bsrc
        cur_guard = after_B
        if affine is not None:
            st = self.stats(cur, cur_guard)
            rv = self.ln_rows(D, eps_pre, st)
            bt = self.bcast(rv)
            spp, bpp = affine
            tV = sV.bump(1)
            def apply_aff(e, cur=cur):
                r = None
                for m in range(8):
                    t1 = self.T1[:, 512 * m:512 * (m + 1)]
                    e.tensor_tensor(out=t1, in0=cur[m], in1=C1, op=ALU.mult)
                    e.tensor_tensor(out=t1, in0=t1, in1=C2, op=ALU.add)
                    r = e.tensor_scalar(out=t1, in0=t1, scalar1=spp[:, m:m + 1],
                                        scalar2=bpp[:, m:m + 1], op0=ALU.mult,
                                        op1=ALU.add)
                return r
            P.on("vector", apply_aff, waits=[cur_guard, bt], inc=(sV, 1))
            guard_B = tV
            cur = [self.T1[:, 512 * m:512 * (m + 1)] for m in range(8)]
            cur_guard = (sV, tV)
        st = self.stats(cur, cur_guard)
        rv = self.ln_rows(D, 1e-5, st)
        bt = self.bcast(rv)
        warx = [(sV, sV.n)]
        for m in range(8):
            P.dma("scalar", self.Xr[:, 512 * m:512 * (m + 1)],
                  g[resid_in][m, :, tok0:tok0 + 512], waits=warx)
        ld_tgt = sAd.n
        tV = sV.bump(1)
        def apply_res(e, cur=cur):
            r = None
            for m in range(8):
                dst = self.T1[:, 512 * m:512 * (m + 1)]
                e.tensor_tensor(out=dst, in0=cur[m], in1=C1, op=ALU.mult)
                e.tensor_tensor(out=dst, in0=dst, in1=C2, op=ALU.add)
                r = e.tensor_tensor(out=dst, in0=dst,
                                    in1=self.Xr[:, 512 * m:512 * (m + 1)],
                                    op=ALU.add)
            return r
        P.on("vector", apply_res, waits=[cur_guard, bt, (sAd, ld_tgt)], inc=(sV, 1))
        ret_guard = (sV, tV)
        xnew = [self.T1[:, 512 * m:512 * (m + 1)] for m in range(8)]
        out_name = "out" if final else res_out
        for m in range(8):
            P.dma("scalar", g[out_name][m, :, tok0:tok0 + 512], xnew[m],
                  waits=[(sV, tV)])
        if final:
            return ret_guard
        st = self.stats(xnew, (sV, tV))
        rv = self.ln_rows(D, 1e-5, st)
        bt = self.bcast(rv)
        tV2 = sV.bump(1)
        def apply_ln(e):
            r = None
            for m in range(8):
                xr = self.Xr[:, 512 * m:512 * (m + 1)]
                e.tensor_tensor(out=xr, in0=xnew[m], in1=C1, op=ALU.mult)
                r = e.tensor_tensor(out=self.Bbf[:, 512 * m:512 * (m + 1)],
                                    in0=xr, in1=C2, op=ALU.add)
            return r
        P.on("vector", apply_ln,
             waits=[(sV, tV), bt] + ([self.war_bbf] if self.war_bbf else []),
             inc=(sV, 1))
        for m in range(8):
            P.dma("scalar", g[ln_out][m, :, tok0:tok0 + 512],
                  self.Bbf[:, 512 * m:512 * (m + 1)], waits=[(sV, tV2)])
        self.war_bbf = (P.sem("scalar_dma"), P.sem("scalar_dma").n)
        return ret_guard



def phase_ln0(K):
    P = K.P
    g = K.g
    with contextlib.ExitStack() as stk:
        ps = K.ps8(stk, "ln0")
        bd = Boundary(K, stk, "ln0", ps[4:8])
        X = K.sb(stk, "ln0_x", [128, 8 * 512], F32)
        sA = P.sem("sync_dma")
        sV = P.sem("v")
        for b in range(NB):
            for tt in range(NT):
                tok0 = b * S + tt * 512
                war = [(sV, sV.n)]
                for m in range(8):
                    P.dma("sync", X[:, 512 * m:512 * (m + 1)],
                          g["x0"][m, :, tok0:tok0 + 512], waits=war)
                ldt = sA.n
                xt = [X[:, 512 * m:512 * (m + 1)] for m in range(8)]
                st = bd.stats(xt, (sA, ldt))
                rv = bd.ln_rows(D, 1e-5, st)
                bt = bd.bcast(rv)
                K.td.flush()
                tV = sV.bump(1)
                def apply_ln(e, xt=xt):
                    r = None
                    for m in range(8):
                        t1 = bd.T1[:, 512 * m:512 * (m + 1)]
                        e.tensor_tensor(out=t1, in0=xt[m], in1=bd.pb[2][:],
                                        op=ALU.mult)
                        r = e.tensor_tensor(out=bd.Bsq[:, 512 * m:512 * (m + 1)],
                                            in0=t1, in1=bd.pb[3][:], op=ALU.add)
                    return r
                P.on("vector", apply_ln, waits=[bt], inc=(sV, 1))
                for m in range(8):
                    P.dma("sync", g["xln"][m, :, tok0:tok0 + 512],
                          bd.Bsq[:, 512 * m:512 * (m + 1)], waits=[(sV, tV)])
                bd.war_bbf = (sA, sA.n)
        P.barrier()



def phase_ffn(K, pre, resid_in, xln_in, res_out, ln_out, final=False):
    P = K.P
    g = K.g
    with contextlib.ExitStack() as svk:
        svec = {}
        for snm, w in (("s1", FCH), ("s2", FCH), ("sh1", FCH), ("sh2", FCH),
                       ("s3", NCH)):
            svec[snm] = K.sb(svk, f"{pre}_{snm}", [128, w], F32)
            P.dma("sync", svec[snm][:], g[f"{pre}_{snm}"])
        sW = P.sem("gpsimd_dma")
        sA = P.sem("sync_dma")
        sT = P.sem("t")
        sV = P.sem("v")
        sS = P.sem("s")
        war = {}
        P.barrier()

        for b in range(NB):
            tok0 = b * S
            with contextlib.ExitStack() as hstk:
                hid = K.sb(hstk, f"{pre}_hid{b}", [128, FCH * S], BF16)
                self_dummy = None
                stk = contextlib.ExitStack()
                ps = K.ps8(stk, f"{pre}a{b}")
                xlnb = K.sb(stk, f"{pre}_xln{b}", [128, NCH * S], BF16)
                wa = [K.sb(stk, f"{pre}_wa{i}{b}", [128, NCH * 128], BF16)
                      for i in range(4)]
                sil = K.sb(stk, f"{pre}_sil{b}", [128, 4 * 512], F32)
                for m in range(8):
                    P.dma("sync", xlnb[:, m * S:(m + 1) * S],
                          g[xln_in][m, :, tok0:tok0 + S], waits=[(sT, sT.n)])
                xln_tgt = sA.n
                for m in range(FCH):
                    wb1, wb2 = wa[(m % 2) * 2], wa[(m % 2) * 2 + 1]
                    for wi, wnm in ((0, "w1m"), (1, "w2m")):
                        wk_key = ("wa", b, (m % 2) * 2 + wi)
                        P.dma("gpsimd",
                              (wb1 if wi == 0 else wb2)[:]
                              .rearrange("p (k j) -> p k j", k=NCH),
                              g[f"{pre}_{wnm}"][m].rearrange("(k p) j -> p k j", p=128),
                              waits=[war[wk_key]] if wk_key in war else ())
                    w_tgt = sW.n
                    def mm1(e, wb1=wb1):
                        r = None
                        for tt in range(4):
                            for k in range(NCH):
                                r = e.matmul(
                                    ps[tt][:], wb1[:, k * 128:(k + 1) * 128],
                                    xlnb[:, k * S + tt * 512:k * S + (tt + 1) * 512],
                                    start=(k == 0), stop=(k == NCH - 1))
                        return r
                    t_mm1 = sT.bump(1)
                    P.on("tensor", mm1,
                         waits=[(sW, w_tgt), (sA, xln_tgt), (sV, sV.n), (sS, sS.n)],
                         inc=(sT, 1))
                    war[("wa", b, (m % 2) * 2)] = (sT, t_mm1)
                    t_sil = sS.bump(1)
                    def ev1(e, m=m):
                        r = None
                        for tt in range(4):
                            r = e.activation(out=sil[:, tt * 512:(tt + 1) * 512],
                                             in_=ps[tt][:], func=AF.Silu,
                                             scale=svec["s1"][:, m:m + 1])
                        return r
                    P.on("scalar", ev1, waits=[(sT, t_mm1)], inc=(sS, 1))
                    def mm2(e, wb2=wb2):
                        r = None
                        for tt in range(4):
                            for k in range(NCH):
                                r = e.matmul(
                                    ps[4 + tt][:], wb2[:, k * 128:(k + 1) * 128],
                                    xlnb[:, k * S + tt * 512:k * S + (tt + 1) * 512],
                                    start=(k == 0), stop=(k == NCH - 1))
                        return r
                    t_mm2 = sT.bump(1)
                    P.on("tensor", mm2, waits=[(sV, sV.n)], inc=(sT, 1))
                    war[("wa", b, (m % 2) * 2 + 1)] = (sT, t_mm2)
                    t_u = sV.bump(1)
                    def ev2(e, m=m):
                        r = None
                        for tt in range(4):
                            r = e.scalar_tensor_tensor(
                                out=hid[:, m * S + tt * 512:m * S + (tt + 1) * 512],
                                in0=ps[4 + tt][:], scalar=svec["s2"][:, m:m + 1],
                                in1=sil[:, tt * 512:(tt + 1) * 512],
                                op0=ALU.mult, op1=ALU.mult)
                        return r
                    P.on("vector", ev2, waits=[(sT, t_mm2), (sS, t_sil)],
                         inc=(sV, 1))
                if FFN_DEBUG == 1:
                    for kk in range(FCH):
                        P.dma("sync", g["hscr2"][kk], hid[:, kk * S:(kk + 1) * S],
                              waits=[(sV, sV.n)] if kk == 0 else ())
                P.barrier()
                stk.close()
                stk = contextlib.ExitStack()
                ps = K.ps8(stk, f"{pre}bc{b}")
                wtile = [K.sb(stk, f"{pre}_wt{i}{b}", [128, FCH * 128], BF16)
                         for i in range(2)]
                vstg = [K.sb(stk, f"{pre}_vs{i}{b}", [128, S], BF16)
                        for i in range(2)]
                stages_bc = (("B", "h1m", "sh1", None, "hscr"),) if FFN_DEBUG == 1 else (
                    ("B", "h1m", "sh1", None, "hscr"),
                    ("C", "h2m", "sh2", "hscr", "hscr2"))
                for stage, wnm, snm, src_dram, dst_dram in stages_bc:
                    if src_dram is not None:
                        for kk in range(FCH):
                            P.dma("sync", hid[:, kk * S:(kk + 1) * S],
                                  g[src_dram][kk],
                                  waits=[(sT, sT.n), (sA, sA.n)] if kk == 0 else ())
                    hid_tgt = sA.n
                    for m in range(FCH):
                        wb = wtile[m % 2]
                        wt_key = ("wt", b, m % 2)
                        P.dma("gpsimd", wb[:].rearrange("p (k j) -> p k j", k=FCH),
                              g[f"{pre}_{wnm}"][m].rearrange("(k p) j -> p k j", p=128),
                              waits=[war[wt_key]] if wt_key in war else ())
                        w_tgt = sW.n
                        def mmh(e, wb=wb, m=m):
                            r = None
                            for k in range(FCH):
                                for tt in range(4):
                                    r = e.matmul(
                                        ps[(m % 2) * 4 + tt][:],
                                        wb[:, k * 128:(k + 1) * 128],
                                        hid[:, k * S + tt * 512:k * S + (tt + 1) * 512],
                                        start=(k == 0), stop=(k == FCH - 1))
                            return r
                        t_mm = sT.bump(1)
                        P.on("tensor", mmh,
                             waits=[(sW, w_tgt), (sA, hid_tgt), (sS, sS.n)],
                             inc=(sT, 1))
                        war[("wt", b, m % 2)] = (sT, t_mm)
                        t_g = sS.bump(1)
                        vs = vstg[m % 2]
                        def evg(e, m=m, vs=vs, snm=snm):
                            r = None
                            for tt in range(4):
                                r = e.activation(
                                    out=vs[:, tt * 512:(tt + 1) * 512],
                                    in_=ps[(m % 2) * 4 + tt][:], func=AF.Gelu,
                                    scale=svec[snm][:, m:m + 1])
                            return r
                        P.on("scalar", evg, waits=[(sT, t_mm), (sA, sA.n)],
                             inc=(sS, 1))
                        P.dma("sync", g[dst_dram][m], vs[:], waits=[(sS, t_g)])
                P.barrier()
                stk.close()
            if FFN_DEBUG:
                continue
            with contextlib.ExitStack() as stk:
                ps = K.ps8(stk, f"{pre}d{b}")
                w3sb = K.sb(stk, f"{pre}_w3sb{b}", [128, NCH * FCH * 128], BF16)
                wslab = [K.sb(stk, f"{pre}_ws{i}{b}", [128, 512], BF16)
                         for i in range(4)]
                Bout = [K.sb(stk, f"{pre}_bo{i}{b}", [128, 8 * 512], F32)
                        for i in range(2)]
                bd = Boundary(K, stk, f"{pre}_bd{b}", ps[4:8])
                for m in range(NCH):
                    P.dma("gpsimd",
                          w3sb[:].rearrange("p (k m j) -> p k m j", k=FCH, m=NCH)
                          [:, :, m, :],
                          g[f"{pre}_w3m"][m].rearrange("(k p) j -> p k j", p=128))
                w3_tgt = sW.n
                for tt in range(4):
                    Bo = Bout[tt % 2]
                    t_evs = []
                    for wave in range(2):
                        for k in range(FCH):
                            wsl = wslab[(k + wave * FCH) % 4]
                            ws_key = ("wsl", pre, b, (k + wave * FCH) % 4)
                            P.dma("sync", wsl[:],
                                  g["hscr2"][k][:, tt * 512:(tt + 1) * 512],
                                  waits=[war[ws_key]] if ws_key in war else ())
                            sl_tgt = sA.n
                            waits = [(sA, sl_tgt), (sW, w3_tgt)]
                            if k == 0:
                                bw = war.get(("dbank", pre, b))
                                if bw is not None:
                                    waits.append(bw)
                            def mmd(e, wsl=wsl, k=k, wave=wave):
                                r = None
                                for mi in range(4):
                                    m = wave * 4 + mi
                                    r = e.matmul(
                                        ps[mi][:],
                                        w3sb[:, (k * NCH + m) * 128:
                                             (k * NCH + m + 1) * 128],
                                        wsl[:], start=(k == 0),
                                        stop=(k == FCH - 1))
                                return r
                            t_mm = sT.bump(1)
                            P.on("tensor", mmd, waits=waits, inc=(sT, 1))
                            war[ws_key] = (sT, t_mm)
                            K.td.flush_one()
                        t_ev = sV.bump(1)
                        def evd(e, wave=wave, Bo=Bo):
                            r = None
                            for mi in range(4):
                                m = wave * 4 + mi
                                r = e.tensor_scalar(
                                    out=Bo[:, m * 512:(m + 1) * 512],
                                    in0=ps[mi][:],
                                    scalar1=svec["s3"][:, m:m + 1], scalar2=None,
                                    op0=ALU.mult)
                            return r
                        bo_w = war.get(("bout", pre, b, tt % 2))
                        P.on("vector", evd,
                             waits=[(sT, sT.n)] + ([bo_w] if bo_w else []),
                             inc=(sV, 1))
                        war[("dbank", pre, b)] = (sV, t_ev)
                        t_evs.append(t_ev)
                    Bt = [Bo[:, m * 512:(m + 1) * 512] for m in range(NCH)]
                    rg = bd.run_tt(b, tt, Bt, (sV, t_evs[-1]), resid_in, res_out,
                                   ln_out, affine=None, final=final)
                    war[("bout", pre, b, tt % 2)] = rg
                K.td.flush()
                P.barrier()
    P.barrier()


def phase_l0attn(K, resid_in, xln_in, res_out, ln_out):
    P = K.P
    g = K.g
    c = K.c
    sW = P.sem("gpsimd_dma")
    sA = P.sem("sync_dma")
    sT = P.sem("t")
    sV = P.sem("v")
    sS = P.sem("s")
    sT2 = P.sem("t2")
    war = {}
    GEPS = 1e-6
    GINV = 1.0 / (S * 32)

    with contextlib.ExitStack() as pstk:
        vec = {}
        for nm in ("gns", "gnb", "ln1s", "ln1b", "ln2s", "ln2b", "b2"):
            vec[nm] = K.sb(pstk, f"l0a_{nm}v", [128, NCH], F32)
            P.dma("sync", vec[nm][:], g[f"l0a_{nm}"])
        vec["b1"] = K.sb(pstk, "l0a_b1v", [128, FCH], F32)
        P.dma("sync", vec["b1"][:], g["l0a_b1"])
        P.barrier()

        for b in range(NB):
            tok0 = b * S
            with contextlib.ExitStack() as bstk:
                zb = K.sb(bstk, f"l0a_z{b}", [128, NCH * S], BF16)
                with contextlib.ExitStack() as stk:
                    ps = K.ps8(stk, f"l0ap{b}")
                    xlnb = K.sb(stk, f"l0a_xln{b}", [128, NCH * S], BF16)
                    T = {nm: K.sb(stk, f"l0a_{nm}{b}", [128, S], F32)
                         for nm in ("tq", "tk", "tv", "tar", "tai", "tg",
                                    "sc1", "sc2", "sc3")}
                    wsl = [K.sb(stk, f"l0a_ws{i}{b}", [128, NCH * 128], BF16)
                           for i in range(12)]
                    Sg = K.sb(stk, f"l0a_sg{b}", [128, 8], F32)
                    Sg2 = K.sb(stk, f"l0a_sg2{b}", [8, 8], F32)
                    lnbf = K.sb(stk, f"l0a_lbf{b}", [128, 8 * 512], BF16)
                    lnsq = K.sb(stk, f"l0a_lsq{b}", [128, 8 * 512], BF16)
                    rows = K.sb(stk, f"l0a_rows{b}", [1, 4 * 512], F32)
                    for m in range(8):
                        P.dma("sync", xlnb[:, m * S:(m + 1) * S],
                              g[xln_in][m, :, tok0:tok0 + S],
                              waits=[(sT, sT.n)])
                    xln_tgt = sA.n
                    pool_pi2 = c["c_pool"][:, 0:1]
                    pool_half = c["c_pool"][:, 1:2]

                    pending_stats = []
                    for ch in range(NCH):
                        wset = wsl[(ch % 2) * 6:(ch % 2) * 6 + 6]
                        srcs = [("l0a_wq", ch * 128), ("l0a_wk", ch * 128),
                                ("l0a_wv", ch * 128), ("l0a_wg", ch * 128),
                                ("l0a_wa", ch * 128), ("l0a_wa", D + ch * 128)]
                        wq_key = ("l0aw", b, ch % 2)
                        for wi, (wn, off) in enumerate(srcs):
                            P.dma("gpsimd",
                                  wset[wi][:].rearrange("p (k j) -> p k j", k=NCH),
                                  g[wn][:, off:off + 128]
                                  .rearrange("(k p) j -> p k j", p=128),
                                  waits=([war[wq_key]] if wq_key in war else ())
                                  if wi == 0 else ())
                        w_tgt = sW.n
                        quants = [("tq", 0, "dve"), ("tk", 1, "dve"),
                                  ("tv", 2, "dve"), ("tar", 4, "act"),
                                  ("tai", 5, "act"), ("tg", 3, "silu")]
                        for qi, (dnm, wi, ev) in enumerate(quants):
                            if qi == 1 and pending_stats:
                                for f in pending_stats:
                                    f()
                                pending_stats = []
                            bank = (qi % 2) * 4
                            def mmq(e, wi=wi, bank=bank, wset=wset):
                                r = None
                                for tt in range(4):
                                    for k in range(NCH):
                                        r = e.matmul(
                                            ps[bank + tt][:],
                                            wset[wi][:, k * 128:(k + 1) * 128],
                                            xlnb[:, k * S + tt * 512:
                                                 k * S + (tt + 1) * 512],
                                            start=(k == 0), stop=(k == NCH - 1))
                                return r
                            t_mm = sT.bump(1)
                            bank_war = war.get(("l0abank", b, qi % 2), ())
                            P.on("tensor", mmq,
                                 waits=[(sW, w_tgt), (sA, xln_tgt)] + list(bank_war),
                                 inc=(sT, 1))
                            war[("l0aw", b, ch % 2)] = (sT, t_mm)
                            if ev == "dve":
                                t_e = sV.bump(1)
                                def evq(e, dnm=dnm, bank=bank):
                                    r = None
                                    for tt in range(4):
                                        r = e.tensor_copy(
                                            out=T[dnm][:, tt * 512:(tt + 1) * 512],
                                            in_=ps[bank + tt][:])
                                    return r
                                P.on("vector", evq, waits=[(sT, t_mm)], inc=(sV, 1))
                                war[("l0abank", b, qi % 2)] = ((sV, t_e),)
                            else:
                                fn = AF.Silu if ev == "silu" else AF.Identity
                                t_e = sS.bump(1)
                                def evq(e, dnm=dnm, bank=bank, fn=fn):
                                    r = None
                                    for tt in range(4):
                                        r = e.activation(
                                            out=T[dnm][:, tt * 512:(tt + 1) * 512],
                                            in_=ps[bank + tt][:], func=fn)
                                    return r
                                P.on("scalar", evq, waits=[(sT, t_mm), (sV, sV.n)],
                                     inc=(sS, 1))
                                war[("l0abank", b, qi % 2)] = ((sS, t_e),)
                        evac_v = sV.n
                        evac_s = sS.n

                        def dve(fn, waits=()):
                            t = sV.bump(1)
                            P.on("vector", fn, waits=waits, inc=(sV, 1))
                            return t

                        def act(fn, waits=()):
                            t = sS.bump(1)
                            P.on("scalar", fn, waits=waits, inc=(sS, 1))
                            return t

                        tq, tk, tv = T["tq"][:], T["tk"][:], T["tv"][:]
                        tar, tai, tg = T["tar"][:], T["tai"][:], T["tg"][:]
                        s1, s2, s3 = T["sc1"][:], T["sc2"][:], T["sc3"][:]
                        dve(lambda e: (
                            e.tensor_tensor(out=tk, in0=tk, in1=tv, op=ALU.mult),
                            e.tensor_tensor(out=s1, in0=tar, in1=tar, op=ALU.mult),
                            e.tensor_tensor(out=s2, in0=tai, in1=tai, op=ALU.mult),
                            e.tensor_tensor(out=s1, in0=s1, in1=s2, op=ALU.add),
                        )[-1], waits=[(sS, evac_s)])
                        act(lambda e: (
                            e.activation(out=tv, in_=s1, func=AF.Sqrt),
                            e.activation(out=tv, in_=tv, func=AF.Sigmoid),
                            e.activation(out=s2, in_=tar, func=AF.Sign),
                            e.activation(out=s2, in_=s2, func=AF.Sign, bias=pool_half),
                            e.activation(out=s1, in_=tar, func=AF.Abs),
                        )[-1], waits=[(sV, sV.n)])
                        dve(lambda e: (
                            e.tensor_scalar(out=s1, in0=s1, scalar1=1e-30,
                                            scalar2=None, op0=ALU.max),
                        )[-1], waits=[(sS, sS.n)])
                        act(lambda e: (
                            e.activation(out=s1, in_=s1, func=AF.Ln),
                            e.activation(out=s1, in_=s1, func=AF.Exp, scale=-1.0),
                        )[-1], waits=[(sV, sV.n)])
                        dve(lambda e: (
                            e.tensor_tensor(out=s1, in0=s1, in1=s2, op=ALU.mult),
                            e.tensor_tensor(out=s1, in0=tai, in1=s1, op=ALU.mult),
                        )[-1], waits=[(sS, sS.n)])
                        act(lambda e: (
                            e.activation(out=s1, in_=s1, func=AF.Arctan),
                            e.activation(out=tar, in_=tai, func=AF.Sign),
                        )[-1], waits=[(sV, sV.n)])
                        dve(lambda e: (
                            e.tensor_scalar(out=s3, in0=s2, scalar1=-PI / 2,
                                            scalar2=PI / 2, op0=ALU.mult, op1=ALU.add),
                            e.tensor_tensor(out=s3, in0=s3, in1=tar, op=ALU.mult),
                            e.tensor_tensor(out=s1, in0=s1, in1=s3, op=ALU.add),
                            e.tensor_tensor_scan(out=tai, data0=s1, data1=s1,
                                                 initial=0.0, op0=ALU.add,
                                                 op1=ALU.bypass),
                            e.tensor_scalar(out=s2, in0=tai, scalar1=INV2PI,
                                            scalar2=None, op0=ALU.mult),
                            e.tensor_scalar(out=s3, in0=s2, scalar1=MAGIC,
                                            scalar2=MAGIC, op0=ALU.add,
                                            op1=ALU.subtract),
                            e.scalar_tensor_tensor(out=s3, in0=s3, scalar=-TWOPI,
                                                   in1=tai, op0=ALU.mult,
                                                   op1=ALU.add),
                            e.tensor_scalar(out=s2, in0=s2, scalar1=0.25,
                                            scalar2=None, op0=ALU.add),
                            e.tensor_scalar(out=s2, in0=s2, scalar1=MAGIC,
                                            scalar2=MAGIC, op0=ALU.add,
                                            op1=ALU.subtract),
                            e.scalar_tensor_tensor(out=s2, in0=s2, scalar=-TWOPI,
                                                   in1=tai, op0=ALU.mult,
                                                   op1=ALU.add),
                        )[-1], waits=[(sS, sS.n)])
                        act(lambda e: (
                            e.activation(out=s3, in_=s3, func=AF.Sin),
                            e.activation(out=s2, in_=s2, func=AF.Sin, bias=pool_pi2),
                        )[-1], waits=[(sV, sV.n)])
                        t_chain = dve(lambda e: (
                            e.tensor_tensor(out=tar, in0=s2, in1=tk, op=ALU.mult),
                            e.tensor_tensor(out=tai, in0=s3, in1=tk, op=ALU.mult),
                            e.tensor_tensor_scan(out=tk, data0=tv, data1=tar,
                                                 initial=0.0, op0=ALU.mult,
                                                 op1=ALU.add),
                            e.tensor_tensor_scan(out=s1, data0=tv, data1=tai,
                                                 initial=0.0, op0=ALU.mult,
                                                 op1=ALU.add),
                            e.tensor_tensor(out=tar, in0=s2, in1=tk, op=ALU.mult),
                            e.tensor_tensor(out=tai, in0=s3, in1=s1, op=ALU.mult),
                            e.tensor_tensor(out=tar, in0=tar, in1=tai, op=ALU.add),
                            e.tensor_tensor(out=tar, in0=tq, in1=tar, op=ALU.mult),
                            e.tensor_tensor(out=tai, in0=s3, in1=tk, op=ALU.mult),
                            e.tensor_tensor(out=tk, in0=s2, in1=s1, op=ALU.mult),
                            e.tensor_tensor(out=tai, in0=tai, in1=tk, op=ALU.subtract),
                            e.tensor_tensor(out=tai, in0=tq, in1=tai, op=ALU.mult),
                        )[-1], waits=[(sS, sS.n)])
                        dve(lambda e: (
                            e.tensor_reduce(out=Sg[:, 0:1], in_=tar, axis=AX.X,
                                            op=ALU.add),
                            e.drain(),
                            e.tensor_reduce(out=Sg[:, 1:2], in_=tai, axis=AX.X,
                                            op=ALU.add),
                            e.drain(),
                        )[-1])
                        act(lambda e: (
                            e.activation(out=s2, in_=tar, func=AF.Square,
                                         accum_out=Sg[:, 2:3]),
                            e.drain(),
                            e.activation(out=s3, in_=tai, func=AF.Square,
                                         accum_out=Sg[:, 3:4]),
                            e.drain(),
                        )[-1], waits=[(sV, sV.n)])
                        t_sg = dve(lambda e: (
                            e.tensor_tensor(out=Sg[:, 2:3], in0=Sg[:, 2:3],
                                            in1=Sg[:, 3:4], op=ALU.add),
                            e.drain(),
                        )[-1], waits=[(sS, sS.n)])
                        t_g1 = sT2.bump(1)
                        def emit_g1(t_sg=t_sg):
                            P.on("tensor", lambda e: e.matmul(
                                ps[7][0:4, 0:3], c["c_gsel"][:], Sg[:, 0:3],
                                start=True, stop=True),
                                waits=[(sV, t_sg)], inc=(sT2, 1))
                        pending_stats.append(emit_g1)
                        dve(lambda e: (
                            e.tensor_scalar(out=Sg2[0:4, 2:5], in0=ps[7][0:4, 0:3],
                                            scalar1=GINV, scalar2=None, op0=ALU.mult),
                            e.drain(),
                            e.tensor_tensor(out=Sg2[0:4, 5:6], in0=Sg2[0:4, 2:3],
                                            in1=Sg2[0:4, 2:3], op=ALU.mult),
                            e.tensor_tensor(out=Sg2[0:4, 6:7], in0=Sg2[0:4, 3:4],
                                            in1=Sg2[0:4, 3:4], op=ALU.mult),
                            e.drain(),
                            e.tensor_tensor(out=Sg2[0:4, 5:6], in0=Sg2[0:4, 5:6],
                                            in1=Sg2[0:4, 6:7], op=ALU.add),
                            e.drain(),
                            e.tensor_tensor(out=Sg2[0:4, 5:6], in0=Sg2[0:4, 4:5],
                                            in1=Sg2[0:4, 5:6], op=ALU.subtract),
                            e.drain(),
                            e.tensor_scalar(out=Sg2[0:4, 5:6], in0=Sg2[0:4, 5:6],
                                            scalar1=GEPS, scalar2=None, op0=ALU.add),
                            e.drain(),
                        )[-1], waits=[(sT2, t_g1)])
                        act(lambda e: (
                            e.activation(out=Sg2[0:4, 1:2], in_=Sg2[0:4, 5:6],
                                         func=AF.Ln),
                            e.drain(),
                            e.activation(out=Sg2[0:4, 1:2], in_=Sg2[0:4, 1:2],
                                         func=AF.Exp, scale=-0.5),
                            e.drain(),
                        )[-1], waits=[(sV, sV.n)])
                        t_sm = dve(lambda e: (
                            e.tensor_copy(out=Sg2[0:4, 0:1], in_=Sg2[0:4, 2:3]),
                            e.drain(),
                        )[-1], waits=[(sS, sS.n)])
                        t_g2 = sT2.bump(1)
                        def emit_g2(t_sm=t_sm):
                            P.on("tensor", lambda e: e.matmul(
                                ps[7][:, 4:6], c["c_gexp"][:], Sg2[0:4, 0:2],
                                start=True, stop=True),
                                waits=[(sV, t_sm)], inc=(sT2, 1))
                        pending_stats.append(emit_g2)
                        t_ab = dve(lambda e, ch=ch: (
                            e.tensor_tensor(out=Sg[:, 4:5], in0=ps[7][:, 5:6],
                                            in1=vec["gns"][:, ch:ch + 1], op=ALU.mult),
                            e.drain(),
                            e.tensor_tensor(out=Sg[:, 5:6], in0=ps[7][:, 4:5],
                                            in1=Sg[:, 4:5], op=ALU.mult),
                            e.drain(),
                            e.scalar_tensor_tensor(out=Sg[:, 5:6], in0=Sg[:, 5:6],
                                                   scalar=-1.0,
                                                   in1=vec["gnb"][:, ch:ch + 1],
                                                   op0=ALU.mult, op1=ALU.add),
                            e.drain(),
                            e.tensor_scalar(out=s2, in0=tar, scalar1=Sg[:, 4:5],
                                            scalar2=Sg[:, 5:6], op0=ALU.mult,
                                            op1=ALU.add),
                        )[-1], waits=[(sT2, t_g2)])
                        dve(lambda e, ch=ch: e.tensor_tensor(
                            out=zb[:, ch * S:(ch + 1) * S], in0=s2, in1=tg,
                            op=ALU.mult))

                    for f in pending_stats:
                        f()
                    pending_stats = []
                    P.barrier()
                    for mo in range(8):
                        P.dma("gpsimd",
                              wsl[mo][:].rearrange("p (k j) -> p k j", k=NCH),
                              g["l0a_wo"][:, mo * 128:(mo + 1) * 128]
                              .rearrange("(k p) j -> p k j", p=128))
                    wo_tgt = sW.n
                    oc = c["c_onescol"]
                    o1f = c["c_ones1x128_f32"]
                    for tt in range(4):
                        xi = [T["tq"][:, m * 512:(m + 1) * 512] for m in range(4)] + \
                             [T["tk"][:, m * 512:(m + 1) * 512] for m in range(4)]
                        ev_ts = []
                        for wave in range(2):
                            def mmo(e, tt=tt, wave=wave):
                                r = None
                                for mi in range(4):
                                    mo = wave * 4 + mi
                                    for k in range(NCH):
                                        e.matmul(
                                            ps[mi][:],
                                            wsl[mo][:, k * 128:(k + 1) * 128],
                                            zb[:, k * S + tt * 512:
                                               k * S + (tt + 1) * 512],
                                            start=(k == 0), stop=False)
                                    r = e.matmul(
                                        ps[mi][:], c["c_id128"][:],
                                        xlnb[:, mo * S + tt * 512:
                                             mo * S + (tt + 1) * 512],
                                        start=False, stop=True)
                                return r
                            K.td.flush_one()
                            K.td.flush_one()
                            bw = war.get(("wbank", b))
                            t_mm = sT.bump(1)
                            P.on("tensor", mmo,
                                 waits=[(sW, wo_tgt)] + ([bw] if bw else []),
                                 inc=(sT, 1))
                            t_ev = sV.bump(1)
                            def evo(e, xi=xi, wave=wave):
                                r = None
                                for mi in range(4):
                                    r = e.tensor_copy(out=xi[wave * 4 + mi],
                                                      in_=ps[mi][:])
                                return r
                            P.on("vector", evo, waits=[(sT, t_mm)], inc=(sV, 1))
                            war[("wbank", b)] = (sV, t_ev)
                            ev_ts.append(t_ev)
                        t_p = P.sem("p").bump(1)
                        def castf(e, xi=xi):
                            r = None
                            for m in range(8):
                                r = e.tensor_copy(
                                    out=lnbf[:, m * 512:(m + 1) * 512], in_=xi[m])
                            return r
                        P.on("gpsimd", castf, waits=[(sV, ev_ts[-1])],
                             inc=(P.sem("p"), 1))
                        t_sq = sS.bump(1)
                        def sqf(e, xi=xi):
                            r = None
                            for m in range(8):
                                r = e.activation(
                                    out=lnsq[:, m * 512:(m + 1) * 512], in_=xi[m],
                                    func=AF.Square)
                            return r
                        P.on("scalar", sqf, waits=[(sV, ev_ts[-1])], inc=(sS, 1))
                        def sums(e):
                            r = None
                            for m in range(8):
                                e.matmul(ps[4][0:1, :], oc[:],
                                         lnbf[:, m * 512:(m + 1) * 512],
                                         start=(m == 0), stop=(m == 7))
                            for m in range(8):
                                r = e.matmul(ps[5][0:1, :], oc[:],
                                             lnsq[:, m * 512:(m + 1) * 512],
                                             start=(m == 0), stop=(m == 7))
                            return r
                        t_sum = K.td.ten(sums, [(P.sem("p"), t_p), (sS, t_sq)])
                        mu, rr, nmr, va = (rows[0:1, i * 512:(i + 1) * 512]
                                           for i in range(4))
                        t_r1 = sV.bump(1)
                        P.on("vector", lambda e, mu=mu, rr=rr, nmr=nmr, va=va: (
                            e.tensor_scalar(out=mu, in0=ps[4][0:1, :],
                                            scalar1=1.0 / D, scalar2=None,
                                            op0=ALU.mult),
                            e.tensor_scalar(out=va, in0=ps[5][0:1, :],
                                            scalar1=1.0 / D, scalar2=None,
                                            op0=ALU.mult),
                            e.tensor_tensor(out=nmr, in0=mu, in1=mu, op=ALU.mult),
                            e.tensor_tensor(out=va, in0=va, in1=nmr, op=ALU.subtract),
                            e.tensor_scalar(out=va, in0=va, scalar1=GEPS,
                                            scalar2=None, op0=ALU.add),
                        )[-1], waits=[t_sum], inc=(sV, 1))
                        t_r2 = sS.bump(1)
                        P.on("scalar", lambda e, rr=rr, va=va: (
                            e.activation(out=rr, in_=va, func=AF.Ln),
                            e.activation(out=rr, in_=rr, func=AF.Exp, scale=-0.5),
                        )[-1], waits=[(sV, t_r1)], inc=(sS, 1))
                        t_r3 = sV.bump(1)
                        P.on("vector", lambda e, mu=mu, rr=rr, nmr=nmr: (
                            e.tensor_tensor(out=nmr, in0=mu, in1=rr, op=ALU.mult),
                            e.tensor_scalar(out=nmr, in0=nmr, scalar1=-1.0,
                                            scalar2=None, op0=ALU.mult),
                        )[-1], waits=[(sS, t_r2)], inc=(sV, 1))
                        def bc(e, rr=rr, nmr=nmr):
                            e.matmul(ps[6][:], o1f[:], rr, start=True, stop=True)
                            return e.matmul(ps[7][:], o1f[:], nmr, start=True,
                                            stop=True)
                        t_bc = K.td.ten(bc, [(sV, t_r3)])
                        t_ap = sV.bump(1)
                        def apl(e, xi=xi, tt=tt):
                            r = None
                            for m in range(8):
                                sc = T["sc1"][:, m % 4 * 512:(m % 4 + 1) * 512]
                                e.tensor_tensor(out=sc, in0=xi[m], in1=ps[6][:],
                                                op=ALU.mult)
                                e.tensor_tensor(out=sc, in0=sc, in1=ps[7][:],
                                                op=ALU.add)
                                r = e.tensor_scalar(
                                    out=zb[:, m * S + tt * 512:m * S + (tt + 1) * 512],
                                    in0=sc, scalar1=vec["ln1s"][:, m:m + 1],
                                    scalar2=vec["ln1b"][:, m:m + 1],
                                    op0=ALU.mult, op1=ALU.add)
                            return r
                        P.on("vector", apl, waits=[t_bc], inc=(sV, 1))
                    K.td.flush()
                    P.barrier()
                with contextlib.ExitStack() as stk:
                    ps = K.ps8(stk, f"l0am1{b}")
                    wa = [K.sb(stk, f"l0a_w1t{i}{b}", [128, NCH * 128], BF16)
                          for i in range(2)]
                    vstg = [K.sb(stk, f"l0a_vs{i}{b}", [128, S], BF16)
                            for i in range(2)]
                    for m in range(FCH):
                        wb = wa[m % 2]
                        m1_key = ("l0am1", b, m % 2)
                        P.dma("gpsimd", wb[:].rearrange("p (k j) -> p k j", k=NCH),
                              g["l0a_w1m"][m].rearrange("(k p) j -> p k j", p=128),
                              waits=[war[m1_key]] if m1_key in war else ())
                        w_tgt = sW.n
                        def mmh(e, wb=wb, m=m):
                            r = None
                            for tt in range(4):
                                for k in range(NCH):
                                    r = e.matmul(
                                        ps[(m % 2) * 4 + tt][:],
                                        wb[:, k * 128:(k + 1) * 128],
                                        zb[:, k * S + tt * 512:k * S + (tt + 1) * 512],
                                        start=(k == 0), stop=(k == NCH - 1))
                            return r
                        t_mm = sT.bump(1)
                        P.on("tensor", mmh, waits=[(sW, w_tgt), (sS, sS.n)],
                             inc=(sT, 1))
                        war[("l0am1", b, m % 2)] = (sT, t_mm)
                        t_gl = sS.bump(1)
                        vs = vstg[m % 2]
                        def evh(e, m=m, vs=vs):
                            r = None
                            for tt in range(4):
                                r = e.activation(
                                    out=vs[:, tt * 512:(tt + 1) * 512],
                                    in_=ps[(m % 2) * 4 + tt][:],
                                    func=AF.Gelu_apprx_tanh,
                                    bias=vec["b1"][:, m:m + 1])
                            return r
                        P.on("scalar", evh, waits=[(sT, t_mm), (sA, sA.n)],
                             inc=(sS, 1))
                        P.dma("sync", g["hscr"][m], vs[:], waits=[(sS, t_gl)])
                    P.barrier()
                with contextlib.ExitStack() as stk:
                    ps = K.ps8(stk, f"l0am2{b}")
                    w2sb = K.sb(stk, f"l0a_w2sb{b}", [128, NCH * FCH * 128], BF16)
                    wslab = [K.sb(stk, f"l0a_wsl{i}{b}", [128, 512], BF16)
                             for i in range(2)]
                    Bout = [K.sb(stk, f"l0a_bo{i}{b}", [128, 8 * 512], F32)
                            for i in range(2)]
                    bd = Boundary(K, stk, f"l0a_bd{b}", ps[4:8])
                    for m in range(NCH):
                        P.dma("gpsimd",
                              w2sb[:].rearrange("p (k m j) -> p k m j", k=FCH, m=NCH)
                              [:, :, m, :],
                              g["l0a_w2m"][m].rearrange("(k p) j -> p k j", p=128))
                    w2_tgt = sW.n
                    for tt in range(4):
                        Bo = Bout[tt % 2]
                        t_evs = []
                        for wave in range(2):
                            for k in range(FCH):
                                wsl2 = wslab[k % 2]
                                m2_key = ("l0am2", b, k % 2)
                                P.dma("sync", wsl2[:],
                                      g["hscr"][k][:, tt * 512:(tt + 1) * 512],
                                      waits=[war[m2_key]] if m2_key in war else ())
                                sl_tgt = sA.n
                                waits = [(sA, sl_tgt), (sW, w2_tgt)]
                                if k == 0:
                                    bw = war.get(("m2bank", b))
                                    if bw is not None:
                                        waits.append(bw)
                                def mmd(e, wsl2=wsl2, k=k, wave=wave):
                                    r = None
                                    for mi in range(4):
                                        m = wave * 4 + mi
                                        r = e.matmul(
                                            ps[mi][:],
                                            w2sb[:, (k * NCH + m) * 128:
                                                 (k * NCH + m + 1) * 128],
                                            wsl2[:], start=(k == 0), stop=False)
                                    return r
                                t_mm = sT.bump(1)
                                P.on("tensor", mmd, waits=waits, inc=(sT, 1))
                                war[("l0am2", b, k % 2)] = (sT, t_mm)
                                K.td.flush_one()
                            def mmr(e, tt=tt, wave=wave):
                                r = None
                                for mi in range(4):
                                    m = wave * 4 + mi
                                    r = e.matmul(
                                        ps[mi][:], c["c_id128"][:],
                                        zb[:, m * S + tt * 512:m * S + (tt + 1) * 512],
                                        start=False, stop=True)
                                return r
                            t_mm = sT.bump(1)
                            P.on("tensor", mmr, inc=(sT, 1))
                            t_ev = sV.bump(1)
                            def evd(e, wave=wave, Bo=Bo):
                                r = None
                                for mi in range(4):
                                    m = wave * 4 + mi
                                    r = e.tensor_scalar(
                                        out=Bo[:, m * 512:(m + 1) * 512],
                                        in0=ps[mi][:], scalar1=vec["b2"][:, m:m + 1],
                                        scalar2=None, op0=ALU.add)
                                return r
                            bo_w = war.get(("m2bout", b, tt % 2))
                            P.on("vector", evd,
                                 waits=[(sT, t_mm)] + ([bo_w] if bo_w else []),
                                 inc=(sV, 1))
                            war[("m2bank", b)] = (sV, t_ev)
                            t_evs.append(t_ev)
                        Bt = [Bo[:, m * 512:(m + 1) * 512] for m in range(NCH)]
                        rg = bd.run_tt(b, tt, Bt, (sV, t_evs[-1]), resid_in,
                                       res_out, ln_out,
                                       affine=(vec["ln2s"], vec["ln2b"]),
                                       final=False, eps_pre=GEPS)
                        war[("m2bout", b, tt % 2)] = rg
                    K.td.flush()
                    P.barrier()
    P.barrier()


def phase_l1attn(K, resid_in, xln_in, res_out, ln_out):
    P = K.P
    g = K.g
    c = K.c
    sW = P.sem("gpsimd_dma")
    sA = P.sem("sync_dma")
    sT = P.sem("t")
    sV = P.sem("v")
    sS = P.sem("s")
    sPo = P.sem("p")
    war = {}

    with contextlib.ExitStack() as pstk:
        rc = {}
        for nm, shape, dt in (("c_iota_row", [1, S], F32),
                              ("c_ropecos", [HD, S], F32),
                              ("c_ropesin", [HD, S], F32),
                              ("c_rope_perm", [HD, HD], BF16),
                              ("c_eye4", [1, 12], BF16)):
            rc[nm] = K.sb(pstk, "l1" + nm, shape, dt)
            P.dma("sync", rc[nm][:], g[nm])
        vec = {}
        for nm in ("bo", "gamma", "beta"):
            vec[nm] = K.sb(pstk, f"l1a_{nm}v", [128, NCH], F32)
            P.dma("sync", vec[nm][:], g[f"l1a_{nm}"])
        bdrow = K.sb(pstk, "l1a_bdrow", [1, H], F32)
        P.dma("sync", bdrow[:], g["l1a_bd_row"])
        wdsl = K.sb(pstk, "l1a_wdsl", [128, NCH * H], F32)
        P.dma("sync", wdsl[:].rearrange("p (k j) -> p k j", k=NCH),
              g["l1a_wdt"].rearrange("(k p) j -> p k j", p=128))
        P.barrier()

        for b in range(NB):
            tok0 = b * S
            with contextlib.ExitStack() as bstk:
                zb = K.sb(bstk, f"l1a_z{b}", [128, NCH * S], BF16)
                with contextlib.ExitStack() as stk:
                    ps = K.ps8(stk, f"l1h{b}")
                    xlnb = K.sb(stk, f"l1a_xln{b}", [128, NCH * S], BF16)
                    ATj = K.sb(stk, f"l1a_at{b}", [128, 16 * 512], BF16)
                    Qaug = K.sb(stk, f"l1a_qa{b}", [68, S], BF16)
                    Kaug = K.sb(stk, f"l1a_ka{b}", [68, S], BF16)
                    Tt1 = K.sb(stk, f"l1a_t1{b}", [64, S], F32)
                    Tt2 = K.sb(stk, f"l1a_t2{b}", [64, S], F32)
                    SqB = K.sb(stk, f"l1a_sqb{b}", [64, S], BF16)
                    gsil = K.sb(stk, f"l1a_gs{b}", [64, S], BF16)
                    ogh = K.sb(stk, f"l1a_og{b}", [64, S], F32)
                    Vsb = K.sb(stk, f"l1a_v{b}", [128, 16 * 65], BF16)
                    whd = {nm: K.sb(stk, f"l1a_w{nm}{b}", [128, NCH * HD], BF16)
                           for nm in ("q", "k", "v", "g")}
                    bh = {nm: K.sb(stk, f"l1a_b{nm}{b}", [64, 1], F32)
                          for nm in ("q", "k", "v", "g")}
                    VRf = K.sb(stk, f"l1a_vrf{b}", [1, HD], F32)
                    VRb = K.sb(stk, f"l1a_vrb{b}", [1, HD], BF16)
                    mrow = K.sb(stk, f"l1a_mq{b}", [1, S], F32)
                    srow = K.sb(stk, f"l1a_sr{b}", [1, S], F32)
                    Rb0 = K.sb(stk, f"l1a_rb0{b}", [1, S], BF16)
                    Rb1 = K.sb(stk, f"l1a_rb1{b}", [1, S], BF16)
                    Rb2 = K.sb(stk, f"l1a_rb2{b}", [1, S], BF16)
                    dup = K.sb(stk, f"l1a_du{b}", [128, 128], BF16)
                    dlo = K.sb(stk, f"l1a_dl{b}", [128, 128], BF16)
                    Bh = K.sb(stk, f"l1a_bh{b}", [128, 16], F32)
                    ldc = K.sb(stk, f"l1a_ldc{b}", [128, 1], F32)
                    sm = K.sb(stk, f"l1a_sm{b}", [128, 8], F32)
                    smr = K.sb(stk, f"l1a_smr{b}", [1, 576], F32)
                    mb2 = K.sb(stk, f"l1a_mb2{b}", [64, 4], F32)
                    gbh = K.sb(stk, f"l1a_gbh{b}", [64, 2], F32)

                    def dve(fn, waits=()):
                        t = sV.bump(1)
                        P.on("vector", fn, waits=waits, inc=(sV, 1))
                        return t

                    def act(fn, waits=()):
                        t = sS.bump(1)
                        P.on("scalar", fn, waits=waits, inc=(sS, 1))
                        return t

                    def ten(fn, waits=()):
                        t = sT.bump(1)
                        P.on("tensor", fn, waits=waits, inc=(sT, 1))
                        return t

                    for m in range(8):
                        P.dma("sync", xlnb[:, m * S:(m + 1) * S],
                              g[xln_in][m, :, tok0:tok0 + S], waits=[(sT, sT.n)])
                    xln_tgt = sA.n
                    P.dma("sync", Kaug[64:68, :], g["c_kaugrows"],
                          waits=[(sT, sT.n)])
                    kaug_tgt = sA.n
                    t_ones = sPo.bump(1)
                    P.on("gpsimd", lambda e: e.memset(Vsb[:], 1.0),
                         waits=[(sT, sT.n)], inc=(sPo, 1))

                    def xmean(e):
                        r = None
                        for m in range(8):
                            r = e.tensor_reduce(out=sm[:, m:m + 1],
                                                in_=xlnb[:, m * S:(m + 1) * S],
                                                axis=AX.X, op=ALU.add)
                            r = e.drain()
                        return r
                    t_xm = dve(xmean, waits=[(sA, xln_tgt)])
                    def pdmm(e):
                        r = None
                        for m in range(8):
                            r = e.matmul(ps[0][0:1, 0:H], sm[:, m:m + 1],
                                         wdsl[:, m * H:(m + 1) * H],
                                         start=(m == 0), stop=(m == 7))
                        return r
                    t_pd1 = ten(pdmm, waits=[(sV, t_xm)])
                    ldr = smr[0:1, 0:H]
                    lds = smr[0:1, H:2 * H]
                    t_pd2 = dve(lambda e: (
                        e.tensor_scalar(out=ldr, in0=ps[0][0:1, 0:H],
                                        scalar1=1.0 / S, scalar2=None, op0=ALU.mult),
                        e.drain(),
                        e.tensor_tensor(out=ldr, in0=ldr, in1=bdrow[:], op=ALU.add),
                        e.drain(),
                    )[-1], waits=[(sT, t_pd1)])
                    t_pd3 = act(lambda e: (
                        e.activation(out=ldr, in_=ldr, func=AF.Sigmoid),
                        e.drain(),
                        e.activation(out=lds, in_=ldr, func=AF.Exp,
                                     scale=float(-8.0 * math.log(2.0)),
                                     bias=c["c_pool"][0:1, 2:3]),
                        e.drain(),
                    )[-1], waits=[(sV, t_pd2)])
                    t_pd4 = dve(lambda e: (
                        e.tensor_scalar(out=lds, in0=lds, scalar1=-1.0,
                                        scalar2=1.0, op0=ALU.mult, op1=ALU.add),
                        e.drain(),
                    )[-1], waits=[(sS, t_pd3)])
                    t_pd5 = act(lambda e: (
                        e.activation(out=ldr, in_=lds, func=AF.Ln),
                        e.drain(),
                    )[-1], waits=[(sV, t_pd4)])

                    for h in range(H):
                        ch_h = h // 2
                        pb_h = (h % 2) * 64
                        for nm in ("q", "k", "v", "g"):
                            wh_key = ("whd", b, nm)
                            P.dma("gpsimd",
                                  whd[nm][:].rearrange("p (k j) -> p k j", k=NCH),
                                  g[f"l1a_w{nm}h"][h]
                                  .rearrange("(k p) j -> p k j", p=128),
                                  waits=[war[wh_key]] if wh_key in war else ())
                            P.dma("sync", bh[nm][:],
                                  g[f"l1a_b{nm}h"][h].rearrange("(j o) -> j o", o=1),
                                  waits=[(sS, sS.n)])
                        P.dma("sync", VRf[:], g["l1a_bvh"][h].rearrange("(o j) -> o j", o=1),
                              waits=[(sPo, sPo.n)])
                        P.dma("sync", gbh[:, 0:1],
                              g["l1a_gamma_t"][ch_h, pb_h:pb_h + 64]
                              .rearrange("(j o) -> j o", o=1),
                              waits=[(sV, sV.n)])
                        P.dma("sync", gbh[:, 1:2],
                              g["l1a_beta_t"][ch_h, pb_h:pb_h + 64]
                              .rearrange("(j o) -> j o", o=1))
                        gb_tgt = P.sem("sync_dma").n
                        w_tgt = sW.n
                        b_tgt = sA.n
                        t_vr = sPo.bump(1)
                        P.on("gpsimd", lambda e: e.tensor_copy(out=VRb[:], in_=VRf[:]),
                             waits=[(sA, b_tgt)], inc=(sPo, 1))
                        t_ld1 = ten(lambda e: e.matmul(
                            ps[6][:, 0:1], c["c_ones1x128_f32"][:],
                            smr[0:1, 0:1] if False else ldr[0:1, h:h + 1],
                            start=True, stop=True),
                            waits=[(sS, t_pd5), (sV, sV.n)])
                        t_ld2 = dve(lambda e: (
                            e.tensor_copy(out=ldc[:], in_=ps[6][:, 0:1]),
                            e.drain(),
                            e.tensor_scalar(out=Bh[:], in0=c["c_iota2d"][:],
                                            scalar1=ldc[:], scalar2=None,
                                            op0=ALU.mult),
                            e.drain(),
                        )[-1], waits=[(sT, t_ld1)])
                        for tt in range(4):
                            t_mm = ten(lambda e, tt=tt: [
                                e.matmul(ps[4 + (tt % 2)][0:64, :],
                                         whd["g"][:, k * HD:(k + 1) * HD],
                                         xlnb[:, k * S + tt * 512:k * S + (tt + 1) * 512],
                                         start=(k == 0), stop=(k == NCH - 1))
                                for k in range(NCH)][-1],
                                waits=[(sW, w_tgt), (sA, xln_tgt),
                                       (sS, sS.n), (sV, sV.n)])
                            war[("whd", b, "g")] = (sT, t_mm)
                            act(lambda e, tt=tt: e.activation(
                                out=gsil[:, tt * 512:(tt + 1) * 512],
                                in_=ps[4 + (tt % 2)][0:64, :], func=AF.Silu,
                                bias=bh["g"][:]),
                                waits=[(sT, t_mm), (sA, b_tgt)])
                        for nm, aug in (("q", Qaug), ("k", Kaug)):
                            for tt in range(4):
                                t_qk = ten(lambda e, nm=nm, tt=tt: [
                                    e.matmul(ps[tt][0:64, :],
                                             whd[nm][:, k * HD:(k + 1) * HD],
                                             xlnb[:, k * S + tt * 512:
                                                  k * S + (tt + 1) * 512],
                                             start=(k == 0), stop=(k == NCH - 1))
                                    for k in range(NCH)][-1],
                                    waits=[(sW, w_tgt), (sA, xln_tgt),
                                           (sS, sS.n), (sV, sV.n)])
                                war[("whd", b, nm)] = (sT, t_qk)
                            t_ev = act(lambda e, nm=nm, aug=aug: [
                                e.activation(out=aug[0:64, tt * 512:(tt + 1) * 512],
                                             in_=ps[tt][0:64, :], func=AF.Identity,
                                             bias=bh[nm][:])
                                for tt in range(4)][-1],
                                waits=[(sT, sT.n), (sA, b_tgt)])
                            t_rt = ten(lambda e, aug=aug: [
                                e.matmul(ps[4 + tt][0:64, :], rc["c_rope_perm"][:],
                                         aug[0:64, tt * 512:(tt + 1) * 512],
                                         start=True, stop=True)
                                for tt in range(4)][-1],
                                waits=[(sS, t_ev), (sV, sV.n)])
                            dve(lambda e, aug=aug: (
                                e.tensor_tensor(out=Tt1[:], in0=aug[0:64, :],
                                                in1=rc["c_ropecos"][:], op=ALU.mult),
                            )[-1], waits=[(sS, t_ev)])
                            dve(lambda e, aug=aug: ([
                                e.tensor_tensor(
                                    out=Tt2[:, tt * 512:(tt + 1) * 512],
                                    in0=ps[4 + tt][0:64, :],
                                    in1=rc["c_ropesin"][:, tt * 512:(tt + 1) * 512],
                                    op=ALU.mult)
                                for tt in range(4)],
                                e.tensor_tensor(out=aug[0:64, :], in0=Tt1[:],
                                                in1=Tt2[:], op=ALU.add),
                            )[-1], waits=[(sT, t_rt)])
                        for tc in range(16):
                            bank = 4 + (tc % 4)
                            t_mm = ten(lambda e, tc=tc, bank=bank: (
                                [e.matmul(ps[bank][:, 0:HD],
                                          xlnb[:, k * S + tc * 128:k * S + tc * 128 + 128],
                                          whd["v"][:, k * HD:(k + 1) * HD],
                                          start=(k == 0), stop=False)
                                 for k in range(NCH)],
                                e.matmul(ps[bank][:, 0:HD], c["c_ones1x128"][:],
                                         VRb[:], start=False, stop=True),
                            )[-1], waits=[(sW, w_tgt), (sPo, t_vr), (sV, sV.n)])
                            war[("whd", b, "v")] = (sT, t_mm)
                            dve(lambda e, tc=tc, bank=bank: e.tensor_copy(
                                out=Vsb[:, tc * 65:tc * 65 + HD],
                                in_=ps[bank][:, 0:HD]),
                                waits=[(sT, t_mm)])
                        t_sq = act(lambda e: (
                            e.activation(out=Tt1[:], in_=Qaug[0:64, :],
                                         func=AF.Square),
                        )[-1], waits=[(sV, sV.n)])
                        t_cb = sPo.bump(1)
                        P.on("gpsimd", lambda e: e.tensor_copy(out=SqB[:], in_=Tt1[:]),
                             waits=[(sS, t_sq)], inc=(sPo, 1))
                        t_qn = ten(lambda e: [
                            e.matmul(ps[tt][0:1, :], c["c_onescol"][0:64, :],
                                     SqB[:, tt * 512:(tt + 1) * 512],
                                     start=True, stop=True)
                            for tt in range(4)][-1],
                            waits=[(sPo, t_cb)])
                        t_qs = act(lambda e: [
                            e.activation(out=mrow[0:1, tt * 512:(tt + 1) * 512],
                                         in_=ps[tt][0:1, :], func=AF.Sqrt)
                            for tt in range(4)][-1],
                            waits=[(sT, t_qn)])
                        t_sq2 = act(lambda e: (
                            e.activation(out=Tt1[:], in_=Kaug[0:64, :],
                                         func=AF.Square),
                        )[-1])
                        t_cb2 = sPo.bump(1)
                        P.on("gpsimd", lambda e: e.tensor_copy(out=SqB[:], in_=Tt1[:]),
                             waits=[(sS, t_sq2)], inc=(sPo, 1))
                        t_kn = ten(lambda e: [
                            e.matmul(ps[4 + tt][0:1, :], c["c_onescol"][0:64, :],
                                     SqB[:, tt * 512:(tt + 1) * 512],
                                     start=True, stop=True)
                            for tt in range(4)][-1],
                            waits=[(sPo, t_cb2)])
                        t_kmax = dve(lambda e: ([
                            e.tensor_copy(out=srow[0:1, tt * 512:(tt + 1) * 512],
                                          in_=ps[4 + tt][0:1, :])
                            for tt in range(4)],
                            e.drain(),
                            e.tensor_reduce(out=smr[0:1, 32:33], in_=srow[:],
                                            axis=AX.X, op=ALU.max),
                            e.drain(),
                        )[-1], waits=[(sT, t_kn)])
                        t_ks = act(lambda e: (
                            e.activation(out=smr[0:1, 32:33], in_=smr[0:1, 32:33],
                                         func=AF.Sqrt),
                            e.drain(),
                        )[-1], waits=[(sV, t_kmax)])
                        t_rows = dve(lambda e: (
                            e.tensor_scalar(out=mrow[:], in0=mrow[:],
                                            scalar1=smr[0:1, 32:33], scalar2=-1.0,
                                            op0=ALU.mult, op1=ALU.mult),
                            e.tensor_copy(out=Rb0[:], in_=mrow[:]),
                            e.tensor_scalar(out=srow[:], in0=rc["c_iota_row"][:],
                                            scalar1=ldc[0:1, 0:1], scalar2=-8.0,
                                            op0=ALU.mult, op1=ALU.mult),
                            e.tensor_copy(out=Rb1[:], in_=srow[:]),
                            e.drain(),
                            e.tensor_tensor(out=mrow[:], in0=srow[:], in1=Rb1[:],
                                            op=ALU.subtract),
                            e.tensor_copy(out=Rb2[:], in_=mrow[:]),
                            e.drain(),
                        )[-1], waits=[(sS, t_ks), (sV, t_ld2)])
                        t_aug = None
                        for jj in range(4):
                            t_stk = ten(lambda e, jj=jj: (
                                e.matmul(ps[7][0:4, :], rc["c_eye4"][0:1, 0:4],
                                         Rb0[0:1, jj * 512:(jj + 1) * 512],
                                         start=True, stop=False),
                                e.matmul(ps[7][0:4, :], rc["c_eye4"][0:1, 4:8],
                                         Rb1[0:1, jj * 512:(jj + 1) * 512],
                                         start=False, stop=False),
                                e.matmul(ps[7][0:4, :], rc["c_eye4"][0:1, 8:12],
                                         Rb2[0:1, jj * 512:(jj + 1) * 512],
                                         start=False, stop=True),
                            )[-1], waits=[(sV, t_aug if t_aug else t_rows)])
                            t_aug = dve(lambda e, jj=jj: e.tensor_copy(
                                out=Qaug[64:68, jj * 512:(jj + 1) * 512],
                                in_=ps[7][0:4, :]), waits=[(sT, t_stk)])
                        for j in range(4):
                            for cc in range(16):
                                bank = (0, 1, 2, 6, 7)[cc % 5]
                                if cc < 4 * j:
                                    t_mm = ten(lambda e, cc=cc, j=j, bank=bank:
                                        e.matmul(ps[bank][:],
                                                 Kaug[0:65, cc * 128:(cc + 1) * 128],
                                                 Qaug[0:65, j * 512:(j + 1) * 512],
                                                 start=True, stop=True),
                                        waits=[(sV, t_aug), (sS, sS.n)])
                                    act(lambda e, cc=cc, bank=bank: e.activation(
                                        out=ATj[:, cc * 512:(cc + 1) * 512],
                                        in_=ps[bank][:], func=AF.Exp, scale=0.125),
                                        waits=[(sT, t_mm)])
                                elif cc >= 4 * (j + 1):
                                    t_mm = ten(lambda e, cc=cc, j=j, bank=bank:
                                        e.matmul(ps[bank][:],
                                                 Kaug[0:68, cc * 128:(cc + 1) * 128],
                                                 Qaug[0:68, j * 512:(j + 1) * 512],
                                                 start=True, stop=True),
                                        waits=[(sV, t_aug), (sS, sS.n)])
                                    act(lambda e, cc=cc, bank=bank: e.activation(
                                        out=ATj[:, cc * 512:(cc + 1) * 512],
                                        in_=ps[bank][:], func=AF.Exp, scale=0.125,
                                        bias=Bh[:, cc:cc + 1]),
                                        waits=[(sT, t_mm)])
                                else:
                                    for ssub in range(4):
                                        jq = 4 * j + ssub
                                        q0 = j * 512 + ssub * 128
                                        if jq < cc:
                                            t_mm = ten(lambda e, cc=cc, q0=q0, bank=bank:
                                                e.matmul(ps[bank][:, 0:128],
                                                         Kaug[0:68, cc * 128:(cc + 1) * 128],
                                                         Qaug[0:68, q0:q0 + 128],
                                                         start=True, stop=True),
                                                waits=[(sV, t_aug), (sS, sS.n)])
                                            act(lambda e, cc=cc, q0=q0, bank=bank, j=j:
                                                e.activation(
                                                    out=ATj[:, cc * 512 + (q0 - j * 512):
                                                            cc * 512 + (q0 - j * 512) + 128],
                                                    in_=ps[bank][:, 0:128],
                                                    func=AF.Exp, scale=0.125,
                                                    bias=Bh[:, cc:cc + 1]),
                                                waits=[(sT, t_mm)])
                                        elif jq > cc:
                                            t_mm = ten(lambda e, cc=cc, q0=q0, bank=bank:
                                                e.matmul(ps[bank][:, 0:128],
                                                         Kaug[0:65, cc * 128:(cc + 1) * 128],
                                                         Qaug[0:65, q0:q0 + 128],
                                                         start=True, stop=True),
                                                waits=[(sV, t_aug), (sS, sS.n)])
                                            act(lambda e, cc=cc, q0=q0, bank=bank, j=j:
                                                e.activation(
                                                    out=ATj[:, cc * 512 + (q0 - j * 512):
                                                            cc * 512 + (q0 - j * 512) + 128],
                                                    in_=ps[bank][:, 0:128],
                                                    func=AF.Exp, scale=0.125),
                                                waits=[(sT, t_mm)])
                                        else:
                                            t_mm = ten(lambda e, cc=cc, q0=q0:
                                                (e.matmul(ps[3][:, 0:128],
                                                          Kaug[0:68, cc * 128:(cc + 1) * 128],
                                                          Qaug[0:68, q0:q0 + 128],
                                                          start=True, stop=True),
                                                 e.matmul(ps[3][:, 128:256],
                                                          Kaug[0:65, cc * 128:(cc + 1) * 128],
                                                          Qaug[0:65, q0:q0 + 128],
                                                          start=True, stop=True),
                                                 )[-1],
                                                waits=[(sV, sV.n), (sS, sS.n)])
                                            t_ex = act(lambda e, cc=cc: (
                                                e.activation(out=dup[:],
                                                             in_=ps[3][:, 0:128],
                                                             func=AF.Exp, scale=0.125,
                                                             bias=Bh[:, cc:cc + 1]),
                                                e.activation(out=dlo[:],
                                                             in_=ps[3][:, 128:256],
                                                             func=AF.Exp, scale=0.125),
                                            )[-1], waits=[(sT, t_mm)])
                                            dve(lambda e, cc=cc, q0=q0, j=j:
                                                e.select(
                                                    out=ATj[:, cc * 512 + (q0 - j * 512):
                                                            cc * 512 + (q0 - j * 512) + 128],
                                                    mask=c["c_diagmask"][:],
                                                    on_true=dup[:], on_false=dlo[:]),
                                                waits=[(sS, t_ex)])
                            t_av = ten(lambda e, j=j: [
                                e.matmul(ps[4][0:65, :], Vsb[:, tc * 65:(tc + 1) * 65],
                                         ATj[:, tc * 512:(tc + 1) * 512],
                                         start=(tc == 0), stop=(tc == 15))
                                for tc in range(16)][-1],
                                waits=[(sS, sS.n), (sV, sV.n)])
                            nr = smr[0:1, 64:576]
                            t_nr = act(lambda e: (
                                e.activation(out=nr, in_=ps[4][64:65, :], func=AF.Ln),
                                e.activation(out=nr, in_=nr, func=AF.Exp, scale=-1.0),
                            )[-1], waits=[(sT, t_av)])
                            t_nb = ten(lambda e: e.matmul(
                                ps[5][0:64, :], c["c_ones1x128_f32"][0:1, 0:64], nr,
                                start=True, stop=True),
                                waits=[(sS, t_nr)])
                            dve(lambda e, j=j: (
                                e.tensor_copy(out=Tt1[:, 0:512], in_=ps[4][0:64, :]),
                                e.tensor_tensor(out=ogh[:, j * 512:(j + 1) * 512],
                                                in0=Tt1[:, 0:512], in1=ps[5][0:64, :],
                                                op=ALU.mult),
                            )[-1], waits=[(sT, t_nb)])
                        t_st = dve(lambda e: (
                            e.tensor_reduce(out=sm[0:64, 0:1], in_=ogh[:],
                                            axis=AX.X, op=ALU.add),
                            e.drain(),
                        )[-1])
                        t_st2 = act(lambda e: (
                            e.activation(out=Tt1[:], in_=ogh[:], func=AF.Square,
                                         accum_out=sm[0:64, 1:2]),
                            e.drain(),
                        )[-1], waits=[(sV, t_st)])
                        t_st3 = ten(lambda e: e.matmul(
                            ps[6][0:1, 0:2], c["c_onescol_f32"][0:64, :],
                            sm[0:64, 0:2], start=True, stop=True),
                            waits=[(sS, t_st2)])
                        t_st4 = dve(lambda e: (
                            e.tensor_scalar(out=smr[0:1, 34:36], in0=ps[6][0:1, 0:2],
                                            scalar1=1.0 / (S * HD), scalar2=None,
                                            op0=ALU.mult),
                            e.drain(),
                            e.tensor_tensor(out=smr[0:1, 36:37], in0=smr[0:1, 34:35],
                                            in1=smr[0:1, 34:35], op=ALU.mult),
                            e.drain(),
                            e.tensor_tensor(out=smr[0:1, 35:36], in0=smr[0:1, 35:36],
                                            in1=smr[0:1, 36:37], op=ALU.subtract),
                            e.drain(),
                            e.tensor_scalar(out=smr[0:1, 35:36], in0=smr[0:1, 35:36],
                                            scalar1=1e-5, scalar2=None, op0=ALU.add),
                            e.drain(),
                        )[-1], waits=[(sT, t_st3)])
                        t_st5 = act(lambda e: (
                            e.activation(out=smr[0:1, 35:36], in_=smr[0:1, 35:36],
                                         func=AF.Ln),
                            e.drain(),
                            e.activation(out=smr[0:1, 35:36], in_=smr[0:1, 35:36],
                                         func=AF.Exp, scale=-0.5),
                            e.drain(),
                        )[-1], waits=[(sV, t_st4)])
                        t_bc2 = ten(lambda e: e.matmul(
                            ps[7][0:64, 0:2], c["c_ones1x128_f32"][0:1, 0:64],
                            smr[0:1, 34:36], start=True, stop=True),
                            waits=[(sS, t_st5)])
                        t_ab2 = dve(lambda e, ch_h=ch_h, pb_h=pb_h: (
                            e.tensor_copy(out=mb2[:, 0:2], in_=ps[7][0:64, 0:2]),
                            e.drain(),
                            e.tensor_tensor(out=mb2[:, 2:3], in0=mb2[:, 1:2],
                                            in1=gbh[:, 0:1], op=ALU.mult),
                            e.drain(),
                            e.tensor_tensor(out=mb2[:, 3:4], in0=mb2[:, 0:1],
                                            in1=mb2[:, 2:3], op=ALU.mult),
                            e.drain(),
                            e.scalar_tensor_tensor(out=mb2[:, 3:4], in0=mb2[:, 3:4],
                                                   scalar=-1.0,
                                                   in1=gbh[:, 1:2],
                                                   op0=ALU.mult, op1=ALU.add),
                            e.drain(),
                            e.tensor_scalar(out=Tt1[:], in0=ogh[:],
                                            scalar1=mb2[:, 2:3], scalar2=mb2[:, 3:4],
                                            op0=ALU.mult, op1=ALU.add),
                            e.tensor_tensor(
                                out=zb[pb_h:pb_h + 64, ch_h * S:(ch_h + 1) * S],
                                in0=Tt1[:], in1=gsil[:], op=ALU.mult),
                        )[-1], waits=[(sT, t_bc2), (P.sem("sync_dma"), gb_tgt)])
                P.barrier()
                with contextlib.ExitStack() as stk:
                    ps = K.ps8(stk, f"l1wo{b}")
                    wsl = [K.sb(stk, f"l1a_wo{i}{b}", [128, NCH * 128], BF16)
                           for i in range(8)]
                    Bout = [K.sb(stk, f"l1a_bo2{i}{b}", [128, 8 * 512], F32)
                            for i in range(2)]
                    bd = Boundary(K, stk, f"l1a_bd{b}", ps[4:8])
                    for mo in range(8):
                        P.dma("gpsimd",
                              wsl[mo][:].rearrange("p (k j) -> p k j", k=NCH),
                              g["l1a_wot"][:, mo * 128:(mo + 1) * 128]
                              .rearrange("(k p) j -> p k j", p=128))
                    wo_tgt = sW.n
                    for tt in range(4):
                        Bo = Bout[tt % 2]
                        t_evs = []
                        for wave in range(2):
                            K.td.flush_one()
                            K.td.flush_one()
                            def mmo(e, tt=tt, wave=wave):
                                r = None
                                for mi in range(4):
                                    mo = wave * 4 + mi
                                    for k in range(NCH):
                                        r = e.matmul(
                                            ps[mi][:],
                                            wsl[mo][:, k * 128:(k + 1) * 128],
                                            zb[:, k * S + tt * 512:
                                               k * S + (tt + 1) * 512],
                                            start=(k == 0), stop=(k == NCH - 1))
                                return r
                            bw = war.get(("wobank", b))
                            t_mm = sT.bump(1)
                            P.on("tensor", mmo,
                                 waits=[(sW, wo_tgt)] + ([bw] if bw else []),
                                 inc=(sT, 1))
                            t_ev = sV.bump(1)
                            def evo(e, wave=wave, Bo=Bo):
                                r = None
                                for mi in range(4):
                                    m = wave * 4 + mi
                                    r = e.tensor_scalar(
                                        out=Bo[:, m * 512:(m + 1) * 512],
                                        in0=ps[mi][:], scalar1=vec["bo"][:, m:m + 1],
                                        scalar2=None, op0=ALU.add)
                                return r
                            bo_w = war.get(("wobout", b, tt % 2))
                            P.on("vector", evo,
                                 waits=[(sT, t_mm)] + ([bo_w] if bo_w else []),
                                 inc=(sV, 1))
                            war[("wobank", b)] = (sV, t_ev)
                            t_evs.append(t_ev)
                        Bt = [Bo[:, m * 512:(m + 1) * 512] for m in range(NCH)]
                        rg = bd.run_tt(b, tt, Bt, (sV, t_evs[-1]), resid_in,
                                       res_out, ln_out, affine=None, final=False)
                        war[("wobout", b, tt % 2)] = rg
                    K.td.flush()
                    P.barrier()
    P.barrier()



def build_program(phases, stage_ext=None):
    nc = bass.Bass("TRN2", target_bir_lowering=False)
    stack = contextlib.ExitStack()
    P = Prog(nc, stack)
    K = Kern(nc, P, stack)
    K.td = TDefer(P)
    declare_io(K, stage_ext)
    with contextlib.ExitStack() as cstk:
        load_consts(K, cstk)
        for ph in phases:
            if ph == "ln0":
                phase_ln0(K)
            elif ph == "l0a":
                phase_l0attn(K, "x0", "xln", "xres", "xln")
            elif ph == "l0f":
                phase_ffn(K, "l0f", "xres", "xln", "xres2", "xln2")
            elif ph == "l1a":
                phase_l1attn(K, "xres2", "xln2", "xres", "xln")
            elif ph == "l1f":
                phase_ffn(K, "l1f", "xres", "xln", None, None, final=True)
            else:
                raise ValueError(ph)
        P.finalize()
    stack.close()
    return nc


ALL_PHASES = ["ln0", "l0a", "l0f", "l1a", "l1f"]


def kernel(**inputs):
    maps = prep_host(inputs)
    nc = build_program(ALL_PHASES)
    res = run_bass_kernel_spmd(nc, maps, core_ids=list(range(8)))
    out = np.empty((16, S, D), np.float32)
    for cid in range(8):
        o = np.asarray(res.results[cid]["out"], np.float32)
        out[2 * cid:2 * cid + 2] = o.reshape(D, NB, S).transpose(1, 2, 0)
    return out


# version 04
# speedup vs baseline: 1.0099x; 1.0021x over previous
import math
import contextlib
import numpy as np
import ml_dtypes

import concourse.bass as bass
import concourse.mybir as mybir
from concourse.bass_utils import run_bass_kernel_spmd

F32 = mybir.dt.float32
BF16 = mybir.dt.bfloat16
U8 = mybir.dt.uint8
AF = mybir.ActivationFunctionType
ALU = mybir.AluOpType
AX = mybir.AxisListType
NPBF16 = ml_dtypes.bfloat16

D = 1024
S = 2048
NB = 2
TOK = NB * S
NCH = D // 128
F = 4096
FCH = F // 128
H = 16
HD = 64
NT = S // 512
MAGIC = 12582912.0
INV2PI = float(1.0 / (2 * np.pi))
TWOPI = float(2 * np.pi)
PI = float(np.pi)
SIGC = 1.5957691216057308

ENGINES = ("sync", "gpsimd", "vector", "scalar", "tensor")
FFN_DEBUG = 0
FFN_SERIAL = 0


class Sem:
    __slots__ = ("h", "n", "name")
    def __init__(self, h, name):
        self.h = h; self.n = 0; self.name = name
    def bump(self, k):
        self.n += k
        return self.n


class Prog:
    def __init__(self, nc, stack):
        self.nc = nc
        self.stack = stack
        self.q = {e: [] for e in ENGINES}
        self._sems = {}

    def sem(self, name):
        if name not in self._sems:
            h = self.stack.enter_context(self.nc.semaphore(name))
            self._sems[name] = Sem(h, name)
        return self._sems[name]

    def on(self, eng, fn, waits=(), inc=None):
        def thunk(e, fn=fn, waits=tuple(waits), inc=inc):
            for s, v in waits:
                e.wait_ge(s.h, v)
            r = fn(e)
            if inc is not None:
                s, a = inc
                r.then_inc(s.h, a)
        self.q[eng].append(thunk)

    def dma(self, eng, out, in_, waits=(), slot=None):
        nm = eng + "_dma" if slot is None else f"{eng}_dma@{slot}"
        s = self.sem(nm)
        tgt = s.bump(16)
        self.on(eng, lambda e: e.dma_start(out=out, in_=in_), waits, (s, 16))
        return s, tgt

    def barrier(self):
        b = self.sem("bar")
        dma_waits = [(s, s.n) for nm, s in list(self._sems.items())
                     if "_dma" in nm]
        tgt = b.n + len(ENGINES)
        for eng in ENGINES:
            def thunk(e, dma_waits=tuple(dma_waits), b=b, tgt=tgt):
                for s, v in dma_waits:
                    e.wait_ge(s.h, v)
                e.sem_inc(b.h, 1)
                e.wait_ge(b.h, tgt)
            self.q[eng].append(thunk)
        b.n = tgt

    def finalize(self):
        with self.nc.Block() as blk:
            @blk.sync
            def _(e):
                for t in self.q["sync"]:
                    t(e)
            @blk.gpsimd
            def _(e):
                for t in self.q["gpsimd"]:
                    t(e)
            @blk.vector
            def _(e):
                for t in self.q["vector"]:
                    t(e)
            @blk.scalar
            def _(e):
                for t in self.q["scalar"]:
                    t(e)
            @blk.tensor
            def _(e):
                for t in self.q["tensor"]:
                    t(e)



def _pp(vec):
    return np.ascontiguousarray(np.asarray(vec, np.float32).reshape(-1, 128).T)


def prep_host(inputs):
    f32 = lambda a: np.ascontiguousarray(np.asarray(a, np.float32))
    sh = {}
    sh["l0a_wq"] = f32(inputs["L0_attn_Wq"])
    sh["l0a_wk"] = f32(inputs["L0_attn_Wk"])
    sh["l0a_wv"] = f32(inputs["L0_attn_Wv"])
    sh["l0a_wg"] = f32(inputs["L0_attn_Wg"])
    sh["l0a_wa"] = f32(inputs["L0_attn_Wa"])
    sh["l0a_wo"] = f32(inputs["L0_attn_Wo"])
    w1 = f32(inputs["L0_attn_W1"])
    w2 = f32(inputs["L0_attn_W2"])
    sh["l0a_w1m"] = f32(w1.reshape(D, FCH, 128).transpose(1, 0, 2))
    sh["l0a_w2m"] = f32(w2.reshape(F, NCH, 128).transpose(1, 0, 2))
    for nm, src in (("gns", "gn_scale"), ("gnb", "gn_bias"), ("ln1s", "ln1_scale"),
                    ("ln1b", "ln1_bias"), ("ln2s", "ln2_scale"), ("ln2b", "ln2_bias"),
                    ("b2", "b2")):
        sh[f"l0a_{nm}"] = _pp(inputs[f"L0_attn_{src}"])
    sh["l0a_b1"] = _pp(inputs["L0_attn_b1"])
    for pre, src in (("l0f", "L0_ffn"), ("l1f", "L1_ffn")):
        w1t = f32(np.asarray(inputs[f"{src}_w1_w"]).T)
        w2t = f32(np.asarray(inputs[f"{src}_w2_w"]).T)
        h1t = f32(np.asarray(inputs[f"{src}_h1_w"]).T)
        h2t = f32(np.asarray(inputs[f"{src}_h2_w"]).T)
        w3t = f32(np.asarray(inputs[f"{src}_w3_w"]).T)
        sh[f"{pre}_w1m"] = f32(w1t.reshape(D, FCH, 128).transpose(1, 0, 2))
        sh[f"{pre}_w2m"] = f32(w2t.reshape(D, FCH, 128).transpose(1, 0, 2))
        sh[f"{pre}_h1m"] = f32(h1t.reshape(F, FCH, 128).transpose(1, 0, 2))
        sh[f"{pre}_h2m"] = f32(h2t.reshape(F, FCH, 128).transpose(1, 0, 2))
        sh[f"{pre}_w3m"] = f32(w3t.reshape(F, NCH, 128).transpose(1, 0, 2))
        sh[f"{pre}_s1"] = _pp(inputs[f"{src}_w1_s"])
        sh[f"{pre}_s2"] = _pp(inputs[f"{src}_w2_s"])
        sh[f"{pre}_sh1"] = _pp(inputs[f"{src}_h1_s"])
        sh[f"{pre}_sh2"] = _pp(inputs[f"{src}_h2_s"])
        sh[f"{pre}_s3"] = _pp(inputs[f"{src}_w3_s"])
    for nm in ("wq", "wk", "wv", "wg"):
        wt = f32(np.asarray(inputs[f"L1_attn_{nm}"]).T)
        sh[f"l1a_{nm}h"] = f32(wt.reshape(D, H, HD).transpose(1, 0, 2))
        b = np.asarray(inputs[f"L1_attn_b{nm[1]}"], np.float32)
        sh[f"l1a_b{nm[1]}h"] = f32(b.reshape(H, HD))
    sh["l1a_wot"] = f32(np.asarray(inputs["L1_attn_wo"]).T)
    sh["l1a_wdt"] = f32(np.asarray(inputs["L1_attn_wd"]).T)
    sh["l1a_bo"] = _pp(inputs["L1_attn_bo"])
    sh["l1a_bd_row"] = f32(np.asarray(inputs["L1_attn_bd"]).reshape(1, H))
    sh["l1a_gamma"] = _pp(inputs["L1_attn_gamma"])
    sh["l1a_beta"] = _pp(inputs["L1_attn_beta"])
    sh["l1a_gamma_t"] = f32(np.asarray(inputs["L1_attn_gamma"]).reshape(NCH, 128))
    sh["l1a_beta_t"] = f32(np.asarray(inputs["L1_attn_beta"]).reshape(NCH, 128))
    bfa = lambda a: np.ascontiguousarray(np.asarray(a, np.float32).astype(NPBF16))
    sh["c_id128"] = bfa(np.eye(128))
    sh["c_ones1x128"] = bfa(np.ones((1, 128)))
    sh["c_onescol"] = bfa(np.ones((128, 1)))
    sh["c_ones1x128_f32"] = np.ones((1, 128), np.float32)
    sh["c_iota_row"] = np.arange(S, dtype=np.float32).reshape(1, S)
    sh["c_iota2d"] = np.ascontiguousarray(
        np.arange(128, dtype=np.float32)[:, None]
        + 128.0 * np.arange(16, dtype=np.float32)[None, :])
    inv_freq = (1.0 / 10000.0 ** (np.arange(0, HD, 2, dtype=np.float32) / HD)
                ).astype(np.float32)
    freqs = np.arange(S, dtype=np.float32)[:, None] * inv_freq[None, :]
    emb = np.concatenate([freqs, freqs], axis=-1)
    sh["c_ropecos"] = f32(np.cos(emb).T)
    sh["c_ropesin"] = f32(np.sin(emb).T)
    gsel = np.zeros((128, 4), np.float32)
    for gg in range(4):
        gsel[gg * 32:(gg + 1) * 32, gg] = 1.0
    sh["c_gsel"] = f32(gsel)
    sh["c_gexp"] = f32(gsel.T.copy())
    sh["c_diagmask"] = (np.arange(128)[:, None] > np.arange(128)[None, :]
                        ).astype(np.uint8)
    pm = np.zeros((HD, HD), np.float32)
    for i in range(HD // 2):
        pm[2 * i + 1, 2 * i] = -1.0
        pm[2 * i, 2 * i + 1] = 1.0
    sh["c_rope_perm"] = bfa(pm)
    sh["c_onescol_f32"] = np.ones((128, 1), np.float32)
    kr = np.zeros((4, S), np.float32)
    kr[0:3, :] = 1.0
    sh["c_kaugrows"] = bfa(kr)
    e3 = np.zeros((1, 12), np.float32)
    e3[0, 0] = 1.0; e3[0, 5] = 1.0; e3[0, 10] = 1.0
    sh["c_eye4"] = bfa(e3)
    cp = np.zeros((128, 8), np.float32)
    cp[:, 0] = np.pi / 2
    cp[:, 1] = 0.5
    cp[:, 2] = -5.0 * math.log(2.0)
    sh["c_pool"] = f32(cp)

    x = np.asarray(inputs["L0_attn_x"], np.float32)
    maps = []
    for cid in range(8):
        m = dict(sh)
        xc = x[2 * cid:2 * cid + 2]
        xt = np.ascontiguousarray(xc.transpose(2, 0, 1).reshape(D, TOK))
        m["x0"] = np.ascontiguousarray(xt.reshape(NCH, 128, TOK))
        maps.append(m)
    return maps



class Kern:
    def __init__(self, nc, P, stack):
        self.nc = nc
        self.P = P
        self.stack = stack
        self.g = {}
        self.c = {}

    def sb(self, stk, name, shape, dt=F32):
        return stk.enter_context(self.nc.sbuf_tensor("sb_" + name, shape, dt))

    def ps8(self, stk, tag):
        return [stk.enter_context(self.nc.psum_tensor(f"ps_{tag}_{i}", [128, 512], F32))
                for i in range(8)]


DRAM_SPECS = {
    "x0": ([NCH, 128, TOK], F32),
    "out": ([NCH, 128, TOK], F32),
    "xres": ([NCH, 128, TOK], F32),
    "xres2": ([NCH, 128, TOK], F32),
    "xln": ([NCH, 128, TOK], BF16),
    "xln2": ([NCH, 128, TOK], BF16),
    "hscr": ([FCH, 128, S], BF16),
    "hscr2": ([FCH, 128, S], BF16),
}

WEIGHT_SPECS = {
    "l0a_wq": [D, D], "l0a_wk": [D, D], "l0a_wv": [D, D], "l0a_wg": [D, D],
    "l0a_wa": [D, 2 * D], "l0a_wo": [D, D],
    "l0a_w1m": [FCH, D, 128], "l0a_w2m": [NCH, F, 128],
    "l0a_gns": [128, NCH], "l0a_gnb": [128, NCH],
    "l0a_ln1s": [128, NCH], "l0a_ln1b": [128, NCH],
    "l0a_ln2s": [128, NCH], "l0a_ln2b": [128, NCH],
    "l0a_b1": [128, FCH], "l0a_b2": [128, NCH],
    "l1a_wot": [D, D], "l1a_wdt": [D, H],
    "l1a_bo": [128, NCH], "l1a_bd_row": [1, H],
    "l1a_gamma": [128, NCH], "l1a_beta": [128, NCH],
    "l1a_gamma_t": [NCH, 128], "l1a_beta_t": [NCH, 128],
    "c_ones1x128_f32": [1, 128], "c_onescol_f32": [128, 1],
    "c_iota_row": [1, S], "c_iota2d": [128, 16],
    "c_ropecos": [HD, S], "c_ropesin": [HD, S],
    "c_gsel": [128, 4], "c_gexp": [4, 128], "c_pool": [128, 8],
}
for _pre in ("l0f", "l1f"):
    WEIGHT_SPECS[f"{_pre}_w1m"] = [FCH, D, 128]
    WEIGHT_SPECS[f"{_pre}_w2m"] = [FCH, D, 128]
    WEIGHT_SPECS[f"{_pre}_h1m"] = [FCH, F, 128]
    WEIGHT_SPECS[f"{_pre}_h2m"] = [FCH, F, 128]
    WEIGHT_SPECS[f"{_pre}_w3m"] = [NCH, F, 128]
    for _s in ("s1", "s2", "sh1", "sh2"):
        WEIGHT_SPECS[f"{_pre}_{_s}"] = [128, FCH]
    WEIGHT_SPECS[f"{_pre}_s3"] = [128, NCH]
for _nm in ("wq", "wk", "wv", "wg"):
    WEIGHT_SPECS[f"l1a_{_nm}h"] = [H, D, HD]
    WEIGHT_SPECS[f"l1a_b{_nm[1]}h"] = [H, HD]
BF16_CONSTS = {"c_id128": [128, 128], "c_ones1x128": [1, 128],
               "c_onescol": [128, 1], "c_rope_perm": [HD, HD],
               "c_kaugrows": [4, S], "c_eye4": [1, 12]}


def declare_io(K, stage_ext=None):
    nc = K.nc
    ext = stage_ext or {}
    for name, (shape, dt) in DRAM_SPECS.items():
        kind = ("Internal" if name not in ("x0", "out") else
                ("ExternalInput" if name == "x0" else "ExternalOutput"))
        if name in ext:
            kind = "ExternalInput" if ext[name] == "in" else "ExternalOutput"
        K.g[name] = nc.dram_tensor(name, shape, dt, kind=kind).ap()
    for name, shape in WEIGHT_SPECS.items():
        K.g[name] = nc.dram_tensor(name, shape, F32, kind="ExternalInput").ap()
    for name, shape in BF16_CONSTS.items():
        K.g[name] = nc.dram_tensor(name, shape, BF16, kind="ExternalInput").ap()
    K.g["c_diagmask"] = nc.dram_tensor("c_diagmask", [128, 128], U8,
                                       kind="ExternalInput").ap()


def load_consts(K, stk):
    P = K.P
    c = K.c
    for nm, shape, dt in (
            ("c_id128", [128, 128], BF16), ("c_ones1x128", [1, 128], BF16),
            ("c_onescol", [128, 1], BF16), ("c_ones1x128_f32", [1, 128], F32),
            ("c_onescol_f32", [128, 1], F32), ("c_iota2d", [128, 16], F32),
            ("c_gsel", [128, 4], F32), ("c_gexp", [4, 128], F32),
            ("c_pool", [128, 8], F32), ("c_diagmask", [128, 128], U8)):
        c[nm] = K.sb(stk, "sb" + nm, shape, dt)
        P.dma("sync", c[nm][:], K.g[nm])
    P.barrier()



class TDefer:

    def __init__(self, P):
        self.P = P
        self.sem = P.sem("t3")
        self.pend = []

    IMMEDIATE = False

    def ten(self, fn, waits):
        t = self.sem.bump(1)
        def emit(fn=fn, waits=tuple(waits)):
            self.P.on("tensor", fn, waits=waits, inc=(self.sem, 1))
        if TDefer.IMMEDIATE:
            emit()
        else:
            self.pend.append(emit)
        return (self.sem, t)

    def flush_one(self):
        if self.pend:
            self.pend.pop(0)()

    def flush(self):
        while self.pend:
            self.pend.pop(0)()


class Boundary:

    def __init__(self, K, stk, tag, banks):
        self.K = K
        self.tag = tag
        self.pb = banks
        self.Bbf = K.sb(stk, f"{tag}_bbf", [128, 8 * 512], BF16)
        self.Bsq = K.sb(stk, f"{tag}_bsq", [128, 8 * 512], BF16)
        self.T1 = K.sb(stk, f"{tag}_t1", [128, 8 * 512], F32)
        self.Xr = K.sb(stk, f"{tag}_xr", [128, 8 * 512], F32)
        self.rows = K.sb(stk, f"{tag}_rows", [1, 512 * 4], F32)
        self.war_bbf = None

    def row(self, i):
        return self.rows[0:1, 512 * i:512 * (i + 1)]

    def ln_rows(self, nfeat, eps, after_te):
        K, P = self.K, self.K.P
        sV, sS = P.sem("v"), P.sem("s")
        pSa, pSb = self.pb[0][0:1, :], self.pb[1][0:1, :]
        mu, r, nmr, va = self.row(0), self.row(1), self.row(2), self.row(3)
        inv = 1.0 / nfeat
        tV = sV.bump(1)
        P.on("vector", lambda e: (
            e.tensor_scalar(out=mu, in0=pSa, scalar1=inv, scalar2=None, op0=ALU.mult),
            e.tensor_scalar(out=va, in0=pSb, scalar1=inv, scalar2=None, op0=ALU.mult),
            e.tensor_tensor(out=nmr, in0=mu, in1=mu, op=ALU.mult),
            e.tensor_tensor(out=va, in0=va, in1=nmr, op=ALU.subtract),
            e.tensor_scalar(out=va, in0=va, scalar1=eps, scalar2=None, op0=ALU.add),
        )[-1], waits=[after_te], inc=(sV, 1))
        tS = sS.bump(1)
        P.on("scalar", lambda e: (
            e.activation(out=r, in_=va, func=AF.Ln),
            e.activation(out=r, in_=r, func=AF.Exp, scale=-0.5),
        )[-1], waits=[(sV, tV)], inc=(sS, 1))
        tV2 = sV.bump(1)
        P.on("vector", lambda e: (
            e.tensor_tensor(out=nmr, in0=mu, in1=r, op=ALU.mult),
            e.tensor_scalar(out=nmr, in0=nmr, scalar1=-1.0, scalar2=None,
                            op0=ALU.mult),
        )[-1], waits=[(sS, tS)], inc=(sV, 1))
        return tV2

    def stats(self, src_tiles, after):
        K, P = self.K, self.K.P
        sP, sS = P.sem("p"), P.sem("s")
        guard = [after]
        if self.war_bbf is not None:
            guard.append(self.war_bbf)
        n = len(src_tiles)
        tP = sP.bump(1)
        def cast_all(e):
            r = None
            for m in range(n):
                r = e.tensor_copy(out=self.Bbf[:, 512 * m:512 * (m + 1)],
                                  in_=src_tiles[m])
            return r
        P.on("gpsimd", cast_all, waits=guard, inc=(sP, 1))
        tS = sS.bump(1)
        def sq_all(e):
            r = None
            for m in range(n):
                r = e.activation(out=self.Bsq[:, 512 * m:512 * (m + 1)],
                                 in_=src_tiles[m], func=AF.Square)
            return r
        P.on("scalar", sq_all, waits=guard, inc=(sS, 1))
        oc = K.c["c_onescol"]
        pSa, pSb = self.pb[0][0:1, :], self.pb[1][0:1, :]
        def sums(e):
            r = None
            for m in range(n):
                e.matmul(pSa, oc[:], self.Bbf[:, 512 * m:512 * (m + 1)],
                         start=(m == 0), stop=(m == n - 1))
            for m in range(n):
                r = e.matmul(pSb, oc[:], self.Bsq[:, 512 * m:512 * (m + 1)],
                             start=(m == 0), stop=(m == n - 1))
            return r
        return self.K.td.ten(sums, [(sP, tP), (sS, tS)])

    def bcast(self, after_rows):
        K, P = self.K, self.K.P
        o1 = K.c["c_ones1x128_f32"]
        c1, c2, r1, r2 = self.pb[2][:], self.pb[3][:], self.row(1), self.row(2)
        def bc(e):
            e.matmul(c1, o1[:], r1, start=True, stop=True)
            return e.matmul(c2, o1[:], r2, start=True, stop=True)
        return self.K.td.ten(bc, [(P.sem("v"), after_rows)])

    def run_tt(self, b, tt, Bsrc, after_B, resid_in, res_out, ln_out,
               affine=None, final=False, eps_pre=1e-6):
        K, P = self.K, self.K.P
        g = K.g
        tok0 = b * S + tt * 512
        sV = P.sem("v")
        C1, C2 = self.pb[2][:], self.pb[3][:]
        cur = Bsrc
        cur_guard = after_B
        if affine is not None:
            st = self.stats(cur, cur_guard)
            rv = self.ln_rows(D, eps_pre, st)
            bt = self.bcast(rv)
            spp, bpp = affine
            tV = sV.bump(1)
            def apply_aff(e, cur=cur):
                r = None
                for m in range(8):
                    t1 = self.T1[:, 512 * m:512 * (m + 1)]
                    e.tensor_tensor(out=t1, in0=cur[m], in1=C1, op=ALU.mult)
                    e.tensor_tensor(out=t1, in0=t1, in1=C2, op=ALU.add)
                    r = e.tensor_scalar(out=t1, in0=t1, scalar1=spp[:, m:m + 1],
                                        scalar2=bpp[:, m:m + 1], op0=ALU.mult,
                                        op1=ALU.add)
                return r
            P.on("vector", apply_aff, waits=[cur_guard, bt], inc=(sV, 1))
            guard_B = tV
            cur = [self.T1[:, 512 * m:512 * (m + 1)] for m in range(8)]
            cur_guard = (sV, tV)
        st = self.stats(cur, cur_guard)
        rv = self.ln_rows(D, 1e-5, st)
        bt = self.bcast(rv)
        warx = [(sV, sV.n)]
        for m in range(8):
            ldw = P.dma("scalar", self.Xr[:, 512 * m:512 * (m + 1)],
                        g[resid_in][m, :, tok0:tok0 + 512], waits=warx,
                        slot="bx" + self.tag)
        tV = sV.bump(1)
        def apply_res(e, cur=cur):
            r = None
            for m in range(8):
                dst = self.T1[:, 512 * m:512 * (m + 1)]
                e.tensor_tensor(out=dst, in0=cur[m], in1=C1, op=ALU.mult)
                e.tensor_tensor(out=dst, in0=dst, in1=C2, op=ALU.add)
                r = e.tensor_tensor(out=dst, in0=dst,
                                    in1=self.Xr[:, 512 * m:512 * (m + 1)],
                                    op=ALU.add)
            return r
        P.on("vector", apply_res, waits=[cur_guard, bt, ldw], inc=(sV, 1))
        ret_guard = (sV, tV)
        xnew = [self.T1[:, 512 * m:512 * (m + 1)] for m in range(8)]
        out_name = "out" if final else res_out
        for m in range(8):
            P.dma("scalar", g[out_name][m, :, tok0:tok0 + 512], xnew[m],
                  waits=[(sV, tV)], slot="bs" + self.tag)
        if final:
            return ret_guard
        st = self.stats(xnew, (sV, tV))
        rv = self.ln_rows(D, 1e-5, st)
        bt = self.bcast(rv)
        tV2 = sV.bump(1)
        def apply_ln(e):
            r = None
            for m in range(8):
                xr = self.Xr[:, 512 * m:512 * (m + 1)]
                e.tensor_tensor(out=xr, in0=xnew[m], in1=C1, op=ALU.mult)
                r = e.tensor_tensor(out=self.Bbf[:, 512 * m:512 * (m + 1)],
                                    in0=xr, in1=C2, op=ALU.add)
            return r
        P.on("vector", apply_ln,
             waits=[(sV, tV), bt] + ([self.war_bbf] if self.war_bbf else []),
             inc=(sV, 1))
        sw = None
        for m in range(8):
            sw = P.dma("scalar", g[ln_out][m, :, tok0:tok0 + 512],
                       self.Bbf[:, 512 * m:512 * (m + 1)], waits=[(sV, tV2)],
                       slot="bs" + self.tag)
        self.war_bbf = sw
        return ret_guard



def phase_ln0(K):
    P = K.P
    g = K.g
    with contextlib.ExitStack() as stk:
        ps = K.ps8(stk, "ln0")
        bd = Boundary(K, stk, "ln0", ps[4:8])
        X = K.sb(stk, "ln0_x", [128, 8 * 512], F32)
        sA = P.sem("sync_dma")
        sV = P.sem("v")
        for b in range(NB):
            for tt in range(NT):
                tok0 = b * S + tt * 512
                war = [(sV, sV.n)]
                for m in range(8):
                    xw = P.dma("sync", X[:, 512 * m:512 * (m + 1)],
                               g["x0"][m, :, tok0:tok0 + 512], waits=war,
                               slot="nx")
                xt = [X[:, 512 * m:512 * (m + 1)] for m in range(8)]
                st = bd.stats(xt, xw)
                rv = bd.ln_rows(D, 1e-5, st)
                bt = bd.bcast(rv)
                K.td.flush()
                tV = sV.bump(1)
                def apply_ln(e, xt=xt):
                    r = None
                    for m in range(8):
                        t1 = bd.T1[:, 512 * m:512 * (m + 1)]
                        e.tensor_tensor(out=t1, in0=xt[m], in1=bd.pb[2][:],
                                        op=ALU.mult)
                        r = e.tensor_tensor(out=bd.Bsq[:, 512 * m:512 * (m + 1)],
                                            in0=t1, in1=bd.pb[3][:], op=ALU.add)
                    return r
                P.on("vector", apply_ln, waits=[bt], inc=(sV, 1))
                nw = None
                for m in range(8):
                    nw = P.dma("sync", g["xln"][m, :, tok0:tok0 + 512],
                               bd.Bsq[:, 512 * m:512 * (m + 1)],
                               waits=[(sV, tV)], slot="ns")
                bd.war_bbf = nw
        P.barrier()



def phase_ffn(K, pre, resid_in, xln_in, res_out, ln_out, final=False):
    P = K.P
    g = K.g
    with contextlib.ExitStack() as svk:
        svec = {}
        for snm, w in (("s1", FCH), ("s2", FCH), ("sh1", FCH), ("sh2", FCH),
                       ("s3", NCH)):
            svec[snm] = K.sb(svk, f"{pre}_{snm}", [128, w], F32)
            P.dma("sync", svec[snm][:], g[f"{pre}_{snm}"])
        sW = P.sem("gpsimd_dma")
        sA = P.sem("sync_dma")
        sT = P.sem("t")
        sV = P.sem("v")
        sS = P.sem("s")
        war = {}
        P.barrier()

        for b in range(NB):
            tok0 = b * S
            with contextlib.ExitStack() as hstk:
                hid = K.sb(hstk, f"{pre}_hid{b}", [128, FCH * S], BF16)
                self_dummy = None
                stk = contextlib.ExitStack()
                ps = K.ps8(stk, f"{pre}a{b}")
                xlnb = K.sb(stk, f"{pre}_xln{b}", [128, NCH * S], BF16)
                wa = [K.sb(stk, f"{pre}_wa{i}{b}", [128, NCH * 128], BF16)
                      for i in range(4)]
                sil = K.sb(stk, f"{pre}_sil{b}", [128, 4 * 512], F32)
                for m in range(8):
                    xw = P.dma("sync", xlnb[:, m * S:(m + 1) * S],
                               g[xln_in][m, :, tok0:tok0 + S],
                               waits=[(sT, sT.n)], slot="fx")
                for m in range(FCH):
                    wb1, wb2 = wa[(m % 2) * 2], wa[(m % 2) * 2 + 1]
                    ww = {}
                    for wi, wnm in ((0, "w1m"), (1, "w2m")):
                        wk_key = ("wa", b, (m % 2) * 2 + wi)
                        ww[wi] = P.dma("gpsimd",
                              (wb1 if wi == 0 else wb2)[:]
                              .rearrange("p (k j) -> p k j", k=NCH),
                              g[f"{pre}_{wnm}"][m].rearrange("(k p) j -> p k j", p=128),
                              waits=[war[wk_key]] if wk_key in war else (),
                              slot=f"fa{(m % 2) * 2 + wi}")
                    def mm1(e, wb1=wb1):
                        r = None
                        for tt in range(4):
                            for k in range(NCH):
                                r = e.matmul(
                                    ps[tt][:], wb1[:, k * 128:(k + 1) * 128],
                                    xlnb[:, k * S + tt * 512:k * S + (tt + 1) * 512],
                                    start=(k == 0), stop=(k == NCH - 1))
                        return r
                    t_mm1 = sT.bump(1)
                    P.on("tensor", mm1,
                         waits=[ww[0], xw, (sV, sV.n), (sS, sS.n)],
                         inc=(sT, 1))
                    war[("wa", b, (m % 2) * 2)] = (sT, t_mm1)
                    t_sil = sS.bump(1)
                    def ev1(e, m=m):
                        r = None
                        for tt in range(4):
                            r = e.activation(out=sil[:, tt * 512:(tt + 1) * 512],
                                             in_=ps[tt][:], func=AF.Silu,
                                             scale=svec["s1"][:, m:m + 1])
                        return r
                    P.on("scalar", ev1, waits=[(sT, t_mm1)], inc=(sS, 1))
                    def mm2(e, wb2=wb2):
                        r = None
                        for tt in range(4):
                            for k in range(NCH):
                                r = e.matmul(
                                    ps[4 + tt][:], wb2[:, k * 128:(k + 1) * 128],
                                    xlnb[:, k * S + tt * 512:k * S + (tt + 1) * 512],
                                    start=(k == 0), stop=(k == NCH - 1))
                        return r
                    t_mm2 = sT.bump(1)
                    P.on("tensor", mm2, waits=[ww[1], (sV, sV.n)], inc=(sT, 1))
                    war[("wa", b, (m % 2) * 2 + 1)] = (sT, t_mm2)
                    t_u = sV.bump(1)
                    def ev2(e, m=m):
                        r = None
                        for tt in range(4):
                            r = e.scalar_tensor_tensor(
                                out=hid[:, m * S + tt * 512:m * S + (tt + 1) * 512],
                                in0=ps[4 + tt][:], scalar=svec["s2"][:, m:m + 1],
                                in1=sil[:, tt * 512:(tt + 1) * 512],
                                op0=ALU.mult, op1=ALU.mult)
                        return r
                    P.on("vector", ev2, waits=[(sT, t_mm2), (sS, t_sil)],
                         inc=(sV, 1))
                if FFN_DEBUG == 1:
                    for kk in range(FCH):
                        P.dma("sync", g["hscr2"][kk], hid[:, kk * S:(kk + 1) * S],
                              waits=[(sV, sV.n)] if kk == 0 else ())
                P.barrier()
                stk.close()
                stk = contextlib.ExitStack()
                ps = K.ps8(stk, f"{pre}bc{b}")
                wtile = [K.sb(stk, f"{pre}_wt{i}{b}", [128, FCH * 128], BF16)
                         for i in range(2)]
                vstg = [K.sb(stk, f"{pre}_vs{i}{b}", [128, S], BF16)
                        for i in range(2)]
                stages_bc = (("B", "h1m", "sh1", None, "hscr"),) if FFN_DEBUG == 1 else (
                    ("B", "h1m", "sh1", None, "hscr"),
                    ("C", "h2m", "sh2", "hscr", "hscr2"))
                for stage, wnm, snm, src_dram, dst_dram in stages_bc:
                    hidw = None
                    if src_dram is not None:
                        stw = war.get(("vs", b, 0)), war.get(("vs", b, 1))
                        w0 = [(sT, sT.n)] + [x for x in stw if x]
                        for kk in range(FCH):
                            hidw = P.dma("sync", hid[:, kk * S:(kk + 1) * S],
                                         g[src_dram][kk],
                                         waits=w0 if kk == 0 else (), slot="fh")
                    for m in range(FCH):
                        wb = wtile[m % 2]
                        wt_key = ("wt", b, m % 2)
                        wtw = P.dma("gpsimd",
                              wb[:].rearrange("p (k j) -> p k j", k=FCH),
                              g[f"{pre}_{wnm}"][m].rearrange("(k p) j -> p k j", p=128),
                              waits=[war[wt_key]] if wt_key in war else (),
                              slot=f"fw{m % 2}")
                        def mmh(e, wb=wb, m=m):
                            r = None
                            for k in range(FCH):
                                for tt in range(4):
                                    r = e.matmul(
                                        ps[(m % 2) * 4 + tt][:],
                                        wb[:, k * 128:(k + 1) * 128],
                                        hid[:, k * S + tt * 512:k * S + (tt + 1) * 512],
                                        start=(k == 0), stop=(k == FCH - 1))
                            return r
                        t_mm = sT.bump(1)
                        P.on("tensor", mmh,
                             waits=[wtw, (sS, sS.n)] + ([hidw] if hidw else [xw]),
                             inc=(sT, 1))
                        war[("wt", b, m % 2)] = (sT, t_mm)
                        t_g = sS.bump(1)
                        vs = vstg[m % 2]
                        def evg(e, m=m, vs=vs, snm=snm):
                            r = None
                            for tt in range(4):
                                r = e.activation(
                                    out=vs[:, tt * 512:(tt + 1) * 512],
                                    in_=ps[(m % 2) * 4 + tt][:], func=AF.Gelu,
                                    scale=svec[snm][:, m:m + 1])
                            return r
                        vsw = war.get(("vs", b, m % 2))
                        P.on("scalar", evg,
                             waits=[(sT, t_mm)] + ([vsw] if vsw else []),
                             inc=(sS, 1))
                        war[("vs", b, m % 2)] = P.dma(
                            "sync", g[dst_dram][m], vs[:], waits=[(sS, t_g)],
                            slot=f"fs{m % 2}")
                P.barrier()
                stk.close()
            if FFN_DEBUG:
                continue
            with contextlib.ExitStack() as stk:
                ps = K.ps8(stk, f"{pre}d{b}")
                w3sb = K.sb(stk, f"{pre}_w3sb{b}", [128, NCH * FCH * 128], BF16)
                wslab = [K.sb(stk, f"{pre}_ws{i}{b}", [128, 512], BF16)
                         for i in range(4)]
                Bout = [K.sb(stk, f"{pre}_bo{i}{b}", [128, 8 * 512], F32)
                        for i in range(2)]
                bd = Boundary(K, stk, f"{pre}_bd{b}", ps[4:8])
                for m in range(NCH):
                    w3w = P.dma("gpsimd",
                          w3sb[:].rearrange("p (k m j) -> p k m j", k=FCH, m=NCH)
                          [:, :, m, :],
                          g[f"{pre}_w3m"][m].rearrange("(k p) j -> p k j", p=128),
                          slot="fw3")
                for tt in range(4):
                    Bo = Bout[tt % 2]
                    t_evs = []
                    for wave in range(2):
                        for k in range(FCH):
                            wsl = wslab[k % 4]
                            ws_key = ("wsl", pre, b, k % 4)
                            slw = P.dma("sync", wsl[:],
                                  g["hscr2"][k][:, tt * 512:(tt + 1) * 512],
                                  waits=[war[ws_key]] if ws_key in war else (),
                                  slot=f"fd{k % 4}")
                            waits = [slw, w3w]
                            if k == 0:
                                bw = war.get(("dbank", pre, b))
                                if bw is not None:
                                    waits.append(bw)
                            def mmd(e, wsl=wsl, k=k, wave=wave):
                                r = None
                                for mi in range(4):
                                    m = wave * 4 + mi
                                    r = e.matmul(
                                        ps[mi][:],
                                        w3sb[:, (k * NCH + m) * 128:
                                             (k * NCH + m + 1) * 128],
                                        wsl[:], start=(k == 0),
                                        stop=(k == FCH - 1))
                                return r
                            t_mm = sT.bump(1)
                            P.on("tensor", mmd, waits=waits, inc=(sT, 1))
                            war[ws_key] = (sT, t_mm)
                            K.td.flush_one()
                        t_ev = sV.bump(1)
                        def evd(e, wave=wave, Bo=Bo):
                            r = None
                            for mi in range(4):
                                m = wave * 4 + mi
                                r = e.tensor_scalar(
                                    out=Bo[:, m * 512:(m + 1) * 512],
                                    in0=ps[mi][:],
                                    scalar1=svec["s3"][:, m:m + 1], scalar2=None,
                                    op0=ALU.mult)
                            return r
                        bo_w = war.get(("bout", pre, b, tt % 2))
                        P.on("vector", evd,
                             waits=[(sT, sT.n)] + ([bo_w] if bo_w else []),
                             inc=(sV, 1))
                        war[("dbank", pre, b)] = (sV, t_ev)
                        t_evs.append(t_ev)
                    Bt = [Bo[:, m * 512:(m + 1) * 512] for m in range(NCH)]
                    rg = bd.run_tt(b, tt, Bt, (sV, t_evs[-1]), resid_in, res_out,
                                   ln_out, affine=None, final=final)
                    war[("bout", pre, b, tt % 2)] = rg
                    if FFN_SERIAL:
                        K.td.flush()
                        war[("dbank", pre, b)] = rg
                K.td.flush()
                P.barrier()
    P.barrier()


def phase_l0attn(K, resid_in, xln_in, res_out, ln_out):
    P = K.P
    g = K.g
    c = K.c
    sW = P.sem("gpsimd_dma")
    sA = P.sem("sync_dma")
    sT = P.sem("t")
    sV = P.sem("v")
    sS = P.sem("s")
    sT2 = P.sem("t2")
    war = {}
    GEPS = 1e-6
    GINV = 1.0 / (S * 32)

    with contextlib.ExitStack() as pstk:
        vec = {}
        for nm in ("gns", "gnb", "ln1s", "ln1b", "ln2s", "ln2b", "b2"):
            vec[nm] = K.sb(pstk, f"l0a_{nm}v", [128, NCH], F32)
            P.dma("sync", vec[nm][:], g[f"l0a_{nm}"])
        vec["b1"] = K.sb(pstk, "l0a_b1v", [128, FCH], F32)
        P.dma("sync", vec["b1"][:], g["l0a_b1"])
        P.barrier()

        for b in range(NB):
            tok0 = b * S
            with contextlib.ExitStack() as bstk:
                zb = K.sb(bstk, f"l0a_z{b}", [128, NCH * S], BF16)
                with contextlib.ExitStack() as stk:
                    ps = K.ps8(stk, f"l0ap{b}")
                    xlnb = K.sb(stk, f"l0a_xln{b}", [128, NCH * S], BF16)
                    T = {nm: K.sb(stk, f"l0a_{nm}{b}", [128, S], F32)
                         for nm in ("tq", "tk", "tv", "tar", "tai", "tg",
                                    "sc1", "sc2", "sc3")}
                    wsl = [K.sb(stk, f"l0a_ws{i}{b}", [128, NCH * 128], BF16)
                           for i in range(12)]
                    Sg = K.sb(stk, f"l0a_sg{b}", [128, 8], F32)
                    Sg2 = K.sb(stk, f"l0a_sg2{b}", [8, 8], F32)
                    lnbf = K.sb(stk, f"l0a_lbf{b}", [128, 8 * 512], BF16)
                    lnsq = K.sb(stk, f"l0a_lsq{b}", [128, 8 * 512], BF16)
                    rows = K.sb(stk, f"l0a_rows{b}", [1, 4 * 512], F32)
                    for m in range(8):
                        xw = P.dma("sync", xlnb[:, m * S:(m + 1) * S],
                                   g[xln_in][m, :, tok0:tok0 + S],
                                   waits=[(sT, sT.n)], slot="ax")
                    pool_pi2 = c["c_pool"][:, 0:1]
                    pool_half = c["c_pool"][:, 1:2]

                    pending_stats = []
                    for ch in range(NCH):
                        wset = wsl[(ch % 2) * 6:(ch % 2) * 6 + 6]
                        srcs = [("l0a_wq", ch * 128), ("l0a_wk", ch * 128),
                                ("l0a_wv", ch * 128), ("l0a_wg", ch * 128),
                                ("l0a_wa", ch * 128), ("l0a_wa", D + ch * 128)]
                        wq_key = ("l0aw", b, ch % 2)
                        for wi, (wn, off) in enumerate(srcs):
                            wsw = P.dma("gpsimd",
                                  wset[wi][:].rearrange("p (k j) -> p k j", k=NCH),
                                  g[wn][:, off:off + 128]
                                  .rearrange("(k p) j -> p k j", p=128),
                                  waits=([war[wq_key]] if wq_key in war else ())
                                  if wi == 0 else (), slot=f"aw{ch % 2}")
                        quants = [("tq", 0, "dve"), ("tk", 1, "dve"),
                                  ("tv", 2, "dve"), ("tar", 4, "act"),
                                  ("tai", 5, "act"), ("tg", 3, "silu")]
                        for qi, (dnm, wi, ev) in enumerate(quants):
                            if qi == 1 and pending_stats:
                                for f in pending_stats:
                                    f()
                                pending_stats = []
                            bank = (qi % 2) * 4
                            def mmq(e, wi=wi, bank=bank, wset=wset):
                                r = None
                                for tt in range(4):
                                    for k in range(NCH):
                                        r = e.matmul(
                                            ps[bank + tt][:],
                                            wset[wi][:, k * 128:(k + 1) * 128],
                                            xlnb[:, k * S + tt * 512:
                                                 k * S + (tt + 1) * 512],
                                            start=(k == 0), stop=(k == NCH - 1))
                                return r
                            t_mm = sT.bump(1)
                            bank_war = war.get(("l0abank", b, qi % 2), ())
                            P.on("tensor", mmq,
                                 waits=[wsw, xw] + list(bank_war),
                                 inc=(sT, 1))
                            war[("l0aw", b, ch % 2)] = (sT, t_mm)
                            if ev == "dve":
                                t_e = sV.bump(1)
                                def evq(e, dnm=dnm, bank=bank):
                                    r = None
                                    for tt in range(4):
                                        r = e.tensor_copy(
                                            out=T[dnm][:, tt * 512:(tt + 1) * 512],
                                            in_=ps[bank + tt][:])
                                    return r
                                P.on("vector", evq, waits=[(sT, t_mm)], inc=(sV, 1))
                                war[("l0abank", b, qi % 2)] = ((sV, t_e),)
                            else:
                                fn = AF.Silu if ev == "silu" else AF.Identity
                                t_e = sS.bump(1)
                                def evq(e, dnm=dnm, bank=bank, fn=fn):
                                    r = None
                                    for tt in range(4):
                                        r = e.activation(
                                            out=T[dnm][:, tt * 512:(tt + 1) * 512],
                                            in_=ps[bank + tt][:], func=fn)
                                    return r
                                P.on("scalar", evq, waits=[(sT, t_mm), (sV, sV.n)],
                                     inc=(sS, 1))
                                war[("l0abank", b, qi % 2)] = ((sS, t_e),)
                        evac_v = sV.n
                        evac_s = sS.n

                        def dve(fn, waits=()):
                            t = sV.bump(1)
                            P.on("vector", fn, waits=waits, inc=(sV, 1))
                            return t

                        def act(fn, waits=()):
                            t = sS.bump(1)
                            P.on("scalar", fn, waits=waits, inc=(sS, 1))
                            return t

                        tq, tk, tv = T["tq"][:], T["tk"][:], T["tv"][:]
                        tar, tai, tg = T["tar"][:], T["tai"][:], T["tg"][:]
                        s1, s2, s3 = T["sc1"][:], T["sc2"][:], T["sc3"][:]
                        dve(lambda e: (
                            e.tensor_tensor(out=tk, in0=tk, in1=tv, op=ALU.mult),
                            e.tensor_tensor(out=s1, in0=tar, in1=tar, op=ALU.mult),
                            e.tensor_tensor(out=s2, in0=tai, in1=tai, op=ALU.mult),
                            e.tensor_tensor(out=s1, in0=s1, in1=s2, op=ALU.add),
                        )[-1], waits=[(sS, evac_s)])
                        act(lambda e: (
                            e.activation(out=tv, in_=s1, func=AF.Sqrt),
                            e.activation(out=tv, in_=tv, func=AF.Sigmoid),
                            e.activation(out=s2, in_=tar, func=AF.Sign),
                            e.activation(out=s2, in_=s2, func=AF.Sign, bias=pool_half),
                            e.activation(out=s1, in_=tar, func=AF.Abs),
                        )[-1], waits=[(sV, sV.n)])
                        dve(lambda e: (
                            e.tensor_scalar(out=s1, in0=s1, scalar1=1e-30,
                                            scalar2=None, op0=ALU.max),
                        )[-1], waits=[(sS, sS.n)])
                        act(lambda e: (
                            e.activation(out=s1, in_=s1, func=AF.Ln),
                            e.activation(out=s1, in_=s1, func=AF.Exp, scale=-1.0),
                        )[-1], waits=[(sV, sV.n)])
                        dve(lambda e: (
                            e.tensor_tensor(out=s1, in0=s1, in1=s2, op=ALU.mult),
                            e.tensor_tensor(out=s1, in0=tai, in1=s1, op=ALU.mult),
                        )[-1], waits=[(sS, sS.n)])
                        act(lambda e: (
                            e.activation(out=s1, in_=s1, func=AF.Arctan),
                            e.activation(out=tar, in_=tai, func=AF.Sign),
                        )[-1], waits=[(sV, sV.n)])
                        dve(lambda e: (
                            e.tensor_scalar(out=s3, in0=s2, scalar1=-PI / 2,
                                            scalar2=PI / 2, op0=ALU.mult, op1=ALU.add),
                            e.tensor_tensor(out=s3, in0=s3, in1=tar, op=ALU.mult),
                            e.tensor_tensor(out=s1, in0=s1, in1=s3, op=ALU.add),
                            e.tensor_tensor_scan(out=tai, data0=s1, data1=s1,
                                                 initial=0.0, op0=ALU.add,
                                                 op1=ALU.bypass),
                            e.tensor_scalar(out=s2, in0=tai, scalar1=INV2PI,
                                            scalar2=None, op0=ALU.mult),
                            e.tensor_scalar(out=s3, in0=s2, scalar1=MAGIC,
                                            scalar2=MAGIC, op0=ALU.add,
                                            op1=ALU.subtract),
                            e.scalar_tensor_tensor(out=s3, in0=s3, scalar=-TWOPI,
                                                   in1=tai, op0=ALU.mult,
                                                   op1=ALU.add),
                            e.tensor_scalar(out=s2, in0=s2, scalar1=0.25,
                                            scalar2=None, op0=ALU.add),
                            e.tensor_scalar(out=s2, in0=s2, scalar1=MAGIC,
                                            scalar2=MAGIC, op0=ALU.add,
                                            op1=ALU.subtract),
                            e.scalar_tensor_tensor(out=s2, in0=s2, scalar=-TWOPI,
                                                   in1=tai, op0=ALU.mult,
                                                   op1=ALU.add),
                        )[-1], waits=[(sS, sS.n)])
                        act(lambda e: (
                            e.activation(out=s3, in_=s3, func=AF.Sin),
                            e.activation(out=s2, in_=s2, func=AF.Sin, bias=pool_pi2),
                        )[-1], waits=[(sV, sV.n)])
                        t_chain = dve(lambda e: (
                            e.tensor_tensor(out=tar, in0=s2, in1=tk, op=ALU.mult),
                            e.tensor_tensor(out=tai, in0=s3, in1=tk, op=ALU.mult),
                            e.tensor_tensor_scan(out=tk, data0=tv, data1=tar,
                                                 initial=0.0, op0=ALU.mult,
                                                 op1=ALU.add),
                            e.tensor_tensor_scan(out=s1, data0=tv, data1=tai,
                                                 initial=0.0, op0=ALU.mult,
                                                 op1=ALU.add),
                            e.tensor_tensor(out=tar, in0=s2, in1=tk, op=ALU.mult),
                            e.tensor_tensor(out=tai, in0=s3, in1=s1, op=ALU.mult),
                            e.tensor_tensor(out=tar, in0=tar, in1=tai, op=ALU.add),
                            e.tensor_tensor(out=tar, in0=tq, in1=tar, op=ALU.mult),
                            e.tensor_tensor(out=tai, in0=s3, in1=tk, op=ALU.mult),
                            e.tensor_tensor(out=tk, in0=s2, in1=s1, op=ALU.mult),
                            e.tensor_tensor(out=tai, in0=tai, in1=tk, op=ALU.subtract),
                            e.tensor_tensor(out=tai, in0=tq, in1=tai, op=ALU.mult),
                        )[-1], waits=[(sS, sS.n)])
                        dve(lambda e: (
                            e.tensor_reduce(out=Sg[:, 0:1], in_=tar, axis=AX.X,
                                            op=ALU.add),
                            e.drain(),
                            e.tensor_reduce(out=Sg[:, 1:2], in_=tai, axis=AX.X,
                                            op=ALU.add),
                            e.drain(),
                        )[-1])
                        act(lambda e: (
                            e.activation(out=s2, in_=tar, func=AF.Square,
                                         accum_out=Sg[:, 2:3]),
                            e.drain(),
                            e.activation(out=s3, in_=tai, func=AF.Square,
                                         accum_out=Sg[:, 3:4]),
                            e.drain(),
                        )[-1], waits=[(sV, sV.n)])
                        t_sg = dve(lambda e: (
                            e.tensor_tensor(out=Sg[:, 2:3], in0=Sg[:, 2:3],
                                            in1=Sg[:, 3:4], op=ALU.add),
                            e.drain(),
                        )[-1], waits=[(sS, sS.n)])
                        t_g1 = sT2.bump(1)
                        def emit_g1(t_sg=t_sg):
                            P.on("tensor", lambda e: e.matmul(
                                ps[7][0:4, 0:3], c["c_gsel"][:], Sg[:, 0:3],
                                start=True, stop=True),
                                waits=[(sV, t_sg)], inc=(sT2, 1))
                        pending_stats.append(emit_g1)
                        dve(lambda e: (
                            e.tensor_scalar(out=Sg2[0:4, 2:5], in0=ps[7][0:4, 0:3],
                                            scalar1=GINV, scalar2=None, op0=ALU.mult),
                            e.drain(),
                            e.tensor_tensor(out=Sg2[0:4, 5:6], in0=Sg2[0:4, 2:3],
                                            in1=Sg2[0:4, 2:3], op=ALU.mult),
                            e.tensor_tensor(out=Sg2[0:4, 6:7], in0=Sg2[0:4, 3:4],
                                            in1=Sg2[0:4, 3:4], op=ALU.mult),
                            e.drain(),
                            e.tensor_tensor(out=Sg2[0:4, 5:6], in0=Sg2[0:4, 5:6],
                                            in1=Sg2[0:4, 6:7], op=ALU.add),
                            e.drain(),
                            e.tensor_tensor(out=Sg2[0:4, 5:6], in0=Sg2[0:4, 4:5],
                                            in1=Sg2[0:4, 5:6], op=ALU.subtract),
                            e.drain(),
                            e.tensor_scalar(out=Sg2[0:4, 5:6], in0=Sg2[0:4, 5:6],
                                            scalar1=GEPS, scalar2=None, op0=ALU.add),
                            e.drain(),
                        )[-1], waits=[(sT2, t_g1)])
                        act(lambda e: (
                            e.activation(out=Sg2[0:4, 1:2], in_=Sg2[0:4, 5:6],
                                         func=AF.Ln),
                            e.drain(),
                            e.activation(out=Sg2[0:4, 1:2], in_=Sg2[0:4, 1:2],
                                         func=AF.Exp, scale=-0.5),
                            e.drain(),
                        )[-1], waits=[(sV, sV.n)])
                        t_sm = dve(lambda e: (
                            e.tensor_copy(out=Sg2[0:4, 0:1], in_=Sg2[0:4, 2:3]),
                            e.drain(),
                        )[-1], waits=[(sS, sS.n)])
                        t_g2 = sT2.bump(1)
                        def emit_g2(t_sm=t_sm):
                            P.on("tensor", lambda e: e.matmul(
                                ps[7][:, 4:6], c["c_gexp"][:], Sg2[0:4, 0:2],
                                start=True, stop=True),
                                waits=[(sV, t_sm)], inc=(sT2, 1))
                        pending_stats.append(emit_g2)
                        t_ab = dve(lambda e, ch=ch: (
                            e.tensor_tensor(out=Sg[:, 4:5], in0=ps[7][:, 5:6],
                                            in1=vec["gns"][:, ch:ch + 1], op=ALU.mult),
                            e.drain(),
                            e.tensor_tensor(out=Sg[:, 5:6], in0=ps[7][:, 4:5],
                                            in1=Sg[:, 4:5], op=ALU.mult),
                            e.drain(),
                            e.scalar_tensor_tensor(out=Sg[:, 5:6], in0=Sg[:, 5:6],
                                                   scalar=-1.0,
                                                   in1=vec["gnb"][:, ch:ch + 1],
                                                   op0=ALU.mult, op1=ALU.add),
                            e.drain(),
                            e.tensor_scalar(out=s2, in0=tar, scalar1=Sg[:, 4:5],
                                            scalar2=Sg[:, 5:6], op0=ALU.mult,
                                            op1=ALU.add),
                        )[-1], waits=[(sT2, t_g2)])
                        dve(lambda e, ch=ch: e.tensor_tensor(
                            out=zb[:, ch * S:(ch + 1) * S], in0=s2, in1=tg,
                            op=ALU.mult))

                    for f in pending_stats:
                        f()
                    pending_stats = []
                    P.barrier()
                    for mo in range(8):
                        wow = P.dma("gpsimd",
                              wsl[mo][:].rearrange("p (k j) -> p k j", k=NCH),
                              g["l0a_wo"][:, mo * 128:(mo + 1) * 128]
                              .rearrange("(k p) j -> p k j", p=128), slot="ao")
                    oc = c["c_onescol"]
                    o1f = c["c_ones1x128_f32"]
                    for tt in range(4):
                        xi = [T["tq"][:, m * 512:(m + 1) * 512] for m in range(4)] + \
                             [T["tk"][:, m * 512:(m + 1) * 512] for m in range(4)]
                        ev_ts = []
                        for wave in range(2):
                            def mmo(e, tt=tt, wave=wave):
                                r = None
                                for mi in range(4):
                                    mo = wave * 4 + mi
                                    for k in range(NCH):
                                        e.matmul(
                                            ps[mi][:],
                                            wsl[mo][:, k * 128:(k + 1) * 128],
                                            zb[:, k * S + tt * 512:
                                               k * S + (tt + 1) * 512],
                                            start=(k == 0), stop=False)
                                    r = e.matmul(
                                        ps[mi][:], c["c_id128"][:],
                                        xlnb[:, mo * S + tt * 512:
                                             mo * S + (tt + 1) * 512],
                                        start=False, stop=True)
                                return r
                            K.td.flush_one()
                            K.td.flush_one()
                            bw = war.get(("wbank", b))
                            t_mm = sT.bump(1)
                            P.on("tensor", mmo,
                                 waits=[wow] + ([bw] if bw else []),
                                 inc=(sT, 1))
                            t_ev = sV.bump(1)
                            def evo(e, xi=xi, wave=wave):
                                r = None
                                for mi in range(4):
                                    r = e.tensor_copy(out=xi[wave * 4 + mi],
                                                      in_=ps[mi][:])
                                return r
                            P.on("vector", evo, waits=[(sT, t_mm)], inc=(sV, 1))
                            war[("wbank", b)] = (sV, t_ev)
                            ev_ts.append(t_ev)
                        t_p = P.sem("p").bump(1)
                        def castf(e, xi=xi):
                            r = None
                            for m in range(8):
                                r = e.tensor_copy(
                                    out=lnbf[:, m * 512:(m + 1) * 512], in_=xi[m])
                            return r
                        P.on("gpsimd", castf, waits=[(sV, ev_ts[-1])],
                             inc=(P.sem("p"), 1))
                        t_sq = sS.bump(1)
                        def sqf(e, xi=xi):
                            r = None
                            for m in range(8):
                                r = e.activation(
                                    out=lnsq[:, m * 512:(m + 1) * 512], in_=xi[m],
                                    func=AF.Square)
                            return r
                        P.on("scalar", sqf, waits=[(sV, ev_ts[-1])], inc=(sS, 1))
                        def sums(e):
                            r = None
                            for m in range(8):
                                e.matmul(ps[4][0:1, :], oc[:],
                                         lnbf[:, m * 512:(m + 1) * 512],
                                         start=(m == 0), stop=(m == 7))
                            for m in range(8):
                                r = e.matmul(ps[5][0:1, :], oc[:],
                                             lnsq[:, m * 512:(m + 1) * 512],
                                             start=(m == 0), stop=(m == 7))
                            return r
                        t_sum = K.td.ten(sums, [(P.sem("p"), t_p), (sS, t_sq)])
                        mu, rr, nmr, va = (rows[0:1, i * 512:(i + 1) * 512]
                                           for i in range(4))
                        t_r1 = sV.bump(1)
                        P.on("vector", lambda e, mu=mu, rr=rr, nmr=nmr, va=va: (
                            e.tensor_scalar(out=mu, in0=ps[4][0:1, :],
                                            scalar1=1.0 / D, scalar2=None,
                                            op0=ALU.mult),
                            e.tensor_scalar(out=va, in0=ps[5][0:1, :],
                                            scalar1=1.0 / D, scalar2=None,
                                            op0=ALU.mult),
                            e.tensor_tensor(out=nmr, in0=mu, in1=mu, op=ALU.mult),
                            e.tensor_tensor(out=va, in0=va, in1=nmr, op=ALU.subtract),
                            e.tensor_scalar(out=va, in0=va, scalar1=GEPS,
                                            scalar2=None, op0=ALU.add),
                        )[-1], waits=[t_sum], inc=(sV, 1))
                        t_r2 = sS.bump(1)
                        P.on("scalar", lambda e, rr=rr, va=va: (
                            e.activation(out=rr, in_=va, func=AF.Ln),
                            e.activation(out=rr, in_=rr, func=AF.Exp, scale=-0.5),
                        )[-1], waits=[(sV, t_r1)], inc=(sS, 1))
                        t_r3 = sV.bump(1)
                        P.on("vector", lambda e, mu=mu, rr=rr, nmr=nmr: (
                            e.tensor_tensor(out=nmr, in0=mu, in1=rr, op=ALU.mult),
                            e.tensor_scalar(out=nmr, in0=nmr, scalar1=-1.0,
                                            scalar2=None, op0=ALU.mult),
                        )[-1], waits=[(sS, t_r2)], inc=(sV, 1))
                        def bc(e, rr=rr, nmr=nmr):
                            e.matmul(ps[6][:], o1f[:], rr, start=True, stop=True)
                            return e.matmul(ps[7][:], o1f[:], nmr, start=True,
                                            stop=True)
                        t_bc = K.td.ten(bc, [(sV, t_r3)])
                        t_ap = sV.bump(1)
                        def apl(e, xi=xi, tt=tt):
                            r = None
                            for m in range(8):
                                sc = T["sc1"][:, m % 4 * 512:(m % 4 + 1) * 512]
                                e.tensor_tensor(out=sc, in0=xi[m], in1=ps[6][:],
                                                op=ALU.mult)
                                e.tensor_tensor(out=sc, in0=sc, in1=ps[7][:],
                                                op=ALU.add)
                                r = e.tensor_scalar(
                                    out=zb[:, m * S + tt * 512:m * S + (tt + 1) * 512],
                                    in0=sc, scalar1=vec["ln1s"][:, m:m + 1],
                                    scalar2=vec["ln1b"][:, m:m + 1],
                                    op0=ALU.mult, op1=ALU.add)
                            return r
                        P.on("vector", apl, waits=[t_bc], inc=(sV, 1))
                    K.td.flush()
                    P.barrier()
                with contextlib.ExitStack() as stk:
                    ps = K.ps8(stk, f"l0am1{b}")
                    wa = [K.sb(stk, f"l0a_w1t{i}{b}", [128, NCH * 128], BF16)
                          for i in range(2)]
                    vstg = [K.sb(stk, f"l0a_vs{i}{b}", [128, S], BF16)
                            for i in range(2)]
                    for m in range(FCH):
                        wb = wa[m % 2]
                        m1_key = ("l0am1", b, m % 2)
                        m1w = P.dma("gpsimd",
                              wb[:].rearrange("p (k j) -> p k j", k=NCH),
                              g["l0a_w1m"][m].rearrange("(k p) j -> p k j", p=128),
                              waits=[war[m1_key]] if m1_key in war else (),
                              slot=f"am{m % 2}")
                        def mmh(e, wb=wb, m=m):
                            r = None
                            for tt in range(4):
                                for k in range(NCH):
                                    r = e.matmul(
                                        ps[(m % 2) * 4 + tt][:],
                                        wb[:, k * 128:(k + 1) * 128],
                                        zb[:, k * S + tt * 512:k * S + (tt + 1) * 512],
                                        start=(k == 0), stop=(k == NCH - 1))
                            return r
                        t_mm = sT.bump(1)
                        P.on("tensor", mmh, waits=[m1w, (sS, sS.n)],
                             inc=(sT, 1))
                        war[("l0am1", b, m % 2)] = (sT, t_mm)
                        t_gl = sS.bump(1)
                        vs = vstg[m % 2]
                        def evh(e, m=m, vs=vs):
                            r = None
                            for tt in range(4):
                                r = e.activation(
                                    out=vs[:, tt * 512:(tt + 1) * 512],
                                    in_=ps[(m % 2) * 4 + tt][:],
                                    func=AF.Gelu_apprx_tanh,
                                    bias=vec["b1"][:, m:m + 1])
                            return r
                        vsw = war.get(("avs", b, m % 2))
                        P.on("scalar", evh,
                             waits=[(sT, t_mm)] + ([vsw] if vsw else []),
                             inc=(sS, 1))
                        war[("avs", b, m % 2)] = P.dma(
                            "sync", g["hscr"][m], vs[:], waits=[(sS, t_gl)],
                            slot=f"as{m % 2}")
                    P.barrier()
                with contextlib.ExitStack() as stk:
                    ps = K.ps8(stk, f"l0am2{b}")
                    w2sb = K.sb(stk, f"l0a_w2sb{b}", [128, NCH * FCH * 128], BF16)
                    wslab = [K.sb(stk, f"l0a_wsl{i}{b}", [128, 512], BF16)
                             for i in range(2)]
                    Bout = [K.sb(stk, f"l0a_bo{i}{b}", [128, 8 * 512], F32)
                            for i in range(2)]
                    bd = Boundary(K, stk, f"l0a_bd{b}", ps[4:8])
                    for m in range(NCH):
                        w2w = P.dma("gpsimd",
                              w2sb[:].rearrange("p (k m j) -> p k m j", k=FCH, m=NCH)
                              [:, :, m, :],
                              g["l0a_w2m"][m].rearrange("(k p) j -> p k j", p=128),
                              slot="aw2")
                    for tt in range(4):
                        Bo = Bout[tt % 2]
                        t_evs = []
                        for wave in range(2):
                            for k in range(FCH):
                                wsl2 = wslab[k % 2]
                                m2_key = ("l0am2", b, k % 2)
                                slw = P.dma("sync", wsl2[:],
                                      g["hscr"][k][:, tt * 512:(tt + 1) * 512],
                                      waits=[war[m2_key]] if m2_key in war else (),
                                      slot=f"a2{k % 2}")
                                waits = [slw, w2w]
                                if k == 0:
                                    bw = war.get(("m2bank", b))
                                    if bw is not None:
                                        waits.append(bw)
                                def mmd(e, wsl2=wsl2, k=k, wave=wave):
                                    r = None
                                    for mi in range(4):
                                        m = wave * 4 + mi
                                        r = e.matmul(
                                            ps[mi][:],
                                            w2sb[:, (k * NCH + m) * 128:
                                                 (k * NCH + m + 1) * 128],
                                            wsl2[:], start=(k == 0), stop=False)
                                    return r
                                t_mm = sT.bump(1)
                                P.on("tensor", mmd, waits=waits, inc=(sT, 1))
                                war[("l0am2", b, k % 2)] = (sT, t_mm)
                                K.td.flush_one()
                            def mmr(e, tt=tt, wave=wave):
                                r = None
                                for mi in range(4):
                                    m = wave * 4 + mi
                                    r = e.matmul(
                                        ps[mi][:], c["c_id128"][:],
                                        zb[:, m * S + tt * 512:m * S + (tt + 1) * 512],
                                        start=False, stop=True)
                                return r
                            t_mm = sT.bump(1)
                            P.on("tensor", mmr, inc=(sT, 1))
                            t_ev = sV.bump(1)
                            def evd(e, wave=wave, Bo=Bo):
                                r = None
                                for mi in range(4):
                                    m = wave * 4 + mi
                                    r = e.tensor_scalar(
                                        out=Bo[:, m * 512:(m + 1) * 512],
                                        in0=ps[mi][:], scalar1=vec["b2"][:, m:m + 1],
                                        scalar2=None, op0=ALU.add)
                                return r
                            bo_w = war.get(("m2bout", b, tt % 2))
                            P.on("vector", evd,
                                 waits=[(sT, t_mm)] + ([bo_w] if bo_w else []),
                                 inc=(sV, 1))
                            war[("m2bank", b)] = (sV, t_ev)
                            t_evs.append(t_ev)
                        Bt = [Bo[:, m * 512:(m + 1) * 512] for m in range(NCH)]
                        rg = bd.run_tt(b, tt, Bt, (sV, t_evs[-1]), resid_in,
                                       res_out, ln_out,
                                       affine=(vec["ln2s"], vec["ln2b"]),
                                       final=False, eps_pre=GEPS)
                        war[("m2bout", b, tt % 2)] = rg
                    K.td.flush()
                    P.barrier()
    P.barrier()


def phase_l1attn(K, resid_in, xln_in, res_out, ln_out):
    P = K.P
    g = K.g
    c = K.c
    sW = P.sem("gpsimd_dma")
    sA = P.sem("sync_dma")
    sT = P.sem("t")
    sV = P.sem("v")
    sS = P.sem("s")
    sPo = P.sem("p")
    war = {}

    with contextlib.ExitStack() as pstk:
        rc = {}
        for nm, shape, dt in (("c_iota_row", [1, S], F32),
                              ("c_ropecos", [HD, S], F32),
                              ("c_ropesin", [HD, S], F32),
                              ("c_rope_perm", [HD, HD], BF16),
                              ("c_eye4", [1, 12], BF16)):
            rc[nm] = K.sb(pstk, "l1" + nm, shape, dt)
            P.dma("sync", rc[nm][:], g[nm])
        vec = {}
        for nm in ("bo", "gamma", "beta"):
            vec[nm] = K.sb(pstk, f"l1a_{nm}v", [128, NCH], F32)
            P.dma("sync", vec[nm][:], g[f"l1a_{nm}"])
        bdrow = K.sb(pstk, "l1a_bdrow", [1, H], F32)
        P.dma("sync", bdrow[:], g["l1a_bd_row"])
        wdsl = K.sb(pstk, "l1a_wdsl", [128, NCH * H], F32)
        P.dma("sync", wdsl[:].rearrange("p (k j) -> p k j", k=NCH),
              g["l1a_wdt"].rearrange("(k p) j -> p k j", p=128))
        P.barrier()

        for b in range(NB):
            tok0 = b * S
            with contextlib.ExitStack() as bstk:
                zb = K.sb(bstk, f"l1a_z{b}", [128, NCH * S], BF16)
                with contextlib.ExitStack() as stk:
                    ps = K.ps8(stk, f"l1h{b}")
                    xlnb = K.sb(stk, f"l1a_xln{b}", [128, NCH * S], BF16)
                    ATj = K.sb(stk, f"l1a_at{b}", [128, 16 * 512], BF16)
                    Qaug = K.sb(stk, f"l1a_qa{b}", [68, S], BF16)
                    Kaug = K.sb(stk, f"l1a_ka{b}", [68, S], BF16)
                    Tt1 = K.sb(stk, f"l1a_t1{b}", [64, S], F32)
                    Tt2 = K.sb(stk, f"l1a_t2{b}", [64, S], F32)
                    SqB = K.sb(stk, f"l1a_sqb{b}", [64, S], BF16)
                    gsil = K.sb(stk, f"l1a_gs{b}", [64, S], BF16)
                    ogh = K.sb(stk, f"l1a_og{b}", [64, S], F32)
                    Vsb = K.sb(stk, f"l1a_v{b}", [128, 16 * 65], BF16)
                    whd = {nm: K.sb(stk, f"l1a_w{nm}{b}", [128, NCH * HD], BF16)
                           for nm in ("q", "k", "v", "g")}
                    bh = {nm: K.sb(stk, f"l1a_b{nm}{b}", [64, 1], F32)
                          for nm in ("q", "k", "v", "g")}
                    VRf = K.sb(stk, f"l1a_vrf{b}", [1, HD], F32)
                    VRb = K.sb(stk, f"l1a_vrb{b}", [1, HD], BF16)
                    mrow = K.sb(stk, f"l1a_mq{b}", [1, S], F32)
                    srow = K.sb(stk, f"l1a_sr{b}", [1, S], F32)
                    Rb0 = K.sb(stk, f"l1a_rb0{b}", [1, S], BF16)
                    Rb1 = K.sb(stk, f"l1a_rb1{b}", [1, S], BF16)
                    Rb2 = K.sb(stk, f"l1a_rb2{b}", [1, S], BF16)
                    dup = K.sb(stk, f"l1a_du{b}", [128, 128], BF16)
                    dlo = K.sb(stk, f"l1a_dl{b}", [128, 128], BF16)
                    Bh = K.sb(stk, f"l1a_bh{b}", [128, 16], F32)
                    ldc = K.sb(stk, f"l1a_ldc{b}", [128, 1], F32)
                    sm = K.sb(stk, f"l1a_sm{b}", [128, 8], F32)
                    smr = K.sb(stk, f"l1a_smr{b}", [1, 576], F32)
                    mb2 = K.sb(stk, f"l1a_mb2{b}", [64, 4], F32)
                    gbh = K.sb(stk, f"l1a_gbh{b}", [64, 2], F32)

                    def dve(fn, waits=()):
                        t = sV.bump(1)
                        P.on("vector", fn, waits=waits, inc=(sV, 1))
                        return t

                    def act(fn, waits=()):
                        t = sS.bump(1)
                        P.on("scalar", fn, waits=waits, inc=(sS, 1))
                        return t

                    def ten(fn, waits=()):
                        t = sT.bump(1)
                        P.on("tensor", fn, waits=waits, inc=(sT, 1))
                        return t

                    for m in range(8):
                        xw = P.dma("sync", xlnb[:, m * S:(m + 1) * S],
                                   g[xln_in][m, :, tok0:tok0 + S],
                                   waits=[(sT, sT.n)], slot="lx")
                    kgw = P.dma("sync", Kaug[64:68, :], g["c_kaugrows"],
                                waits=[(sT, sT.n)], slot="lk")
                    t_ones = sPo.bump(1)
                    P.on("gpsimd", lambda e: e.memset(Vsb[:], 1.0),
                         waits=[(sT, sT.n)], inc=(sPo, 1))

                    def xmean(e):
                        r = None
                        for m in range(8):
                            r = e.tensor_reduce(out=sm[:, m:m + 1],
                                                in_=xlnb[:, m * S:(m + 1) * S],
                                                axis=AX.X, op=ALU.add)
                            r = e.drain()
                        return r
                    t_xm = dve(xmean, waits=[xw])
                    def pdmm(e):
                        r = None
                        for m in range(8):
                            r = e.matmul(ps[0][0:1, 0:H], sm[:, m:m + 1],
                                         wdsl[:, m * H:(m + 1) * H],
                                         start=(m == 0), stop=(m == 7))
                        return r
                    t_pd1 = ten(pdmm, waits=[(sV, t_xm)])
                    ldr = smr[0:1, 0:H]
                    lds = smr[0:1, H:2 * H]
                    t_pd2 = dve(lambda e: (
                        e.tensor_scalar(out=ldr, in0=ps[0][0:1, 0:H],
                                        scalar1=1.0 / S, scalar2=None, op0=ALU.mult),
                        e.drain(),
                        e.tensor_tensor(out=ldr, in0=ldr, in1=bdrow[:], op=ALU.add),
                        e.drain(),
                    )[-1], waits=[(sT, t_pd1)])
                    t_pd3 = act(lambda e: (
                        e.activation(out=ldr, in_=ldr, func=AF.Sigmoid),
                        e.drain(),
                        e.activation(out=lds, in_=ldr, func=AF.Exp,
                                     scale=float(-8.0 * math.log(2.0)),
                                     bias=c["c_pool"][0:1, 2:3]),
                        e.drain(),
                    )[-1], waits=[(sV, t_pd2)])
                    t_pd4 = dve(lambda e: (
                        e.tensor_scalar(out=lds, in0=lds, scalar1=-1.0,
                                        scalar2=1.0, op0=ALU.mult, op1=ALU.add),
                        e.drain(),
                    )[-1], waits=[(sS, t_pd3)])
                    t_pd5 = act(lambda e: (
                        e.activation(out=ldr, in_=lds, func=AF.Ln),
                        e.drain(),
                    )[-1], waits=[(sV, t_pd4)])

                    for h in range(H):
                        ch_h = h // 2
                        pb_h = (h % 2) * 64
                        hw = {}
                        for nm in ("q", "k", "v", "g"):
                            wh_key = ("whd", b, nm)
                            hw[nm] = P.dma("gpsimd",
                                  whd[nm][:].rearrange("p (k j) -> p k j", k=NCH),
                                  g[f"l1a_w{nm}h"][h]
                                  .rearrange("(k p) j -> p k j", p=128),
                                  waits=[war[wh_key]] if wh_key in war else (),
                                  slot=f"lw{nm}")
                            bhw = P.dma("sync", bh[nm][:],
                                  g[f"l1a_b{nm}h"][h].rearrange("(j o) -> j o", o=1),
                                  waits=[(sS, sS.n)], slot="lb")
                        bhw = P.dma("sync", VRf[:],
                              g["l1a_bvh"][h].rearrange("(o j) -> o j", o=1),
                              waits=[(sPo, sPo.n)], slot="lb")
                        P.dma("sync", gbh[:, 0:1],
                              g["l1a_gamma_t"][ch_h, pb_h:pb_h + 64]
                              .rearrange("(j o) -> j o", o=1),
                              waits=[(sV, sV.n)], slot="lb")
                        gbw = P.dma("sync", gbh[:, 1:2],
                              g["l1a_beta_t"][ch_h, pb_h:pb_h + 64]
                              .rearrange("(j o) -> j o", o=1), slot="lb")
                        t_vr = sPo.bump(1)
                        P.on("gpsimd", lambda e: e.tensor_copy(out=VRb[:], in_=VRf[:]),
                             waits=[bhw], inc=(sPo, 1))
                        t_ld1 = ten(lambda e: e.matmul(
                            ps[6][:, 0:1], c["c_ones1x128_f32"][:],
                            smr[0:1, 0:1] if False else ldr[0:1, h:h + 1],
                            start=True, stop=True),
                            waits=[(sS, t_pd5), (sV, sV.n)])
                        t_ld2 = dve(lambda e: (
                            e.tensor_copy(out=ldc[:], in_=ps[6][:, 0:1]),
                            e.drain(),
                            e.tensor_scalar(out=Bh[:], in0=c["c_iota2d"][:],
                                            scalar1=ldc[:], scalar2=None,
                                            op0=ALU.mult),
                            e.drain(),
                        )[-1], waits=[(sT, t_ld1)])
                        for tt in range(4):
                            t_mm = ten(lambda e, tt=tt: [
                                e.matmul(ps[4 + (tt % 2)][0:64, :],
                                         whd["g"][:, k * HD:(k + 1) * HD],
                                         xlnb[:, k * S + tt * 512:k * S + (tt + 1) * 512],
                                         start=(k == 0), stop=(k == NCH - 1))
                                for k in range(NCH)][-1],
                                waits=[hw["g"], xw, gbw,
                                       (sS, sS.n), (sV, sV.n)])
                            war[("whd", b, "g")] = (sT, t_mm)
                            act(lambda e, tt=tt: e.activation(
                                out=gsil[:, tt * 512:(tt + 1) * 512],
                                in_=ps[4 + (tt % 2)][0:64, :], func=AF.Silu,
                                bias=bh["g"][:]),
                                waits=[(sT, t_mm), gbw])
                        for nm, aug in (("q", Qaug), ("k", Kaug)):
                            for tt in range(4):
                                t_qk = ten(lambda e, nm=nm, tt=tt: [
                                    e.matmul(ps[tt][0:64, :],
                                             whd[nm][:, k * HD:(k + 1) * HD],
                                             xlnb[:, k * S + tt * 512:
                                                  k * S + (tt + 1) * 512],
                                             start=(k == 0), stop=(k == NCH - 1))
                                    for k in range(NCH)][-1],
                                    waits=[hw[nm], xw,
                                           (sS, sS.n), (sV, sV.n)])
                                war[("whd", b, nm)] = (sT, t_qk)
                            t_ev = act(lambda e, nm=nm, aug=aug: [
                                e.activation(out=aug[0:64, tt * 512:(tt + 1) * 512],
                                             in_=ps[tt][0:64, :], func=AF.Identity,
                                             bias=bh[nm][:])
                                for tt in range(4)][-1],
                                waits=[(sT, sT.n), gbw])
                            t_rt = ten(lambda e, aug=aug: [
                                e.matmul(ps[4 + tt][0:64, :], rc["c_rope_perm"][:],
                                         aug[0:64, tt * 512:(tt + 1) * 512],
                                         start=True, stop=True)
                                for tt in range(4)][-1],
                                waits=[(sS, t_ev), (sV, sV.n)])
                            dve(lambda e, aug=aug: (
                                e.tensor_tensor(out=Tt1[:], in0=aug[0:64, :],
                                                in1=rc["c_ropecos"][:], op=ALU.mult),
                            )[-1], waits=[(sS, t_ev)])
                            dve(lambda e, aug=aug: ([
                                e.tensor_tensor(
                                    out=Tt2[:, tt * 512:(tt + 1) * 512],
                                    in0=ps[4 + tt][0:64, :],
                                    in1=rc["c_ropesin"][:, tt * 512:(tt + 1) * 512],
                                    op=ALU.mult)
                                for tt in range(4)],
                                e.tensor_tensor(out=aug[0:64, :], in0=Tt1[:],
                                                in1=Tt2[:], op=ALU.add),
                            )[-1], waits=[(sT, t_rt)])
                        for tc in range(16):
                            bank = 4 + (tc % 4)
                            t_mm = ten(lambda e, tc=tc, bank=bank: (
                                [e.matmul(ps[bank][:, 0:HD],
                                          xlnb[:, k * S + tc * 128:k * S + tc * 128 + 128],
                                          whd["v"][:, k * HD:(k + 1) * HD],
                                          start=(k == 0), stop=False)
                                 for k in range(NCH)],
                                e.matmul(ps[bank][:, 0:HD], c["c_ones1x128"][:],
                                         VRb[:], start=False, stop=True),
                            )[-1], waits=[hw["v"], (sPo, t_vr), (sV, sV.n)])
                            war[("whd", b, "v")] = (sT, t_mm)
                            dve(lambda e, tc=tc, bank=bank: e.tensor_copy(
                                out=Vsb[:, tc * 65:tc * 65 + HD],
                                in_=ps[bank][:, 0:HD]),
                                waits=[(sT, t_mm)])
                        t_sq = act(lambda e: (
                            e.activation(out=Tt1[:], in_=Qaug[0:64, :],
                                         func=AF.Square),
                        )[-1], waits=[(sV, sV.n)])
                        t_cb = sPo.bump(1)
                        P.on("gpsimd", lambda e: e.tensor_copy(out=SqB[:], in_=Tt1[:]),
                             waits=[(sS, t_sq)], inc=(sPo, 1))
                        t_qn = ten(lambda e: [
                            e.matmul(ps[tt][0:1, :], c["c_onescol"][0:64, :],
                                     SqB[:, tt * 512:(tt + 1) * 512],
                                     start=True, stop=True)
                            for tt in range(4)][-1],
                            waits=[(sPo, t_cb)])
                        t_qs = act(lambda e: [
                            e.activation(out=mrow[0:1, tt * 512:(tt + 1) * 512],
                                         in_=ps[tt][0:1, :], func=AF.Sqrt)
                            for tt in range(4)][-1],
                            waits=[(sT, t_qn)])
                        t_sq2 = act(lambda e: (
                            e.activation(out=Tt1[:], in_=Kaug[0:64, :],
                                         func=AF.Square),
                        )[-1])
                        t_cb2 = sPo.bump(1)
                        P.on("gpsimd", lambda e: e.tensor_copy(out=SqB[:], in_=Tt1[:]),
                             waits=[(sS, t_sq2)], inc=(sPo, 1))
                        t_kn = ten(lambda e: [
                            e.matmul(ps[4 + tt][0:1, :], c["c_onescol"][0:64, :],
                                     SqB[:, tt * 512:(tt + 1) * 512],
                                     start=True, stop=True)
                            for tt in range(4)][-1],
                            waits=[(sPo, t_cb2)])
                        t_kmax = dve(lambda e: ([
                            e.tensor_copy(out=srow[0:1, tt * 512:(tt + 1) * 512],
                                          in_=ps[4 + tt][0:1, :])
                            for tt in range(4)],
                            e.drain(),
                            e.tensor_reduce(out=smr[0:1, 32:33], in_=srow[:],
                                            axis=AX.X, op=ALU.max),
                            e.drain(),
                        )[-1], waits=[(sT, t_kn)])
                        t_ks = act(lambda e: (
                            e.activation(out=smr[0:1, 32:33], in_=smr[0:1, 32:33],
                                         func=AF.Sqrt),
                            e.drain(),
                        )[-1], waits=[(sV, t_kmax)])
                        t_rows = dve(lambda e: (
                            e.tensor_scalar(out=mrow[:], in0=mrow[:],
                                            scalar1=smr[0:1, 32:33], scalar2=-1.0,
                                            op0=ALU.mult, op1=ALU.mult),
                            e.tensor_copy(out=Rb0[:], in_=mrow[:]),
                            e.tensor_scalar(out=srow[:], in0=rc["c_iota_row"][:],
                                            scalar1=ldc[0:1, 0:1], scalar2=-8.0,
                                            op0=ALU.mult, op1=ALU.mult),
                            e.tensor_copy(out=Rb1[:], in_=srow[:]),
                            e.drain(),
                            e.tensor_tensor(out=mrow[:], in0=srow[:], in1=Rb1[:],
                                            op=ALU.subtract),
                            e.tensor_copy(out=Rb2[:], in_=mrow[:]),
                            e.drain(),
                        )[-1], waits=[(sS, t_ks), (sV, t_ld2), kgw])
                        t_aug = None
                        for jj in range(4):
                            t_stk = ten(lambda e, jj=jj: (
                                e.matmul(ps[7][0:4, :], rc["c_eye4"][0:1, 0:4],
                                         Rb0[0:1, jj * 512:(jj + 1) * 512],
                                         start=True, stop=False),
                                e.matmul(ps[7][0:4, :], rc["c_eye4"][0:1, 4:8],
                                         Rb1[0:1, jj * 512:(jj + 1) * 512],
                                         start=False, stop=False),
                                e.matmul(ps[7][0:4, :], rc["c_eye4"][0:1, 8:12],
                                         Rb2[0:1, jj * 512:(jj + 1) * 512],
                                         start=False, stop=True),
                            )[-1], waits=[(sV, t_aug if t_aug else t_rows)])
                            t_aug = dve(lambda e, jj=jj: e.tensor_copy(
                                out=Qaug[64:68, jj * 512:(jj + 1) * 512],
                                in_=ps[7][0:4, :]), waits=[(sT, t_stk)])
                        for j in range(4):
                            for cc in range(16):
                                bank = (0, 1, 2, 6, 7)[cc % 5]
                                if cc < 4 * j:
                                    t_mm = ten(lambda e, cc=cc, j=j, bank=bank:
                                        e.matmul(ps[bank][:],
                                                 Kaug[0:65, cc * 128:(cc + 1) * 128],
                                                 Qaug[0:65, j * 512:(j + 1) * 512],
                                                 start=True, stop=True),
                                        waits=[(sV, t_aug), (sS, sS.n)])
                                    act(lambda e, cc=cc, bank=bank: e.activation(
                                        out=ATj[:, cc * 512:(cc + 1) * 512],
                                        in_=ps[bank][:], func=AF.Exp, scale=0.125),
                                        waits=[(sT, t_mm)])
                                elif cc >= 4 * (j + 1):
                                    t_mm = ten(lambda e, cc=cc, j=j, bank=bank:
                                        e.matmul(ps[bank][:],
                                                 Kaug[0:68, cc * 128:(cc + 1) * 128],
                                                 Qaug[0:68, j * 512:(j + 1) * 512],
                                                 start=True, stop=True),
                                        waits=[(sV, t_aug), (sS, sS.n)])
                                    act(lambda e, cc=cc, bank=bank: e.activation(
                                        out=ATj[:, cc * 512:(cc + 1) * 512],
                                        in_=ps[bank][:], func=AF.Exp, scale=0.125,
                                        bias=Bh[:, cc:cc + 1]),
                                        waits=[(sT, t_mm)])
                                else:
                                    for ssub in range(4):
                                        jq = 4 * j + ssub
                                        q0 = j * 512 + ssub * 128
                                        if jq < cc:
                                            t_mm = ten(lambda e, cc=cc, q0=q0, bank=bank:
                                                e.matmul(ps[bank][:, 0:128],
                                                         Kaug[0:68, cc * 128:(cc + 1) * 128],
                                                         Qaug[0:68, q0:q0 + 128],
                                                         start=True, stop=True),
                                                waits=[(sV, t_aug), (sS, sS.n)])
                                            act(lambda e, cc=cc, q0=q0, bank=bank, j=j:
                                                e.activation(
                                                    out=ATj[:, cc * 512 + (q0 - j * 512):
                                                            cc * 512 + (q0 - j * 512) + 128],
                                                    in_=ps[bank][:, 0:128],
                                                    func=AF.Exp, scale=0.125,
                                                    bias=Bh[:, cc:cc + 1]),
                                                waits=[(sT, t_mm)])
                                        elif jq > cc:
                                            t_mm = ten(lambda e, cc=cc, q0=q0, bank=bank:
                                                e.matmul(ps[bank][:, 0:128],
                                                         Kaug[0:65, cc * 128:(cc + 1) * 128],
                                                         Qaug[0:65, q0:q0 + 128],
                                                         start=True, stop=True),
                                                waits=[(sV, t_aug), (sS, sS.n)])
                                            act(lambda e, cc=cc, q0=q0, bank=bank, j=j:
                                                e.activation(
                                                    out=ATj[:, cc * 512 + (q0 - j * 512):
                                                            cc * 512 + (q0 - j * 512) + 128],
                                                    in_=ps[bank][:, 0:128],
                                                    func=AF.Exp, scale=0.125),
                                                waits=[(sT, t_mm)])
                                        else:
                                            t_mm = ten(lambda e, cc=cc, q0=q0:
                                                (e.matmul(ps[3][:, 0:128],
                                                          Kaug[0:68, cc * 128:(cc + 1) * 128],
                                                          Qaug[0:68, q0:q0 + 128],
                                                          start=True, stop=True),
                                                 e.matmul(ps[3][:, 128:256],
                                                          Kaug[0:65, cc * 128:(cc + 1) * 128],
                                                          Qaug[0:65, q0:q0 + 128],
                                                          start=True, stop=True),
                                                 )[-1],
                                                waits=[(sV, sV.n), (sS, sS.n)])
                                            t_ex = act(lambda e, cc=cc: (
                                                e.activation(out=dup[:],
                                                             in_=ps[3][:, 0:128],
                                                             func=AF.Exp, scale=0.125,
                                                             bias=Bh[:, cc:cc + 1]),
                                                e.activation(out=dlo[:],
                                                             in_=ps[3][:, 128:256],
                                                             func=AF.Exp, scale=0.125),
                                            )[-1], waits=[(sT, t_mm)])
                                            dve(lambda e, cc=cc, q0=q0, j=j:
                                                e.select(
                                                    out=ATj[:, cc * 512 + (q0 - j * 512):
                                                            cc * 512 + (q0 - j * 512) + 128],
                                                    mask=c["c_diagmask"][:],
                                                    on_true=dup[:], on_false=dlo[:]),
                                                waits=[(sS, t_ex)])
                            t_av = ten(lambda e, j=j: [
                                e.matmul(ps[4][0:65, :], Vsb[:, tc * 65:(tc + 1) * 65],
                                         ATj[:, tc * 512:(tc + 1) * 512],
                                         start=(tc == 0), stop=(tc == 15))
                                for tc in range(16)][-1],
                                waits=[(sS, sS.n), (sV, sV.n)])
                            nr = smr[0:1, 64:576]
                            t_nr = act(lambda e: (
                                e.activation(out=nr, in_=ps[4][64:65, :], func=AF.Ln),
                                e.activation(out=nr, in_=nr, func=AF.Exp, scale=-1.0),
                            )[-1], waits=[(sT, t_av)])
                            t_nb = ten(lambda e: e.matmul(
                                ps[5][0:64, :], c["c_ones1x128_f32"][0:1, 0:64], nr,
                                start=True, stop=True),
                                waits=[(sS, t_nr)])
                            dve(lambda e, j=j: (
                                e.tensor_copy(out=Tt1[:, 0:512], in_=ps[4][0:64, :]),
                                e.tensor_tensor(out=ogh[:, j * 512:(j + 1) * 512],
                                                in0=Tt1[:, 0:512], in1=ps[5][0:64, :],
                                                op=ALU.mult),
                            )[-1], waits=[(sT, t_nb)])
                        t_st = dve(lambda e: (
                            e.tensor_reduce(out=sm[0:64, 0:1], in_=ogh[:],
                                            axis=AX.X, op=ALU.add),
                            e.drain(),
                        )[-1])
                        t_st2 = act(lambda e: (
                            e.activation(out=Tt1[:], in_=ogh[:], func=AF.Square,
                                         accum_out=sm[0:64, 1:2]),
                            e.drain(),
                        )[-1], waits=[(sV, t_st)])
                        t_st3 = ten(lambda e: e.matmul(
                            ps[6][0:1, 0:2], c["c_onescol_f32"][0:64, :],
                            sm[0:64, 0:2], start=True, stop=True),
                            waits=[(sS, t_st2)])
                        t_st4 = dve(lambda e: (
                            e.tensor_scalar(out=smr[0:1, 34:36], in0=ps[6][0:1, 0:2],
                                            scalar1=1.0 / (S * HD), scalar2=None,
                                            op0=ALU.mult),
                            e.drain(),
                            e.tensor_tensor(out=smr[0:1, 36:37], in0=smr[0:1, 34:35],
                                            in1=smr[0:1, 34:35], op=ALU.mult),
                            e.drain(),
                            e.tensor_tensor(out=smr[0:1, 35:36], in0=smr[0:1, 35:36],
                                            in1=smr[0:1, 36:37], op=ALU.subtract),
                            e.drain(),
                            e.tensor_scalar(out=smr[0:1, 35:36], in0=smr[0:1, 35:36],
                                            scalar1=1e-5, scalar2=None, op0=ALU.add),
                            e.drain(),
                        )[-1], waits=[(sT, t_st3)])
                        t_st5 = act(lambda e: (
                            e.activation(out=smr[0:1, 35:36], in_=smr[0:1, 35:36],
                                         func=AF.Ln),
                            e.drain(),
                            e.activation(out=smr[0:1, 35:36], in_=smr[0:1, 35:36],
                                         func=AF.Exp, scale=-0.5),
                            e.drain(),
                        )[-1], waits=[(sV, t_st4)])
                        t_bc2 = ten(lambda e: e.matmul(
                            ps[7][0:64, 0:2], c["c_ones1x128_f32"][0:1, 0:64],
                            smr[0:1, 34:36], start=True, stop=True),
                            waits=[(sS, t_st5)])
                        t_ab2 = dve(lambda e, ch_h=ch_h, pb_h=pb_h: (
                            e.tensor_copy(out=mb2[:, 0:2], in_=ps[7][0:64, 0:2]),
                            e.drain(),
                            e.tensor_tensor(out=mb2[:, 2:3], in0=mb2[:, 1:2],
                                            in1=gbh[:, 0:1], op=ALU.mult),
                            e.drain(),
                            e.tensor_tensor(out=mb2[:, 3:4], in0=mb2[:, 0:1],
                                            in1=mb2[:, 2:3], op=ALU.mult),
                            e.drain(),
                            e.scalar_tensor_tensor(out=mb2[:, 3:4], in0=mb2[:, 3:4],
                                                   scalar=-1.0,
                                                   in1=gbh[:, 1:2],
                                                   op0=ALU.mult, op1=ALU.add),
                            e.drain(),
                            e.tensor_scalar(out=Tt1[:], in0=ogh[:],
                                            scalar1=mb2[:, 2:3], scalar2=mb2[:, 3:4],
                                            op0=ALU.mult, op1=ALU.add),
                            e.tensor_tensor(
                                out=zb[pb_h:pb_h + 64, ch_h * S:(ch_h + 1) * S],
                                in0=Tt1[:], in1=gsil[:], op=ALU.mult),
                        )[-1], waits=[(sT, t_bc2), gbw])
                P.barrier()
                with contextlib.ExitStack() as stk:
                    ps = K.ps8(stk, f"l1wo{b}")
                    wsl = [K.sb(stk, f"l1a_wo{i}{b}", [128, NCH * 128], BF16)
                           for i in range(8)]
                    Bout = [K.sb(stk, f"l1a_bo2{i}{b}", [128, 8 * 512], F32)
                            for i in range(2)]
                    bd = Boundary(K, stk, f"l1a_bd{b}", ps[4:8])
                    for mo in range(8):
                        wow = P.dma("gpsimd",
                              wsl[mo][:].rearrange("p (k j) -> p k j", k=NCH),
                              g["l1a_wot"][:, mo * 128:(mo + 1) * 128]
                              .rearrange("(k p) j -> p k j", p=128), slot="lo")
                    for tt in range(4):
                        Bo = Bout[tt % 2]
                        t_evs = []
                        for wave in range(2):
                            K.td.flush_one()
                            K.td.flush_one()
                            def mmo(e, tt=tt, wave=wave):
                                r = None
                                for mi in range(4):
                                    mo = wave * 4 + mi
                                    for k in range(NCH):
                                        r = e.matmul(
                                            ps[mi][:],
                                            wsl[mo][:, k * 128:(k + 1) * 128],
                                            zb[:, k * S + tt * 512:
                                               k * S + (tt + 1) * 512],
                                            start=(k == 0), stop=(k == NCH - 1))
                                return r
                            bw = war.get(("wobank", b))
                            t_mm = sT.bump(1)
                            P.on("tensor", mmo,
                                 waits=[wow] + ([bw] if bw else []),
                                 inc=(sT, 1))
                            t_ev = sV.bump(1)
                            def evo(e, wave=wave, Bo=Bo):
                                r = None
                                for mi in range(4):
                                    m = wave * 4 + mi
                                    r = e.tensor_scalar(
                                        out=Bo[:, m * 512:(m + 1) * 512],
                                        in0=ps[mi][:], scalar1=vec["bo"][:, m:m + 1],
                                        scalar2=None, op0=ALU.add)
                                return r
                            bo_w = war.get(("wobout", b, tt % 2))
                            P.on("vector", evo,
                                 waits=[(sT, t_mm)] + ([bo_w] if bo_w else []),
                                 inc=(sV, 1))
                            war[("wobank", b)] = (sV, t_ev)
                            t_evs.append(t_ev)
                        Bt = [Bo[:, m * 512:(m + 1) * 512] for m in range(NCH)]
                        rg = bd.run_tt(b, tt, Bt, (sV, t_evs[-1]), resid_in,
                                       res_out, ln_out, affine=None, final=False)
                        war[("wobout", b, tt % 2)] = rg
                    K.td.flush()
                    P.barrier()
    P.barrier()



def build_program(phases, stage_ext=None):
    nc = bass.Bass("TRN2", target_bir_lowering=False)
    stack = contextlib.ExitStack()
    P = Prog(nc, stack)
    K = Kern(nc, P, stack)
    K.td = TDefer(P)
    declare_io(K, stage_ext)
    with contextlib.ExitStack() as cstk:
        load_consts(K, cstk)
        for ph in phases:
            if ph == "ln0":
                phase_ln0(K)
            elif ph == "l0a":
                phase_l0attn(K, "x0", "xln", "xres", "xln")
            elif ph == "l0f":
                phase_ffn(K, "l0f", "xres", "xln", "xres2", "xln2")
            elif ph == "l1a":
                phase_l1attn(K, "xres2", "xln2", "xres", "xln")
            elif ph == "l1f":
                phase_ffn(K, "l1f", "xres", "xln", None, None, final=True)
            else:
                raise ValueError(ph)
        P.finalize()
    stack.close()
    return nc


ALL_PHASES = ["ln0", "l0a", "l0f", "l1a", "l1f"]


def kernel(**inputs):
    maps = prep_host(inputs)
    nc = build_program(ALL_PHASES)
    res = run_bass_kernel_spmd(nc, maps, core_ids=list(range(8)))
    out = np.empty((16, S, D), np.float32)
    for cid in range(8):
        o = np.asarray(res.results[cid]["out"], np.float32)
        out[2 * cid:2 * cid + 2] = o.reshape(D, NB, S).transpose(1, 2, 0)
    return out


# version 06
# speedup vs baseline: 1.0119x; 1.0020x over previous
import math
import contextlib
import numpy as np
import ml_dtypes

import concourse.bass as bass
import concourse.mybir as mybir
from concourse.bass_utils import run_bass_kernel_spmd

F32 = mybir.dt.float32
BF16 = mybir.dt.bfloat16
U8 = mybir.dt.uint8
AF = mybir.ActivationFunctionType
ALU = mybir.AluOpType
AX = mybir.AxisListType
NPBF16 = ml_dtypes.bfloat16

D = 1024
S = 2048
NB = 2
TOK = NB * S
NCH = D // 128
F = 4096
FCH = F // 128
H = 16
HD = 64
NT = S // 512
MAGIC = 12582912.0
INV2PI = float(1.0 / (2 * np.pi))
TWOPI = float(2 * np.pi)
PI = float(np.pi)
SIGC = 1.5957691216057308

ENGINES = ("sync", "gpsimd", "vector", "scalar", "tensor")
FFN_DEBUG = 0
FFN_SERIAL = 0


class Sem:
    __slots__ = ("h", "n", "name")
    def __init__(self, h, name):
        self.h = h; self.n = 0; self.name = name
    def bump(self, k):
        self.n += k
        return self.n


class Prog:
    def __init__(self, nc, stack):
        self.nc = nc
        self.stack = stack
        self.q = {e: [] for e in ENGINES}
        self._sems = {}

    def sem(self, name):
        if name not in self._sems:
            h = self.stack.enter_context(self.nc.semaphore(name))
            self._sems[name] = Sem(h, name)
        return self._sems[name]

    def on(self, eng, fn, waits=(), inc=None):
        def thunk(e, fn=fn, waits=tuple(waits), inc=inc):
            for s, v in waits:
                e.wait_ge(s.h, v)
            r = fn(e)
            if inc is not None:
                s, a = inc
                r.then_inc(s.h, a)
        self.q[eng].append(thunk)

    def dma(self, eng, out, in_, waits=(), slot=None):
        nm = eng + "_dma" if slot is None else f"{eng}_dma@{slot}"
        s = self.sem(nm)
        tgt = s.bump(16)
        self.on(eng, lambda e: e.dma_start(out=out, in_=in_), waits, (s, 16))
        return s, tgt

    def barrier(self):
        b = self.sem("bar")
        dma_waits = [(s, s.n) for nm, s in list(self._sems.items())
                     if "_dma" in nm]
        tgt = b.n + len(ENGINES)
        for eng in ENGINES:
            def thunk(e, dma_waits=tuple(dma_waits), b=b, tgt=tgt):
                for s, v in dma_waits:
                    e.wait_ge(s.h, v)
                e.sem_inc(b.h, 1)
                e.wait_ge(b.h, tgt)
            self.q[eng].append(thunk)
        b.n = tgt

    def finalize(self):
        with self.nc.Block() as blk:
            @blk.sync
            def _(e):
                for t in self.q["sync"]:
                    t(e)
            @blk.gpsimd
            def _(e):
                for t in self.q["gpsimd"]:
                    t(e)
            @blk.vector
            def _(e):
                for t in self.q["vector"]:
                    t(e)
            @blk.scalar
            def _(e):
                for t in self.q["scalar"]:
                    t(e)
            @blk.tensor
            def _(e):
                for t in self.q["tensor"]:
                    t(e)



def _pp(vec):
    return np.ascontiguousarray(np.asarray(vec, np.float32).reshape(-1, 128).T)


def prep_host(inputs):
    f32 = lambda a: np.ascontiguousarray(np.asarray(a, np.float32))
    sh = {}
    sh["l0a_wq"] = f32(inputs["L0_attn_Wq"])
    sh["l0a_wk"] = f32(inputs["L0_attn_Wk"])
    sh["l0a_wv"] = f32(inputs["L0_attn_Wv"])
    sh["l0a_wg"] = f32(inputs["L0_attn_Wg"])
    sh["l0a_wa"] = f32(inputs["L0_attn_Wa"])
    sh["l0a_wo"] = f32(inputs["L0_attn_Wo"])
    w1 = f32(inputs["L0_attn_W1"])
    w2 = f32(inputs["L0_attn_W2"])
    sh["l0a_w1m"] = f32(w1.reshape(D, FCH, 128).transpose(1, 0, 2))
    sh["l0a_w2m"] = f32(w2.reshape(F, NCH, 128).transpose(1, 0, 2))
    for nm, src in (("gns", "gn_scale"), ("gnb", "gn_bias"), ("ln1s", "ln1_scale"),
                    ("ln1b", "ln1_bias"), ("ln2s", "ln2_scale"), ("ln2b", "ln2_bias"),
                    ("b2", "b2")):
        sh[f"l0a_{nm}"] = _pp(inputs[f"L0_attn_{src}"])
    sh["l0a_b1"] = _pp(inputs["L0_attn_b1"])
    for pre, src in (("l0f", "L0_ffn"), ("l1f", "L1_ffn")):
        w1t = f32(np.asarray(inputs[f"{src}_w1_w"]).T)
        w2t = f32(np.asarray(inputs[f"{src}_w2_w"]).T)
        h1t = f32(np.asarray(inputs[f"{src}_h1_w"]).T)
        h2t = f32(np.asarray(inputs[f"{src}_h2_w"]).T)
        w3t = f32(np.asarray(inputs[f"{src}_w3_w"]).T)
        sh[f"{pre}_w1m"] = f32(w1t.reshape(D, FCH, 128).transpose(1, 0, 2))
        sh[f"{pre}_w2m"] = f32(w2t.reshape(D, FCH, 128).transpose(1, 0, 2))
        sh[f"{pre}_h1m"] = f32(h1t.reshape(F, FCH, 128).transpose(1, 0, 2))
        sh[f"{pre}_h2m"] = f32(h2t.reshape(F, FCH, 128).transpose(1, 0, 2))
        sh[f"{pre}_w3m"] = f32(w3t.reshape(F, NCH, 128).transpose(1, 0, 2))
        sh[f"{pre}_s1"] = _pp(inputs[f"{src}_w1_s"])
        sh[f"{pre}_s2"] = _pp(inputs[f"{src}_w2_s"])
        sh[f"{pre}_sh1"] = _pp(inputs[f"{src}_h1_s"])
        sh[f"{pre}_sh2"] = _pp(inputs[f"{src}_h2_s"])
        sh[f"{pre}_s3"] = _pp(inputs[f"{src}_w3_s"])
    for nm in ("wq", "wk", "wv", "wg"):
        wt = f32(np.asarray(inputs[f"L1_attn_{nm}"]).T)
        sh[f"l1a_{nm}h"] = f32(wt.reshape(D, H, HD).transpose(1, 0, 2))
        b = np.asarray(inputs[f"L1_attn_b{nm[1]}"], np.float32)
        sh[f"l1a_b{nm[1]}h"] = f32(b.reshape(H, HD))
    sh["l1a_wot"] = f32(np.asarray(inputs["L1_attn_wo"]).T)
    sh["l1a_wdt"] = f32(np.asarray(inputs["L1_attn_wd"]).T)
    sh["l1a_bo"] = _pp(inputs["L1_attn_bo"])
    sh["l1a_bd_row"] = f32(np.asarray(inputs["L1_attn_bd"]).reshape(1, H))
    sh["l1a_gamma"] = _pp(inputs["L1_attn_gamma"])
    sh["l1a_beta"] = _pp(inputs["L1_attn_beta"])
    sh["l1a_gamma_t"] = f32(np.asarray(inputs["L1_attn_gamma"]).reshape(NCH, 128))
    sh["l1a_beta_t"] = f32(np.asarray(inputs["L1_attn_beta"]).reshape(NCH, 128))
    bfa = lambda a: np.ascontiguousarray(np.asarray(a, np.float32).astype(NPBF16))
    sh["c_id128"] = bfa(np.eye(128))
    sh["c_ones1x128"] = bfa(np.ones((1, 128)))
    sh["c_onescol"] = bfa(np.ones((128, 1)))
    sh["c_ones1x128_f32"] = np.ones((1, 128), np.float32)
    sh["c_iota_row"] = np.arange(S, dtype=np.float32).reshape(1, S)
    sh["c_iota2d"] = np.ascontiguousarray(
        np.arange(128, dtype=np.float32)[:, None]
        + 128.0 * np.arange(16, dtype=np.float32)[None, :])
    inv_freq = (1.0 / 10000.0 ** (np.arange(0, HD, 2, dtype=np.float32) / HD)
                ).astype(np.float32)
    freqs = np.arange(S, dtype=np.float32)[:, None] * inv_freq[None, :]
    emb = np.concatenate([freqs, freqs], axis=-1)
    sh["c_ropecos"] = f32(np.cos(emb).T)
    sh["c_ropesin"] = f32(np.sin(emb).T)
    gsel = np.zeros((128, 4), np.float32)
    for gg in range(4):
        gsel[gg * 32:(gg + 1) * 32, gg] = 1.0
    sh["c_gsel"] = f32(gsel)
    sh["c_gexp"] = f32(gsel.T.copy())
    sh["c_diagmask"] = (np.arange(128)[:, None] > np.arange(128)[None, :]
                        ).astype(np.uint8)
    pm = np.zeros((HD, HD), np.float32)
    for i in range(HD // 2):
        pm[2 * i + 1, 2 * i] = -1.0
        pm[2 * i, 2 * i + 1] = 1.0
    sh["c_rope_perm"] = bfa(pm)
    sh["c_onescol_f32"] = np.ones((128, 1), np.float32)
    kr = np.zeros((4, S), np.float32)
    kr[0:3, :] = 1.0
    sh["c_kaugrows"] = bfa(kr)
    e3 = np.zeros((1, 12), np.float32)
    e3[0, 0] = 1.0; e3[0, 5] = 1.0; e3[0, 10] = 1.0
    sh["c_eye4"] = bfa(e3)
    cp = np.zeros((128, 8), np.float32)
    cp[:, 0] = np.pi / 2
    cp[:, 1] = 0.5
    cp[:, 2] = -5.0 * math.log(2.0)
    sh["c_pool"] = f32(cp)

    x = np.asarray(inputs["L0_attn_x"], np.float32)
    maps = []
    for cid in range(8):
        m = dict(sh)
        xc = x[2 * cid:2 * cid + 2]
        xt = np.ascontiguousarray(xc.transpose(2, 0, 1).reshape(D, TOK))
        m["x0"] = np.ascontiguousarray(xt.reshape(NCH, 128, TOK))
        maps.append(m)
    return maps



class Kern:
    def __init__(self, nc, P, stack):
        self.nc = nc
        self.P = P
        self.stack = stack
        self.g = {}
        self.c = {}

    def sb(self, stk, name, shape, dt=F32):
        return stk.enter_context(self.nc.sbuf_tensor("sb_" + name, shape, dt))

    def ps8(self, stk, tag):
        return [stk.enter_context(self.nc.psum_tensor(f"ps_{tag}_{i}", [128, 512], F32))
                for i in range(8)]


DRAM_SPECS = {
    "x0": ([NCH, 128, TOK], F32),
    "out": ([NCH, 128, TOK], F32),
    "xres": ([NCH, 128, TOK], F32),
    "xres2": ([NCH, 128, TOK], F32),
    "xln": ([NCH, 128, TOK], BF16),
    "xln2": ([NCH, 128, TOK], BF16),
    "hscr": ([FCH, 128, S], BF16),
    "hscr2": ([FCH, 128, S], BF16),
}

WEIGHT_SPECS = {
    "l0a_wq": [D, D], "l0a_wk": [D, D], "l0a_wv": [D, D], "l0a_wg": [D, D],
    "l0a_wa": [D, 2 * D], "l0a_wo": [D, D],
    "l0a_w1m": [FCH, D, 128], "l0a_w2m": [NCH, F, 128],
    "l0a_gns": [128, NCH], "l0a_gnb": [128, NCH],
    "l0a_ln1s": [128, NCH], "l0a_ln1b": [128, NCH],
    "l0a_ln2s": [128, NCH], "l0a_ln2b": [128, NCH],
    "l0a_b1": [128, FCH], "l0a_b2": [128, NCH],
    "l1a_wot": [D, D], "l1a_wdt": [D, H],
    "l1a_bo": [128, NCH], "l1a_bd_row": [1, H],
    "l1a_gamma": [128, NCH], "l1a_beta": [128, NCH],
    "l1a_gamma_t": [NCH, 128], "l1a_beta_t": [NCH, 128],
    "c_ones1x128_f32": [1, 128], "c_onescol_f32": [128, 1],
    "c_iota_row": [1, S], "c_iota2d": [128, 16],
    "c_ropecos": [HD, S], "c_ropesin": [HD, S],
    "c_gsel": [128, 4], "c_gexp": [4, 128], "c_pool": [128, 8],
}
for _pre in ("l0f", "l1f"):
    WEIGHT_SPECS[f"{_pre}_w1m"] = [FCH, D, 128]
    WEIGHT_SPECS[f"{_pre}_w2m"] = [FCH, D, 128]
    WEIGHT_SPECS[f"{_pre}_h1m"] = [FCH, F, 128]
    WEIGHT_SPECS[f"{_pre}_h2m"] = [FCH, F, 128]
    WEIGHT_SPECS[f"{_pre}_w3m"] = [NCH, F, 128]
    for _s in ("s1", "s2", "sh1", "sh2"):
        WEIGHT_SPECS[f"{_pre}_{_s}"] = [128, FCH]
    WEIGHT_SPECS[f"{_pre}_s3"] = [128, NCH]
for _nm in ("wq", "wk", "wv", "wg"):
    WEIGHT_SPECS[f"l1a_{_nm}h"] = [H, D, HD]
    WEIGHT_SPECS[f"l1a_b{_nm[1]}h"] = [H, HD]
BF16_CONSTS = {"c_id128": [128, 128], "c_ones1x128": [1, 128],
               "c_onescol": [128, 1], "c_rope_perm": [HD, HD],
               "c_kaugrows": [4, S], "c_eye4": [1, 12]}


def declare_io(K, stage_ext=None):
    nc = K.nc
    ext = stage_ext or {}
    for name, (shape, dt) in DRAM_SPECS.items():
        kind = ("Internal" if name not in ("x0", "out") else
                ("ExternalInput" if name == "x0" else "ExternalOutput"))
        if name in ext:
            kind = "ExternalInput" if ext[name] == "in" else "ExternalOutput"
        K.g[name] = nc.dram_tensor(name, shape, dt, kind=kind).ap()
    for name, shape in WEIGHT_SPECS.items():
        K.g[name] = nc.dram_tensor(name, shape, F32, kind="ExternalInput").ap()
    for name, shape in BF16_CONSTS.items():
        K.g[name] = nc.dram_tensor(name, shape, BF16, kind="ExternalInput").ap()
    K.g["c_diagmask"] = nc.dram_tensor("c_diagmask", [128, 128], U8,
                                       kind="ExternalInput").ap()


def load_consts(K, stk):
    P = K.P
    c = K.c
    for nm, shape, dt in (
            ("c_id128", [128, 128], BF16), ("c_ones1x128", [1, 128], BF16),
            ("c_onescol", [128, 1], BF16), ("c_ones1x128_f32", [1, 128], F32),
            ("c_onescol_f32", [128, 1], F32), ("c_iota2d", [128, 16], F32),
            ("c_gsel", [128, 4], F32), ("c_gexp", [4, 128], F32),
            ("c_pool", [128, 8], F32), ("c_diagmask", [128, 128], U8)):
        c[nm] = K.sb(stk, "sb" + nm, shape, dt)
        P.dma("sync", c[nm][:], K.g[nm])
    P.barrier()



class TDefer:

    def __init__(self, P):
        self.P = P
        self.sem = P.sem("t3")
        self.pend = []

    IMMEDIATE = False

    def ten(self, fn, waits):
        t = self.sem.bump(1)
        def emit(fn=fn, waits=tuple(waits)):
            self.P.on("tensor", fn, waits=waits, inc=(self.sem, 1))
        if TDefer.IMMEDIATE:
            emit()
        else:
            self.pend.append(emit)
        return (self.sem, t)

    def flush_one(self):
        if self.pend:
            self.pend.pop(0)()

    def flush(self):
        while self.pend:
            self.pend.pop(0)()


class Boundary:

    def __init__(self, K, stk, tag, banks):
        self.K = K
        self.tag = tag
        self.pb = banks
        self.Bbf = K.sb(stk, f"{tag}_bbf", [128, 8 * 512], BF16)
        self.Bsq = K.sb(stk, f"{tag}_bsq", [128, 8 * 512], BF16)
        self.T1 = K.sb(stk, f"{tag}_t1", [128, 8 * 512], F32)
        self.Xr = K.sb(stk, f"{tag}_xr", [128, 8 * 512], F32)
        self.rows = K.sb(stk, f"{tag}_rows", [1, 512 * 4], F32)
        self.war_bbf = None

    def row(self, i):
        return self.rows[0:1, 512 * i:512 * (i + 1)]

    def ln_rows(self, nfeat, eps, after_te):
        K, P = self.K, self.K.P
        sV, sS = P.sem("v"), P.sem("s")
        pSa, pSb = self.pb[0][0:1, :], self.pb[1][0:1, :]
        mu, r, nmr, va = self.row(0), self.row(1), self.row(2), self.row(3)
        inv = 1.0 / nfeat
        tV = sV.bump(1)
        P.on("vector", lambda e: (
            e.tensor_scalar(out=mu, in0=pSa, scalar1=inv, scalar2=None, op0=ALU.mult),
            e.tensor_scalar(out=va, in0=pSb, scalar1=inv, scalar2=None, op0=ALU.mult),
            e.tensor_tensor(out=nmr, in0=mu, in1=mu, op=ALU.mult),
            e.tensor_tensor(out=va, in0=va, in1=nmr, op=ALU.subtract),
            e.tensor_scalar(out=va, in0=va, scalar1=eps, scalar2=None, op0=ALU.add),
        )[-1], waits=[after_te], inc=(sV, 1))
        tS = sS.bump(1)
        P.on("scalar", lambda e: (
            e.activation(out=r, in_=va, func=AF.Ln),
            e.activation(out=r, in_=r, func=AF.Exp, scale=-0.5),
        )[-1], waits=[(sV, tV)], inc=(sS, 1))
        tV2 = sV.bump(1)
        P.on("vector", lambda e: (
            e.tensor_tensor(out=nmr, in0=mu, in1=r, op=ALU.mult),
            e.tensor_scalar(out=nmr, in0=nmr, scalar1=-1.0, scalar2=None,
                            op0=ALU.mult),
        )[-1], waits=[(sS, tS)], inc=(sV, 1))
        return tV2

    def stats(self, src_tiles, after):
        K, P = self.K, self.K.P
        sS = P.sem("s")
        guard = [after]
        if self.war_bbf is not None:
            guard.append(self.war_bbf)
        n = len(src_tiles)
        tS = sS.bump(1)
        def sq_all(e):
            r = None
            for m in range(n):
                r = e.activation(out=self.Bsq[:, 512 * m:512 * (m + 1)],
                                 in_=src_tiles[m], func=AF.Square)
            return r
        P.on("scalar", sq_all, waits=guard, inc=(sS, 1))
        oc = K.c["c_onescol"]
        ocf = K.c["c_onescol_f32"]
        pSa, pSb = self.pb[0][0:1, :], self.pb[1][0:1, :]
        def sums(e):
            r = None
            for m in range(n):
                e.matmul(pSa, ocf[:], src_tiles[m],
                         start=(m == 0), stop=(m == n - 1))
            for m in range(n):
                r = e.matmul(pSb, oc[:], self.Bsq[:, 512 * m:512 * (m + 1)],
                             start=(m == 0), stop=(m == n - 1))
            return r
        return self.K.td.ten(sums, guard + [(sS, tS)])

    def bcast(self, after_rows):
        K, P = self.K, self.K.P
        o1 = K.c["c_ones1x128_f32"]
        c1, c2, r1, r2 = self.pb[2][:], self.pb[3][:], self.row(1), self.row(2)
        def bc(e):
            e.matmul(c1, o1[:], r1, start=True, stop=True)
            return e.matmul(c2, o1[:], r2, start=True, stop=True)
        return self.K.td.ten(bc, [(P.sem("v"), after_rows)])

    def run_tt(self, b, tt, Bsrc, after_B, resid_in, res_out, ln_out,
               affine=None, final=False, eps_pre=1e-6):
        K, P = self.K, self.K.P
        g = K.g
        tok0 = b * S + tt * 512
        sV = P.sem("v")
        C1, C2 = self.pb[2][:], self.pb[3][:]
        cur = Bsrc
        cur_guard = after_B
        if affine is not None:
            st = self.stats(cur, cur_guard)
            rv = self.ln_rows(D, eps_pre, st)
            bt = self.bcast(rv)
            spp, bpp = affine
            tV = sV.bump(1)
            def apply_aff(e, cur=cur):
                r = None
                for m in range(8):
                    t1 = self.T1[:, 512 * m:512 * (m + 1)]
                    e.tensor_tensor(out=t1, in0=cur[m], in1=C1, op=ALU.mult)
                    e.tensor_tensor(out=t1, in0=t1, in1=C2, op=ALU.add)
                    r = e.tensor_scalar(out=t1, in0=t1, scalar1=spp[:, m:m + 1],
                                        scalar2=bpp[:, m:m + 1], op0=ALU.mult,
                                        op1=ALU.add)
                return r
            P.on("vector", apply_aff, waits=[cur_guard, bt], inc=(sV, 1))
            guard_B = tV
            cur = [self.T1[:, 512 * m:512 * (m + 1)] for m in range(8)]
            cur_guard = (sV, tV)
        st = self.stats(cur, cur_guard)
        rv = self.ln_rows(D, 1e-5, st)
        bt = self.bcast(rv)
        warx = [(sV, sV.n)]
        for m in range(8):
            ldw = P.dma("scalar", self.Xr[:, 512 * m:512 * (m + 1)],
                        g[resid_in][m, :, tok0:tok0 + 512], waits=warx,
                        slot="bx" + self.tag)
        tV = sV.bump(1)
        def apply_res(e, cur=cur):
            r = None
            for m in range(8):
                dst = self.T1[:, 512 * m:512 * (m + 1)]
                e.tensor_tensor(out=dst, in0=cur[m], in1=C1, op=ALU.mult)
                e.tensor_tensor(out=dst, in0=dst, in1=C2, op=ALU.add)
                r = e.tensor_tensor(out=dst, in0=dst,
                                    in1=self.Xr[:, 512 * m:512 * (m + 1)],
                                    op=ALU.add)
            return r
        P.on("vector", apply_res, waits=[cur_guard, bt, ldw], inc=(sV, 1))
        ret_guard = (sV, tV)
        xnew = [self.T1[:, 512 * m:512 * (m + 1)] for m in range(8)]
        out_name = "out" if final else res_out
        for m in range(8):
            P.dma("scalar", g[out_name][m, :, tok0:tok0 + 512], xnew[m],
                  waits=[(sV, tV)], slot="bs" + self.tag)
        if final:
            return ret_guard
        st = self.stats(xnew, (sV, tV))
        rv = self.ln_rows(D, 1e-5, st)
        bt = self.bcast(rv)
        tV2 = sV.bump(1)
        def apply_ln(e):
            r = None
            for m in range(8):
                xr = self.Xr[:, 512 * m:512 * (m + 1)]
                e.tensor_tensor(out=xr, in0=xnew[m], in1=C1, op=ALU.mult)
                r = e.tensor_tensor(out=self.Bbf[:, 512 * m:512 * (m + 1)],
                                    in0=xr, in1=C2, op=ALU.add)
            return r
        P.on("vector", apply_ln,
             waits=[(sV, tV), bt] + ([self.war_bbf] if self.war_bbf else []),
             inc=(sV, 1))
        sw = None
        for m in range(8):
            sw = P.dma("scalar", g[ln_out][m, :, tok0:tok0 + 512],
                       self.Bbf[:, 512 * m:512 * (m + 1)], waits=[(sV, tV2)],
                       slot="bs" + self.tag)
        self.war_bbf = sw
        return ret_guard



def phase_ln0(K):
    P = K.P
    g = K.g
    with contextlib.ExitStack() as stk:
        ps = K.ps8(stk, "ln0")
        bd = Boundary(K, stk, "ln0", ps[4:8])
        X = K.sb(stk, "ln0_x", [128, 8 * 512], F32)
        sA = P.sem("sync_dma")
        sV = P.sem("v")
        for b in range(NB):
            for tt in range(NT):
                tok0 = b * S + tt * 512
                war = [(sV, sV.n)]
                for m in range(8):
                    xw = P.dma("sync", X[:, 512 * m:512 * (m + 1)],
                               g["x0"][m, :, tok0:tok0 + 512], waits=war,
                               slot="nx")
                xt = [X[:, 512 * m:512 * (m + 1)] for m in range(8)]
                st = bd.stats(xt, xw)
                rv = bd.ln_rows(D, 1e-5, st)
                bt = bd.bcast(rv)
                K.td.flush()
                tV = sV.bump(1)
                def apply_ln(e, xt=xt):
                    r = None
                    for m in range(8):
                        t1 = bd.T1[:, 512 * m:512 * (m + 1)]
                        e.tensor_tensor(out=t1, in0=xt[m], in1=bd.pb[2][:],
                                        op=ALU.mult)
                        r = e.tensor_tensor(out=bd.Bsq[:, 512 * m:512 * (m + 1)],
                                            in0=t1, in1=bd.pb[3][:], op=ALU.add)
                    return r
                P.on("vector", apply_ln, waits=[bt], inc=(sV, 1))
                nw = None
                for m in range(8):
                    nw = P.dma("sync", g["xln"][m, :, tok0:tok0 + 512],
                               bd.Bsq[:, 512 * m:512 * (m + 1)],
                               waits=[(sV, tV)], slot="ns")
                bd.war_bbf = nw
        P.barrier()



def phase_ffn(K, pre, resid_in, xln_in, res_out, ln_out, final=False):
    P = K.P
    g = K.g
    with contextlib.ExitStack() as svk:
        svec = {}
        for snm, w in (("s1", FCH), ("s2", FCH), ("sh1", FCH), ("sh2", FCH),
                       ("s3", NCH)):
            svec[snm] = K.sb(svk, f"{pre}_{snm}", [128, w], F32)
            P.dma("sync", svec[snm][:], g[f"{pre}_{snm}"])
        sW = P.sem("gpsimd_dma")
        sA = P.sem("sync_dma")
        sT = P.sem("t")
        sV = P.sem("v")
        sS = P.sem("s")
        war = {}
        P.barrier()

        for b in range(NB):
            tok0 = b * S
            with contextlib.ExitStack() as hstk:
                hid = K.sb(hstk, f"{pre}_hid{b}", [128, FCH * S], BF16)
                self_dummy = None
                stk = contextlib.ExitStack()
                ps = K.ps8(stk, f"{pre}a{b}")
                xlnb = K.sb(stk, f"{pre}_xln{b}", [128, NCH * S], BF16)
                wa = [K.sb(stk, f"{pre}_wa{i}{b}", [128, NCH * 128], BF16)
                      for i in range(4)]
                sil = K.sb(stk, f"{pre}_sil{b}", [128, 4 * 512], F32)
                for m in range(8):
                    xw = P.dma("sync", xlnb[:, m * S:(m + 1) * S],
                               g[xln_in][m, :, tok0:tok0 + S],
                               waits=[(sT, sT.n)], slot="fx")
                for m in range(FCH):
                    wb1, wb2 = wa[(m % 2) * 2], wa[(m % 2) * 2 + 1]
                    ww = {}
                    for wi, wnm in ((0, "w1m"), (1, "w2m")):
                        wk_key = ("wa", b, (m % 2) * 2 + wi)
                        ww[wi] = P.dma("gpsimd",
                              (wb1 if wi == 0 else wb2)[:]
                              .rearrange("p (k j) -> p k j", k=NCH),
                              g[f"{pre}_{wnm}"][m].rearrange("(k p) j -> p k j", p=128),
                              waits=[war[wk_key]] if wk_key in war else (),
                              slot=f"fa{(m % 2) * 2 + wi}")
                    def mm1(e, wb1=wb1):
                        r = None
                        for tt in range(4):
                            for k in range(NCH):
                                r = e.matmul(
                                    ps[tt][:], wb1[:, k * 128:(k + 1) * 128],
                                    xlnb[:, k * S + tt * 512:k * S + (tt + 1) * 512],
                                    start=(k == 0), stop=(k == NCH - 1))
                        return r
                    t_mm1 = sT.bump(1)
                    P.on("tensor", mm1,
                         waits=[ww[0], xw, (sV, sV.n), (sS, sS.n)],
                         inc=(sT, 1))
                    war[("wa", b, (m % 2) * 2)] = (sT, t_mm1)
                    t_sil = sS.bump(1)
                    def ev1(e, m=m):
                        r = None
                        for tt in range(4):
                            r = e.activation(out=sil[:, tt * 512:(tt + 1) * 512],
                                             in_=ps[tt][:], func=AF.Silu,
                                             scale=svec["s1"][:, m:m + 1])
                        return r
                    P.on("scalar", ev1, waits=[(sT, t_mm1)], inc=(sS, 1))
                    def mm2(e, wb2=wb2):
                        r = None
                        for tt in range(4):
                            for k in range(NCH):
                                r = e.matmul(
                                    ps[4 + tt][:], wb2[:, k * 128:(k + 1) * 128],
                                    xlnb[:, k * S + tt * 512:k * S + (tt + 1) * 512],
                                    start=(k == 0), stop=(k == NCH - 1))
                        return r
                    t_mm2 = sT.bump(1)
                    P.on("tensor", mm2, waits=[ww[1], (sV, sV.n)], inc=(sT, 1))
                    war[("wa", b, (m % 2) * 2 + 1)] = (sT, t_mm2)
                    t_u = sV.bump(1)
                    def ev2(e, m=m):
                        r = None
                        for tt in range(4):
                            r = e.scalar_tensor_tensor(
                                out=hid[:, m * S + tt * 512:m * S + (tt + 1) * 512],
                                in0=ps[4 + tt][:], scalar=svec["s2"][:, m:m + 1],
                                in1=sil[:, tt * 512:(tt + 1) * 512],
                                op0=ALU.mult, op1=ALU.mult)
                        return r
                    P.on("vector", ev2, waits=[(sT, t_mm2), (sS, t_sil)],
                         inc=(sV, 1))
                if FFN_DEBUG == 1:
                    for kk in range(FCH):
                        P.dma("sync", g["hscr2"][kk], hid[:, kk * S:(kk + 1) * S],
                              waits=[(sV, sV.n)] if kk == 0 else ())
                P.barrier()
                stk.close()
                stk = contextlib.ExitStack()
                ps = K.ps8(stk, f"{pre}bc{b}")
                wtile = [K.sb(stk, f"{pre}_wt{i}{b}", [128, FCH * 128], BF16)
                         for i in range(2)]
                vstg = [K.sb(stk, f"{pre}_vs{i}{b}", [128, S], BF16)
                        for i in range(2)]
                stages_bc = (("B", "h1m", "sh1", None, "hscr"),) if FFN_DEBUG == 1 else (
                    ("B", "h1m", "sh1", None, "hscr"),
                    ("C", "h2m", "sh2", "hscr", "hscr2"))
                for stage, wnm, snm, src_dram, dst_dram in stages_bc:
                    hidw = None
                    if src_dram is not None:
                        stw = war.get(("vs", b, 0)), war.get(("vs", b, 1))
                        w0 = [(sT, sT.n)] + [x for x in stw if x]
                        for kk in range(FCH):
                            hidw = P.dma("sync", hid[:, kk * S:(kk + 1) * S],
                                         g[src_dram][kk],
                                         waits=w0 if kk == 0 else (), slot="fh")
                    for m in range(FCH):
                        wb = wtile[m % 2]
                        wt_key = ("wt", b, m % 2)
                        wtw = P.dma("gpsimd",
                              wb[:].rearrange("p (k j) -> p k j", k=FCH),
                              g[f"{pre}_{wnm}"][m].rearrange("(k p) j -> p k j", p=128),
                              waits=[war[wt_key]] if wt_key in war else (),
                              slot=f"fw{m % 2}")
                        def mmh(e, wb=wb, m=m):
                            r = None
                            for k in range(FCH):
                                for tt in range(4):
                                    r = e.matmul(
                                        ps[(m % 2) * 4 + tt][:],
                                        wb[:, k * 128:(k + 1) * 128],
                                        hid[:, k * S + tt * 512:k * S + (tt + 1) * 512],
                                        start=(k == 0), stop=(k == FCH - 1))
                            return r
                        t_mm = sT.bump(1)
                        P.on("tensor", mmh,
                             waits=[wtw, (sS, sS.n)] + ([hidw] if hidw else [xw]),
                             inc=(sT, 1))
                        war[("wt", b, m % 2)] = (sT, t_mm)
                        t_g = sS.bump(1)
                        vs = vstg[m % 2]
                        def evg(e, m=m, vs=vs, snm=snm):
                            r = None
                            for tt in range(4):
                                r = e.activation(
                                    out=vs[:, tt * 512:(tt + 1) * 512],
                                    in_=ps[(m % 2) * 4 + tt][:], func=AF.Gelu,
                                    scale=svec[snm][:, m:m + 1])
                            return r
                        vsw = war.get(("vs", b, m % 2))
                        P.on("scalar", evg,
                             waits=[(sT, t_mm)] + ([vsw] if vsw else []),
                             inc=(sS, 1))
                        war[("vs", b, m % 2)] = P.dma(
                            "sync", g[dst_dram][m], vs[:], waits=[(sS, t_g)],
                            slot=f"fs{m % 2}")
                P.barrier()
                stk.close()
            if FFN_DEBUG:
                continue
            with contextlib.ExitStack() as stk:
                ps = K.ps8(stk, f"{pre}d{b}")
                w3sb = K.sb(stk, f"{pre}_w3sb{b}", [128, NCH * FCH * 128], BF16)
                wslab = [K.sb(stk, f"{pre}_ws{i}{b}", [128, 512], BF16)
                         for i in range(4)]
                Bout = [K.sb(stk, f"{pre}_bo{i}{b}", [128, 8 * 512], F32)
                        for i in range(2)]
                bd = Boundary(K, stk, f"{pre}_bd{b}", ps[4:8])
                for m in range(NCH):
                    w3w = P.dma("gpsimd",
                          w3sb[:].rearrange("p (k m j) -> p k m j", k=FCH, m=NCH)
                          [:, :, m, :],
                          g[f"{pre}_w3m"][m].rearrange("(k p) j -> p k j", p=128),
                          slot="fw3")
                for tt in range(4):
                    Bo = Bout[tt % 2]
                    t_evs = []
                    for wave in range(2):
                        for k in range(FCH):
                            wsl = wslab[k % 4]
                            ws_key = ("wsl", pre, b, k % 4)
                            slw = P.dma("sync", wsl[:],
                                  g["hscr2"][k][:, tt * 512:(tt + 1) * 512],
                                  waits=[war[ws_key]] if ws_key in war else (),
                                  slot=f"fd{k % 4}")
                            waits = [slw, w3w]
                            if k == 0:
                                bw = war.get(("dbank", pre, b))
                                if bw is not None:
                                    waits.append(bw)
                            def mmd(e, wsl=wsl, k=k, wave=wave):
                                r = None
                                for mi in range(4):
                                    m = wave * 4 + mi
                                    r = e.matmul(
                                        ps[mi][:],
                                        w3sb[:, (k * NCH + m) * 128:
                                             (k * NCH + m + 1) * 128],
                                        wsl[:], start=(k == 0),
                                        stop=(k == FCH - 1))
                                return r
                            t_mm = sT.bump(1)
                            P.on("tensor", mmd, waits=waits, inc=(sT, 1))
                            war[ws_key] = (sT, t_mm)
                            if k % 4 == 2:
                                K.td.flush_one()
                        t_ev = sV.bump(1)
                        def evd(e, wave=wave, Bo=Bo):
                            r = None
                            for mi in range(4):
                                m = wave * 4 + mi
                                r = e.tensor_scalar(
                                    out=Bo[:, m * 512:(m + 1) * 512],
                                    in0=ps[mi][:],
                                    scalar1=svec["s3"][:, m:m + 1], scalar2=None,
                                    op0=ALU.mult)
                            return r
                        bo_w = war.get(("bout", pre, b, tt % 2))
                        P.on("vector", evd,
                             waits=[(sT, sT.n)] + ([bo_w] if bo_w else []),
                             inc=(sV, 1))
                        war[("dbank", pre, b)] = (sV, t_ev)
                        t_evs.append(t_ev)
                    Bt = [Bo[:, m * 512:(m + 1) * 512] for m in range(NCH)]
                    rg = bd.run_tt(b, tt, Bt, (sV, t_evs[-1]), resid_in, res_out,
                                   ln_out, affine=None, final=final)
                    war[("bout", pre, b, tt % 2)] = rg
                    if FFN_SERIAL:
                        K.td.flush()
                        war[("dbank", pre, b)] = rg
                K.td.flush()
                P.barrier()
    P.barrier()


def phase_l0attn(K, resid_in, xln_in, res_out, ln_out):
    P = K.P
    g = K.g
    c = K.c
    sW = P.sem("gpsimd_dma")
    sA = P.sem("sync_dma")
    sT = P.sem("t")
    sV = P.sem("v")
    sS = P.sem("s")
    sT2 = P.sem("t2")
    war = {}
    GEPS = 1e-6
    GINV = 1.0 / (S * 32)

    with contextlib.ExitStack() as pstk:
        vec = {}
        for nm in ("gns", "gnb", "ln1s", "ln1b", "ln2s", "ln2b", "b2"):
            vec[nm] = K.sb(pstk, f"l0a_{nm}v", [128, NCH], F32)
            P.dma("sync", vec[nm][:], g[f"l0a_{nm}"])
        vec["b1"] = K.sb(pstk, "l0a_b1v", [128, FCH], F32)
        P.dma("sync", vec["b1"][:], g["l0a_b1"])
        P.barrier()

        for b in range(NB):
            tok0 = b * S
            with contextlib.ExitStack() as bstk:
                zb = K.sb(bstk, f"l0a_z{b}", [128, NCH * S], BF16)
                with contextlib.ExitStack() as stk:
                    ps = K.ps8(stk, f"l0ap{b}")
                    xlnb = K.sb(stk, f"l0a_xln{b}", [128, NCH * S], BF16)
                    T = {nm: K.sb(stk, f"l0a_{nm}{b}", [128, S], F32)
                         for nm in ("tq", "tk", "tv", "tar", "tai", "tg",
                                    "tyr", "sc1", "sc2", "sc3")}
                    wsl = [K.sb(stk, f"l0a_ws{i}{b}", [128, NCH * 128], BF16)
                           for i in range(12)]
                    Sg = K.sb(stk, f"l0a_sg{b}", [128, 8], F32)
                    Sg2 = K.sb(stk, f"l0a_sg2{b}", [8, 8], F32)
                    lnbf = K.sb(stk, f"l0a_lbf{b}", [128, 8 * 512], BF16)
                    lnsq = K.sb(stk, f"l0a_lsq{b}", [128, 8 * 512], BF16)
                    rows = K.sb(stk, f"l0a_rows{b}", [1, 4 * 512], F32)
                    for m in range(8):
                        xw = P.dma("sync", xlnb[:, m * S:(m + 1) * S],
                                   g[xln_in][m, :, tok0:tok0 + S],
                                   waits=[(sT, sT.n)], slot="ax")
                    pool_pi2 = c["c_pool"][:, 0:1]
                    pool_half = c["c_pool"][:, 1:2]

                    pending_stats = []
                    for ch in range(NCH):
                        wset = wsl[(ch % 2) * 6:(ch % 2) * 6 + 6]
                        srcs = [("l0a_wq", ch * 128), ("l0a_wk", ch * 128),
                                ("l0a_wv", ch * 128), ("l0a_wg", ch * 128),
                                ("l0a_wa", ch * 128), ("l0a_wa", D + ch * 128)]
                        wq_key = ("l0aw", b, ch % 2)
                        for wi, (wn, off) in enumerate(srcs):
                            wsw = P.dma("gpsimd",
                                  wset[wi][:].rearrange("p (k j) -> p k j", k=NCH),
                                  g[wn][:, off:off + 128]
                                  .rearrange("(k p) j -> p k j", p=128),
                                  waits=([war[wq_key]] if wq_key in war else ())
                                  if wi == 0 else (), slot=f"aw{ch % 2}")
                        quants = [("tq", 0, "dve"), ("tk", 1, "dve"),
                                  ("tv", 2, "dve"), ("tar", 4, "act"),
                                  ("tai", 5, "act"), ("tg", 3, "silu")]
                        for qi, (dnm, wi, ev) in enumerate(quants):
                            if qi == 1 and pending_stats:
                                for f in pending_stats:
                                    f()
                                pending_stats = []
                            bank = (qi % 2) * 4
                            def mmq(e, wi=wi, bank=bank, wset=wset):
                                r = None
                                for tt in range(4):
                                    for k in range(NCH):
                                        r = e.matmul(
                                            ps[bank + tt][:],
                                            wset[wi][:, k * 128:(k + 1) * 128],
                                            xlnb[:, k * S + tt * 512:
                                                 k * S + (tt + 1) * 512],
                                            start=(k == 0), stop=(k == NCH - 1))
                                return r
                            t_mm = sT.bump(1)
                            bank_war = war.get(("l0abank", b, qi % 2), ())
                            P.on("tensor", mmq,
                                 waits=[wsw, xw] + list(bank_war),
                                 inc=(sT, 1))
                            war[("l0aw", b, ch % 2)] = (sT, t_mm)
                            if ev == "dve":
                                t_e = sV.bump(1)
                                def evq(e, dnm=dnm, bank=bank):
                                    r = None
                                    for tt in range(4):
                                        r = e.tensor_copy(
                                            out=T[dnm][:, tt * 512:(tt + 1) * 512],
                                            in_=ps[bank + tt][:])
                                    return r
                                sqw = war.get(("l0asq", b)) if dnm == "tv" else None
                                P.on("vector", evq,
                                     waits=[(sT, t_mm)] + ([sqw] if sqw else []),
                                     inc=(sV, 1))
                                war[("l0abank", b, qi % 2)] = ((sV, t_e),)
                            else:
                                fn = AF.Silu if ev == "silu" else AF.Identity
                                t_e = sS.bump(1)
                                def evq(e, dnm=dnm, bank=bank, fn=fn):
                                    r = None
                                    for tt in range(4):
                                        r = e.activation(
                                            out=T[dnm][:, tt * 512:(tt + 1) * 512],
                                            in_=ps[bank + tt][:], func=fn)
                                    return r
                                wkey = ("l0azv", b) if ev == "silu" else                                        ("l0ayi", b)
                                pw = war.get(wkey)
                                P.on("scalar", evq,
                                     waits=[(sT, t_mm)] + ([pw] if pw else []),
                                     inc=(sS, 1))
                                war[("l0abank", b, qi % 2)] = ((sS, t_e),)
                        evac_v = sV.n
                        evac_s = sS.n

                        def dve(fn, waits=()):
                            t = sV.bump(1)
                            P.on("vector", fn, waits=waits, inc=(sV, 1))
                            return t

                        def act(fn, waits=()):
                            t = sS.bump(1)
                            P.on("scalar", fn, waits=waits, inc=(sS, 1))
                            return t

                        tq, tk, tv = T["tq"][:], T["tk"][:], T["tv"][:]
                        tar, tai, tg = T["tar"][:], T["tai"][:], T["tg"][:]
                        tyr = T["tyr"][:]
                        s1, s2, s3 = T["sc1"][:], T["sc2"][:], T["sc3"][:]
                        dve(lambda e: (
                            e.tensor_tensor(out=tk, in0=tk, in1=tv, op=ALU.mult),
                            e.tensor_tensor(out=s1, in0=tar, in1=tar, op=ALU.mult),
                            e.tensor_tensor(out=s2, in0=tai, in1=tai, op=ALU.mult),
                            e.tensor_tensor(out=s1, in0=s1, in1=s2, op=ALU.add),
                        )[-1], waits=[(sS, evac_s)])
                        act(lambda e: (
                            e.activation(out=tv, in_=s1, func=AF.Sqrt),
                            e.activation(out=tv, in_=tv, func=AF.Sigmoid),
                            e.activation(out=s2, in_=tar, func=AF.Sign),
                            e.activation(out=s2, in_=s2, func=AF.Sign, bias=pool_half),
                            e.activation(out=s1, in_=tar, func=AF.Abs),
                        )[-1], waits=[(sV, sV.n)])
                        dve(lambda e: (
                            e.tensor_scalar(out=s1, in0=s1, scalar1=1e-30,
                                            scalar2=None, op0=ALU.max),
                        )[-1], waits=[(sS, sS.n)])
                        act(lambda e: (
                            e.activation(out=s1, in_=s1, func=AF.Ln),
                            e.activation(out=s1, in_=s1, func=AF.Exp, scale=-1.0),
                        )[-1], waits=[(sV, sV.n)])
                        dve(lambda e: (
                            e.tensor_tensor(out=s1, in0=s1, in1=s2, op=ALU.mult),
                            e.tensor_tensor(out=s1, in0=tai, in1=s1, op=ALU.mult),
                        )[-1], waits=[(sS, sS.n)])
                        act(lambda e: (
                            e.activation(out=s1, in_=s1, func=AF.Arctan),
                            e.activation(out=tar, in_=tai, func=AF.Sign),
                        )[-1], waits=[(sV, sV.n)])
                        dve(lambda e: (
                            e.tensor_scalar(out=s3, in0=s2, scalar1=-PI / 2,
                                            scalar2=PI / 2, op0=ALU.mult, op1=ALU.add),
                            e.tensor_tensor(out=s3, in0=s3, in1=tar, op=ALU.mult),
                            e.tensor_tensor(out=s1, in0=s1, in1=s3, op=ALU.add),
                            e.tensor_tensor_scan(out=tai, data0=s1, data1=s1,
                                                 initial=0.0, op0=ALU.add,
                                                 op1=ALU.bypass),
                            e.tensor_scalar(out=s2, in0=tai, scalar1=INV2PI,
                                            scalar2=None, op0=ALU.mult),
                            e.tensor_scalar(out=s3, in0=s2, scalar1=MAGIC,
                                            scalar2=MAGIC, op0=ALU.add,
                                            op1=ALU.subtract),
                            e.scalar_tensor_tensor(out=s3, in0=s3, scalar=-TWOPI,
                                                   in1=tai, op0=ALU.mult,
                                                   op1=ALU.add),
                            e.tensor_scalar(out=s2, in0=s2, scalar1=0.25,
                                            scalar2=None, op0=ALU.add),
                            e.tensor_scalar(out=s2, in0=s2, scalar1=MAGIC,
                                            scalar2=MAGIC, op0=ALU.add,
                                            op1=ALU.subtract),
                            e.scalar_tensor_tensor(out=s2, in0=s2, scalar=-TWOPI,
                                                   in1=tai, op0=ALU.mult,
                                                   op1=ALU.add),
                        )[-1], waits=[(sS, sS.n)])
                        act(lambda e: (
                            e.activation(out=s3, in_=s3, func=AF.Sin),
                            e.activation(out=s2, in_=s2, func=AF.Sin, bias=pool_pi2),
                        )[-1], waits=[(sV, sV.n)])
                        t_chain = dve(lambda e: (
                            e.tensor_tensor(out=tar, in0=s2, in1=tk, op=ALU.mult),
                            e.tensor_tensor(out=tai, in0=s3, in1=tk, op=ALU.mult),
                            e.tensor_tensor_scan(out=tk, data0=tv, data1=tar,
                                                 initial=0.0, op0=ALU.mult,
                                                 op1=ALU.add),
                            e.tensor_tensor_scan(out=s1, data0=tv, data1=tai,
                                                 initial=0.0, op0=ALU.mult,
                                                 op1=ALU.add),
                            e.tensor_tensor(out=tar, in0=s2, in1=tk, op=ALU.mult),
                            e.tensor_tensor(out=tai, in0=s3, in1=s1, op=ALU.mult),
                            e.tensor_tensor(out=tar, in0=tar, in1=tai, op=ALU.add),
                            e.tensor_tensor(out=tyr, in0=tq, in1=tar, op=ALU.mult),
                            e.tensor_tensor(out=tai, in0=s3, in1=tk, op=ALU.mult),
                            e.tensor_tensor(out=tk, in0=s2, in1=s1, op=ALU.mult),
                            e.tensor_tensor(out=tai, in0=tai, in1=tk, op=ALU.subtract),
                            e.tensor_tensor(out=tv, in0=tq, in1=tai, op=ALU.mult),
                        )[-1], waits=[(sS, sS.n)])
                        war[("l0ayi", b)] = (sV, t_chain)
                        dve(lambda e: (
                            e.tensor_reduce(out=Sg[:, 0:1], in_=tyr, axis=AX.X,
                                            op=ALU.add),
                            e.drain(),
                            e.tensor_reduce(out=Sg[:, 1:2], in_=tv, axis=AX.X,
                                            op=ALU.add),
                            e.drain(),
                        )[-1])
                        act(lambda e: (
                            e.activation(out=s2, in_=tyr, func=AF.Square,
                                         accum_out=Sg[:, 2:3]),
                            e.drain(),
                            e.activation(out=s3, in_=tv, func=AF.Square,
                                         accum_out=Sg[:, 3:4]),
                            e.drain(),
                        )[-1], waits=[(sV, sV.n)])
                        war[("l0asq", b)] = (sS, sS.n)
                        t_sg = dve(lambda e: (
                            e.tensor_tensor(out=Sg[:, 2:3], in0=Sg[:, 2:3],
                                            in1=Sg[:, 3:4], op=ALU.add),
                            e.drain(),
                        )[-1], waits=[(sS, sS.n)])
                        t_g1 = sT2.bump(1)
                        def emit_g1(t_sg=t_sg):
                            P.on("tensor", lambda e: e.matmul(
                                ps[7][0:4, 0:3], c["c_gsel"][:], Sg[:, 0:3],
                                start=True, stop=True),
                                waits=[(sV, t_sg)], inc=(sT2, 1))
                        pending_stats.append(emit_g1)
                        dve(lambda e: (
                            e.tensor_scalar(out=Sg2[0:4, 2:5], in0=ps[7][0:4, 0:3],
                                            scalar1=GINV, scalar2=None, op0=ALU.mult),
                            e.drain(),
                            e.tensor_tensor(out=Sg2[0:4, 5:6], in0=Sg2[0:4, 2:3],
                                            in1=Sg2[0:4, 2:3], op=ALU.mult),
                            e.tensor_tensor(out=Sg2[0:4, 6:7], in0=Sg2[0:4, 3:4],
                                            in1=Sg2[0:4, 3:4], op=ALU.mult),
                            e.drain(),
                            e.tensor_tensor(out=Sg2[0:4, 5:6], in0=Sg2[0:4, 5:6],
                                            in1=Sg2[0:4, 6:7], op=ALU.add),
                            e.drain(),
                            e.tensor_tensor(out=Sg2[0:4, 5:6], in0=Sg2[0:4, 4:5],
                                            in1=Sg2[0:4, 5:6], op=ALU.subtract),
                            e.drain(),
                            e.tensor_scalar(out=Sg2[0:4, 5:6], in0=Sg2[0:4, 5:6],
                                            scalar1=GEPS, scalar2=None, op0=ALU.add),
                            e.drain(),
                        )[-1], waits=[(sT2, t_g1)])
                        act(lambda e: (
                            e.activation(out=Sg2[0:4, 1:2], in_=Sg2[0:4, 5:6],
                                         func=AF.Ln),
                            e.drain(),
                            e.activation(out=Sg2[0:4, 1:2], in_=Sg2[0:4, 1:2],
                                         func=AF.Exp, scale=-0.5),
                            e.drain(),
                        )[-1], waits=[(sV, sV.n)])
                        t_sm = dve(lambda e: (
                            e.tensor_copy(out=Sg2[0:4, 0:1], in_=Sg2[0:4, 2:3]),
                            e.drain(),
                        )[-1], waits=[(sS, sS.n)])
                        t_g2 = sT2.bump(1)
                        def emit_g2(t_sm=t_sm):
                            P.on("tensor", lambda e: e.matmul(
                                ps[7][:, 4:6], c["c_gexp"][:], Sg2[0:4, 0:2],
                                start=True, stop=True),
                                waits=[(sV, t_sm)], inc=(sT2, 1))
                        pending_stats.append(emit_g2)
                        t_ab = dve(lambda e, ch=ch: (
                            e.tensor_tensor(out=Sg[:, 4:5], in0=ps[7][:, 5:6],
                                            in1=vec["gns"][:, ch:ch + 1], op=ALU.mult),
                            e.drain(),
                            e.tensor_tensor(out=Sg[:, 5:6], in0=ps[7][:, 4:5],
                                            in1=Sg[:, 4:5], op=ALU.mult),
                            e.drain(),
                            e.scalar_tensor_tensor(out=Sg[:, 5:6], in0=Sg[:, 5:6],
                                                   scalar=-1.0,
                                                   in1=vec["gnb"][:, ch:ch + 1],
                                                   op0=ALU.mult, op1=ALU.add),
                            e.drain(),
                            e.tensor_scalar(out=s2, in0=tyr, scalar1=Sg[:, 4:5],
                                            scalar2=Sg[:, 5:6], op0=ALU.mult,
                                            op1=ALU.add),
                        )[-1], waits=[(sT2, t_g2)])
                        t_z = dve(lambda e, ch=ch: e.tensor_tensor(
                            out=zb[:, ch * S:(ch + 1) * S], in0=s2, in1=tg,
                            op=ALU.mult))
                        war[("l0azv", b)] = (sV, t_z)

                    for f in pending_stats:
                        f()
                    pending_stats = []
                    P.barrier()
                    for mo in range(8):
                        wow = P.dma("gpsimd",
                              wsl[mo][:].rearrange("p (k j) -> p k j", k=NCH),
                              g["l0a_wo"][:, mo * 128:(mo + 1) * 128]
                              .rearrange("(k p) j -> p k j", p=128), slot="ao")
                    oc = c["c_onescol"]
                    o1f = c["c_ones1x128_f32"]
                    for tt in range(4):
                        xi = [T["tq"][:, m * 512:(m + 1) * 512] for m in range(4)] + \
                             [T["tk"][:, m * 512:(m + 1) * 512] for m in range(4)]
                        ev_ts = []
                        for wave in range(2):
                            def mmo(e, tt=tt, wave=wave):
                                r = None
                                for mi in range(4):
                                    mo = wave * 4 + mi
                                    for k in range(NCH):
                                        e.matmul(
                                            ps[mi][:],
                                            wsl[mo][:, k * 128:(k + 1) * 128],
                                            zb[:, k * S + tt * 512:
                                               k * S + (tt + 1) * 512],
                                            start=(k == 0), stop=False)
                                    r = e.matmul(
                                        ps[mi][:], c["c_id128"][:],
                                        xlnb[:, mo * S + tt * 512:
                                             mo * S + (tt + 1) * 512],
                                        start=False, stop=True)
                                return r
                            K.td.flush_one()
                            K.td.flush_one()
                            bw = war.get(("wbank", b))
                            t_mm = sT.bump(1)
                            P.on("tensor", mmo,
                                 waits=[wow] + ([bw] if bw else []),
                                 inc=(sT, 1))
                            t_ev = sV.bump(1)
                            def evo(e, xi=xi, wave=wave):
                                r = None
                                for mi in range(4):
                                    r = e.tensor_copy(out=xi[wave * 4 + mi],
                                                      in_=ps[mi][:])
                                return r
                            P.on("vector", evo, waits=[(sT, t_mm)], inc=(sV, 1))
                            war[("wbank", b)] = (sV, t_ev)
                            ev_ts.append(t_ev)
                        t_p = P.sem("p").bump(1)
                        def castf(e, xi=xi):
                            r = None
                            for m in range(8):
                                r = e.tensor_copy(
                                    out=lnbf[:, m * 512:(m + 1) * 512], in_=xi[m])
                            return r
                        P.on("gpsimd", castf, waits=[(sV, ev_ts[-1])],
                             inc=(P.sem("p"), 1))
                        t_sq = sS.bump(1)
                        def sqf(e, xi=xi):
                            r = None
                            for m in range(8):
                                r = e.activation(
                                    out=lnsq[:, m * 512:(m + 1) * 512], in_=xi[m],
                                    func=AF.Square)
                            return r
                        P.on("scalar", sqf, waits=[(sV, ev_ts[-1])], inc=(sS, 1))
                        def sums(e):
                            r = None
                            for m in range(8):
                                e.matmul(ps[4][0:1, :], oc[:],
                                         lnbf[:, m * 512:(m + 1) * 512],
                                         start=(m == 0), stop=(m == 7))
                            for m in range(8):
                                r = e.matmul(ps[5][0:1, :], oc[:],
                                             lnsq[:, m * 512:(m + 1) * 512],
                                             start=(m == 0), stop=(m == 7))
                            return r
                        t_sum = K.td.ten(sums, [(P.sem("p"), t_p), (sS, t_sq)])
                        mu, rr, nmr, va = (rows[0:1, i * 512:(i + 1) * 512]
                                           for i in range(4))
                        t_r1 = sV.bump(1)
                        P.on("vector", lambda e, mu=mu, rr=rr, nmr=nmr, va=va: (
                            e.tensor_scalar(out=mu, in0=ps[4][0:1, :],
                                            scalar1=1.0 / D, scalar2=None,
                                            op0=ALU.mult),
                            e.tensor_scalar(out=va, in0=ps[5][0:1, :],
                                            scalar1=1.0 / D, scalar2=None,
                                            op0=ALU.mult),
                            e.tensor_tensor(out=nmr, in0=mu, in1=mu, op=ALU.mult),
                            e.tensor_tensor(out=va, in0=va, in1=nmr, op=ALU.subtract),
                            e.tensor_scalar(out=va, in0=va, scalar1=GEPS,
                                            scalar2=None, op0=ALU.add),
                        )[-1], waits=[t_sum], inc=(sV, 1))
                        t_r2 = sS.bump(1)
                        P.on("scalar", lambda e, rr=rr, va=va: (
                            e.activation(out=rr, in_=va, func=AF.Ln),
                            e.activation(out=rr, in_=rr, func=AF.Exp, scale=-0.5),
                        )[-1], waits=[(sV, t_r1)], inc=(sS, 1))
                        t_r3 = sV.bump(1)
                        P.on("vector", lambda e, mu=mu, rr=rr, nmr=nmr: (
                            e.tensor_tensor(out=nmr, in0=mu, in1=rr, op=ALU.mult),
                            e.tensor_scalar(out=nmr, in0=nmr, scalar1=-1.0,
                                            scalar2=None, op0=ALU.mult),
                        )[-1], waits=[(sS, t_r2)], inc=(sV, 1))
                        def bc(e, rr=rr, nmr=nmr):
                            e.matmul(ps[6][:], o1f[:], rr, start=True, stop=True)
                            return e.matmul(ps[7][:], o1f[:], nmr, start=True,
                                            stop=True)
                        t_bc = K.td.ten(bc, [(sV, t_r3)])
                        t_ap = sV.bump(1)
                        def apl(e, xi=xi, tt=tt):
                            r = None
                            for m in range(8):
                                sc = T["sc1"][:, m % 4 * 512:(m % 4 + 1) * 512]
                                e.tensor_tensor(out=sc, in0=xi[m], in1=ps[6][:],
                                                op=ALU.mult)
                                e.tensor_tensor(out=sc, in0=sc, in1=ps[7][:],
                                                op=ALU.add)
                                r = e.tensor_scalar(
                                    out=zb[:, m * S + tt * 512:m * S + (tt + 1) * 512],
                                    in0=sc, scalar1=vec["ln1s"][:, m:m + 1],
                                    scalar2=vec["ln1b"][:, m:m + 1],
                                    op0=ALU.mult, op1=ALU.add)
                            return r
                        P.on("vector", apl, waits=[t_bc], inc=(sV, 1))
                    K.td.flush()
                    P.barrier()
                with contextlib.ExitStack() as stk:
                    ps = K.ps8(stk, f"l0am1{b}")
                    wa = [K.sb(stk, f"l0a_w1t{i}{b}", [128, NCH * 128], BF16)
                          for i in range(2)]
                    vstg = [K.sb(stk, f"l0a_vs{i}{b}", [128, S], BF16)
                            for i in range(2)]
                    for m in range(FCH):
                        wb = wa[m % 2]
                        m1_key = ("l0am1", b, m % 2)
                        m1w = P.dma("gpsimd",
                              wb[:].rearrange("p (k j) -> p k j", k=NCH),
                              g["l0a_w1m"][m].rearrange("(k p) j -> p k j", p=128),
                              waits=[war[m1_key]] if m1_key in war else (),
                              slot=f"am{m % 2}")
                        def mmh(e, wb=wb, m=m):
                            r = None
                            for tt in range(4):
                                for k in range(NCH):
                                    r = e.matmul(
                                        ps[(m % 2) * 4 + tt][:],
                                        wb[:, k * 128:(k + 1) * 128],
                                        zb[:, k * S + tt * 512:k * S + (tt + 1) * 512],
                                        start=(k == 0), stop=(k == NCH - 1))
                            return r
                        t_mm = sT.bump(1)
                        P.on("tensor", mmh, waits=[m1w, (sS, sS.n)],
                             inc=(sT, 1))
                        war[("l0am1", b, m % 2)] = (sT, t_mm)
                        t_gl = sS.bump(1)
                        vs = vstg[m % 2]
                        def evh(e, m=m, vs=vs):
                            r = None
                            for tt in range(4):
                                r = e.activation(
                                    out=vs[:, tt * 512:(tt + 1) * 512],
                                    in_=ps[(m % 2) * 4 + tt][:],
                                    func=AF.Gelu_apprx_tanh,
                                    bias=vec["b1"][:, m:m + 1])
                            return r
                        vsw = war.get(("avs", b, m % 2))
                        P.on("scalar", evh,
                             waits=[(sT, t_mm)] + ([vsw] if vsw else []),
                             inc=(sS, 1))
                        war[("avs", b, m % 2)] = P.dma(
                            "sync", g["hscr"][m], vs[:], waits=[(sS, t_gl)],
                            slot=f"as{m % 2}")
                    P.barrier()
                with contextlib.ExitStack() as stk:
                    ps = K.ps8(stk, f"l0am2{b}")
                    w2sb = K.sb(stk, f"l0a_w2sb{b}", [128, NCH * FCH * 128], BF16)
                    wslab = [K.sb(stk, f"l0a_wsl{i}{b}", [128, 512], BF16)
                             for i in range(2)]
                    Bout = [K.sb(stk, f"l0a_bo{i}{b}", [128, 8 * 512], F32)
                            for i in range(2)]
                    bd = Boundary(K, stk, f"l0a_bd{b}", ps[4:8])
                    for m in range(NCH):
                        w2w = P.dma("gpsimd",
                              w2sb[:].rearrange("p (k m j) -> p k m j", k=FCH, m=NCH)
                              [:, :, m, :],
                              g["l0a_w2m"][m].rearrange("(k p) j -> p k j", p=128),
                              slot="aw2")
                    for tt in range(4):
                        Bo = Bout[tt % 2]
                        t_evs = []
                        for wave in range(2):
                            for k in range(FCH):
                                wsl2 = wslab[k % 2]
                                m2_key = ("l0am2", b, k % 2)
                                slw = P.dma("sync", wsl2[:],
                                      g["hscr"][k][:, tt * 512:(tt + 1) * 512],
                                      waits=[war[m2_key]] if m2_key in war else (),
                                      slot=f"a2{k % 2}")
                                waits = [slw, w2w]
                                if k == 0:
                                    bw = war.get(("m2bank", b))
                                    if bw is not None:
                                        waits.append(bw)
                                def mmd(e, wsl2=wsl2, k=k, wave=wave):
                                    r = None
                                    for mi in range(4):
                                        m = wave * 4 + mi
                                        r = e.matmul(
                                            ps[mi][:],
                                            w2sb[:, (k * NCH + m) * 128:
                                                 (k * NCH + m + 1) * 128],
                                            wsl2[:], start=(k == 0), stop=False)
                                    return r
                                t_mm = sT.bump(1)
                                P.on("tensor", mmd, waits=waits, inc=(sT, 1))
                                war[("l0am2", b, k % 2)] = (sT, t_mm)
                                if k % 4 == 2:
                                    K.td.flush_one()
                            def mmr(e, tt=tt, wave=wave):
                                r = None
                                for mi in range(4):
                                    m = wave * 4 + mi
                                    r = e.matmul(
                                        ps[mi][:], c["c_id128"][:],
                                        zb[:, m * S + tt * 512:m * S + (tt + 1) * 512],
                                        start=False, stop=True)
                                return r
                            t_mm = sT.bump(1)
                            P.on("tensor", mmr, inc=(sT, 1))
                            t_ev = sV.bump(1)
                            def evd(e, wave=wave, Bo=Bo):
                                r = None
                                for mi in range(4):
                                    m = wave * 4 + mi
                                    r = e.tensor_scalar(
                                        out=Bo[:, m * 512:(m + 1) * 512],
                                        in0=ps[mi][:], scalar1=vec["b2"][:, m:m + 1],
                                        scalar2=None, op0=ALU.add)
                                return r
                            bo_w = war.get(("m2bout", b, tt % 2))
                            P.on("vector", evd,
                                 waits=[(sT, t_mm)] + ([bo_w] if bo_w else []),
                                 inc=(sV, 1))
                            war[("m2bank", b)] = (sV, t_ev)
                            t_evs.append(t_ev)
                        Bt = [Bo[:, m * 512:(m + 1) * 512] for m in range(NCH)]
                        rg = bd.run_tt(b, tt, Bt, (sV, t_evs[-1]), resid_in,
                                       res_out, ln_out,
                                       affine=(vec["ln2s"], vec["ln2b"]),
                                       final=False, eps_pre=GEPS)
                        war[("m2bout", b, tt % 2)] = rg
                    K.td.flush()
                    P.barrier()
    P.barrier()


def phase_l1attn(K, resid_in, xln_in, res_out, ln_out):
    P = K.P
    g = K.g
    c = K.c
    sW = P.sem("gpsimd_dma")
    sA = P.sem("sync_dma")
    sT = P.sem("t")
    sV = P.sem("v")
    sS = P.sem("s")
    sPo = P.sem("p")
    war = {}

    with contextlib.ExitStack() as pstk:
        rc = {}
        for nm, shape, dt in (("c_iota_row", [1, S], F32),
                              ("c_ropecos", [HD, S], F32),
                              ("c_ropesin", [HD, S], F32),
                              ("c_rope_perm", [HD, HD], BF16),
                              ("c_eye4", [1, 12], BF16)):
            rc[nm] = K.sb(pstk, "l1" + nm, shape, dt)
            P.dma("sync", rc[nm][:], g[nm])
        vec = {}
        for nm in ("bo", "gamma", "beta"):
            vec[nm] = K.sb(pstk, f"l1a_{nm}v", [128, NCH], F32)
            P.dma("sync", vec[nm][:], g[f"l1a_{nm}"])
        bdrow = K.sb(pstk, "l1a_bdrow", [1, H], F32)
        P.dma("sync", bdrow[:], g["l1a_bd_row"])
        wdsl = K.sb(pstk, "l1a_wdsl", [128, NCH * H], F32)
        P.dma("sync", wdsl[:].rearrange("p (k j) -> p k j", k=NCH),
              g["l1a_wdt"].rearrange("(k p) j -> p k j", p=128))
        P.barrier()

        for b in range(NB):
            tok0 = b * S
            with contextlib.ExitStack() as bstk:
                zb = K.sb(bstk, f"l1a_z{b}", [128, NCH * S], BF16)
                with contextlib.ExitStack() as stk:
                    ps = K.ps8(stk, f"l1h{b}")
                    xlnb = K.sb(stk, f"l1a_xln{b}", [128, NCH * S], BF16)
                    ATj = K.sb(stk, f"l1a_at{b}", [128, 16 * 512], BF16)
                    Qaug = K.sb(stk, f"l1a_qa{b}", [68, S], BF16)
                    Kaug = K.sb(stk, f"l1a_ka{b}", [68, S], BF16)
                    Tt1 = K.sb(stk, f"l1a_t1{b}", [64, S], F32)
                    Tt2 = K.sb(stk, f"l1a_t2{b}", [64, S], F32)
                    SqB = K.sb(stk, f"l1a_sqb{b}", [64, S], BF16)
                    gsil = K.sb(stk, f"l1a_gs{b}", [64, S], BF16)
                    ogh = K.sb(stk, f"l1a_og{b}", [64, S], F32)
                    Vsb = K.sb(stk, f"l1a_v{b}", [128, 16 * 65], BF16)
                    whd = {nm: K.sb(stk, f"l1a_w{nm}{b}", [128, NCH * HD], BF16)
                           for nm in ("q", "k", "v", "g")}
                    bh = {nm: K.sb(stk, f"l1a_b{nm}{b}", [64, 1], F32)
                          for nm in ("q", "k", "v", "g")}
                    VRf = K.sb(stk, f"l1a_vrf{b}", [1, HD], F32)
                    VRb = K.sb(stk, f"l1a_vrb{b}", [1, HD], BF16)
                    mrow = K.sb(stk, f"l1a_mq{b}", [1, S], F32)
                    srow = K.sb(stk, f"l1a_sr{b}", [1, S], F32)
                    Rb0 = K.sb(stk, f"l1a_rb0{b}", [1, S], BF16)
                    Rb1 = K.sb(stk, f"l1a_rb1{b}", [1, S], BF16)
                    Rb2 = K.sb(stk, f"l1a_rb2{b}", [1, S], BF16)
                    dup = K.sb(stk, f"l1a_du{b}", [128, 128], BF16)
                    dlo = K.sb(stk, f"l1a_dl{b}", [128, 128], BF16)
                    Bh = K.sb(stk, f"l1a_bh{b}", [128, 16], F32)
                    ldc = K.sb(stk, f"l1a_ldc{b}", [128, 1], F32)
                    sm = K.sb(stk, f"l1a_sm{b}", [128, 8], F32)
                    smr = K.sb(stk, f"l1a_smr{b}", [1, 576], F32)
                    mb2 = K.sb(stk, f"l1a_mb2{b}", [64, 4], F32)
                    gbh = K.sb(stk, f"l1a_gbh{b}", [64, 2], F32)

                    def dve(fn, waits=()):
                        t = sV.bump(1)
                        P.on("vector", fn, waits=waits, inc=(sV, 1))
                        return t

                    def act(fn, waits=()):
                        t = sS.bump(1)
                        P.on("scalar", fn, waits=waits, inc=(sS, 1))
                        return t

                    def ten(fn, waits=()):
                        t = sT.bump(1)
                        P.on("tensor", fn, waits=waits, inc=(sT, 1))
                        return t

                    for m in range(8):
                        xw = P.dma("sync", xlnb[:, m * S:(m + 1) * S],
                                   g[xln_in][m, :, tok0:tok0 + S],
                                   waits=[(sT, sT.n)], slot="lx")
                    kgw = P.dma("sync", Kaug[64:68, :], g["c_kaugrows"],
                                waits=[(sT, sT.n)], slot="lk")
                    t_ones = sPo.bump(1)
                    P.on("gpsimd", lambda e: e.memset(Vsb[:], 1.0),
                         waits=[(sT, sT.n)], inc=(sPo, 1))

                    def xmean(e):
                        r = None
                        for m in range(8):
                            r = e.tensor_reduce(out=sm[:, m:m + 1],
                                                in_=xlnb[:, m * S:(m + 1) * S],
                                                axis=AX.X, op=ALU.add)
                            r = e.drain()
                        return r
                    t_xm = dve(xmean, waits=[xw])
                    def pdmm(e):
                        r = None
                        for m in range(8):
                            r = e.matmul(ps[0][0:1, 0:H], sm[:, m:m + 1],
                                         wdsl[:, m * H:(m + 1) * H],
                                         start=(m == 0), stop=(m == 7))
                        return r
                    t_pd1 = ten(pdmm, waits=[(sV, t_xm)])
                    ldr = smr[0:1, 0:H]
                    lds = smr[0:1, H:2 * H]
                    t_pd2 = dve(lambda e: (
                        e.tensor_scalar(out=ldr, in0=ps[0][0:1, 0:H],
                                        scalar1=1.0 / S, scalar2=None, op0=ALU.mult),
                        e.drain(),
                        e.tensor_tensor(out=ldr, in0=ldr, in1=bdrow[:], op=ALU.add),
                        e.drain(),
                    )[-1], waits=[(sT, t_pd1)])
                    t_pd3 = act(lambda e: (
                        e.activation(out=ldr, in_=ldr, func=AF.Sigmoid),
                        e.drain(),
                        e.activation(out=lds, in_=ldr, func=AF.Exp,
                                     scale=float(-8.0 * math.log(2.0)),
                                     bias=c["c_pool"][0:1, 2:3]),
                        e.drain(),
                    )[-1], waits=[(sV, t_pd2)])
                    t_pd4 = dve(lambda e: (
                        e.tensor_scalar(out=lds, in0=lds, scalar1=-1.0,
                                        scalar2=1.0, op0=ALU.mult, op1=ALU.add),
                        e.drain(),
                    )[-1], waits=[(sS, t_pd3)])
                    t_pd5 = act(lambda e: (
                        e.activation(out=ldr, in_=lds, func=AF.Ln),
                        e.drain(),
                    )[-1], waits=[(sV, t_pd4)])

                    for h in range(H):
                        ch_h = h // 2
                        pb_h = (h % 2) * 64
                        hw = {}
                        for nm in ("q", "k", "v", "g"):
                            wh_key = ("whd", b, nm)
                            hw[nm] = P.dma("gpsimd",
                                  whd[nm][:].rearrange("p (k j) -> p k j", k=NCH),
                                  g[f"l1a_w{nm}h"][h]
                                  .rearrange("(k p) j -> p k j", p=128),
                                  waits=[war[wh_key]] if wh_key in war else (),
                                  slot=f"lw{nm}")
                            bhw = P.dma("sync", bh[nm][:],
                                  g[f"l1a_b{nm}h"][h].rearrange("(j o) -> j o", o=1),
                                  waits=[(sS, sS.n)], slot="lb")
                        bhw = P.dma("sync", VRf[:],
                              g["l1a_bvh"][h].rearrange("(o j) -> o j", o=1),
                              waits=[(sPo, sPo.n)], slot="lb")
                        P.dma("sync", gbh[:, 0:1],
                              g["l1a_gamma_t"][ch_h, pb_h:pb_h + 64]
                              .rearrange("(j o) -> j o", o=1),
                              waits=[(sV, sV.n)], slot="lb")
                        gbw = P.dma("sync", gbh[:, 1:2],
                              g["l1a_beta_t"][ch_h, pb_h:pb_h + 64]
                              .rearrange("(j o) -> j o", o=1), slot="lb")
                        t_vr = sPo.bump(1)
                        P.on("gpsimd", lambda e: e.tensor_copy(out=VRb[:], in_=VRf[:]),
                             waits=[bhw], inc=(sPo, 1))
                        t_ld1 = ten(lambda e: e.matmul(
                            ps[6][:, 0:1], c["c_ones1x128_f32"][:],
                            smr[0:1, 0:1] if False else ldr[0:1, h:h + 1],
                            start=True, stop=True),
                            waits=[(sS, t_pd5), (sV, sV.n)])
                        t_ld2 = dve(lambda e: (
                            e.tensor_copy(out=ldc[:], in_=ps[6][:, 0:1]),
                            e.drain(),
                            e.tensor_scalar(out=Bh[:], in0=c["c_iota2d"][:],
                                            scalar1=ldc[:], scalar2=None,
                                            op0=ALU.mult),
                            e.drain(),
                        )[-1], waits=[(sT, t_ld1)])
                        for tt in range(4):
                            t_mm = ten(lambda e, tt=tt: [
                                e.matmul(ps[4 + (tt % 2)][0:64, :],
                                         whd["g"][:, k * HD:(k + 1) * HD],
                                         xlnb[:, k * S + tt * 512:k * S + (tt + 1) * 512],
                                         start=(k == 0), stop=(k == NCH - 1))
                                for k in range(NCH)][-1],
                                waits=[hw["g"], xw, gbw,
                                       (sS, sS.n), (sV, sV.n)])
                            war[("whd", b, "g")] = (sT, t_mm)
                            act(lambda e, tt=tt: e.activation(
                                out=gsil[:, tt * 512:(tt + 1) * 512],
                                in_=ps[4 + (tt % 2)][0:64, :], func=AF.Silu,
                                bias=bh["g"][:]),
                                waits=[(sT, t_mm), gbw])
                        for nm, aug in (("q", Qaug), ("k", Kaug)):
                            for tt in range(4):
                                t_qk = ten(lambda e, nm=nm, tt=tt: [
                                    e.matmul(ps[tt][0:64, :],
                                             whd[nm][:, k * HD:(k + 1) * HD],
                                             xlnb[:, k * S + tt * 512:
                                                  k * S + (tt + 1) * 512],
                                             start=(k == 0), stop=(k == NCH - 1))
                                    for k in range(NCH)][-1],
                                    waits=[hw[nm], xw,
                                           (sS, sS.n), (sV, sV.n)])
                                war[("whd", b, nm)] = (sT, t_qk)
                            t_ev = act(lambda e, nm=nm, aug=aug: [
                                e.activation(out=aug[0:64, tt * 512:(tt + 1) * 512],
                                             in_=ps[tt][0:64, :], func=AF.Identity,
                                             bias=bh[nm][:])
                                for tt in range(4)][-1],
                                waits=[(sT, sT.n), gbw])
                            t_rt = ten(lambda e, aug=aug: [
                                e.matmul(ps[4 + tt][0:64, :], rc["c_rope_perm"][:],
                                         aug[0:64, tt * 512:(tt + 1) * 512],
                                         start=True, stop=True)
                                for tt in range(4)][-1],
                                waits=[(sS, t_ev), (sV, sV.n)])
                            dve(lambda e, aug=aug: (
                                e.tensor_tensor(out=Tt1[:], in0=aug[0:64, :],
                                                in1=rc["c_ropecos"][:], op=ALU.mult),
                            )[-1], waits=[(sS, t_ev)])
                            dve(lambda e, aug=aug: ([
                                e.tensor_tensor(
                                    out=Tt2[:, tt * 512:(tt + 1) * 512],
                                    in0=ps[4 + tt][0:64, :],
                                    in1=rc["c_ropesin"][:, tt * 512:(tt + 1) * 512],
                                    op=ALU.mult)
                                for tt in range(4)],
                                e.tensor_tensor(out=aug[0:64, :], in0=Tt1[:],
                                                in1=Tt2[:], op=ALU.add),
                            )[-1], waits=[(sT, t_rt)])
                        for tc in range(16):
                            bank = 4 + (tc % 4)
                            t_mm = ten(lambda e, tc=tc, bank=bank: (
                                [e.matmul(ps[bank][:, 0:HD],
                                          xlnb[:, k * S + tc * 128:k * S + tc * 128 + 128],
                                          whd["v"][:, k * HD:(k + 1) * HD],
                                          start=(k == 0), stop=False)
                                 for k in range(NCH)],
                                e.matmul(ps[bank][:, 0:HD], c["c_ones1x128"][:],
                                         VRb[:], start=False, stop=True),
                            )[-1], waits=[hw["v"], (sPo, t_vr), (sV, sV.n)])
                            war[("whd", b, "v")] = (sT, t_mm)
                            dve(lambda e, tc=tc, bank=bank: e.tensor_copy(
                                out=Vsb[:, tc * 65:tc * 65 + HD],
                                in_=ps[bank][:, 0:HD]),
                                waits=[(sT, t_mm)])
                        t_sq = act(lambda e: (
                            e.activation(out=Tt1[:], in_=Qaug[0:64, :],
                                         func=AF.Square),
                        )[-1], waits=[(sV, sV.n)])
                        t_cb = sPo.bump(1)
                        P.on("gpsimd", lambda e: e.tensor_copy(out=SqB[:], in_=Tt1[:]),
                             waits=[(sS, t_sq)], inc=(sPo, 1))
                        t_qn = ten(lambda e: [
                            e.matmul(ps[tt][0:1, :], c["c_onescol"][0:64, :],
                                     SqB[:, tt * 512:(tt + 1) * 512],
                                     start=True, stop=True)
                            for tt in range(4)][-1],
                            waits=[(sPo, t_cb)])
                        t_qs = act(lambda e: [
                            e.activation(out=mrow[0:1, tt * 512:(tt + 1) * 512],
                                         in_=ps[tt][0:1, :], func=AF.Sqrt)
                            for tt in range(4)][-1],
                            waits=[(sT, t_qn)])
                        t_sq2 = act(lambda e: (
                            e.activation(out=Tt1[:], in_=Kaug[0:64, :],
                                         func=AF.Square),
                        )[-1])
                        t_cb2 = sPo.bump(1)
                        P.on("gpsimd", lambda e: e.tensor_copy(out=SqB[:], in_=Tt1[:]),
                             waits=[(sS, t_sq2)], inc=(sPo, 1))
                        t_kn = ten(lambda e: [
                            e.matmul(ps[4 + tt][0:1, :], c["c_onescol"][0:64, :],
                                     SqB[:, tt * 512:(tt + 1) * 512],
                                     start=True, stop=True)
                            for tt in range(4)][-1],
                            waits=[(sPo, t_cb2)])
                        t_kmax = dve(lambda e: ([
                            e.tensor_copy(out=srow[0:1, tt * 512:(tt + 1) * 512],
                                          in_=ps[4 + tt][0:1, :])
                            for tt in range(4)],
                            e.drain(),
                            e.tensor_reduce(out=smr[0:1, 32:33], in_=srow[:],
                                            axis=AX.X, op=ALU.max),
                            e.drain(),
                        )[-1], waits=[(sT, t_kn)])
                        t_ks = act(lambda e: (
                            e.activation(out=smr[0:1, 32:33], in_=smr[0:1, 32:33],
                                         func=AF.Sqrt),
                            e.drain(),
                        )[-1], waits=[(sV, t_kmax)])
                        t_rows = dve(lambda e: (
                            e.tensor_scalar(out=mrow[:], in0=mrow[:],
                                            scalar1=smr[0:1, 32:33], scalar2=-1.0,
                                            op0=ALU.mult, op1=ALU.mult),
                            e.tensor_copy(out=Rb0[:], in_=mrow[:]),
                            e.tensor_scalar(out=srow[:], in0=rc["c_iota_row"][:],
                                            scalar1=ldc[0:1, 0:1], scalar2=-8.0,
                                            op0=ALU.mult, op1=ALU.mult),
                            e.tensor_copy(out=Rb1[:], in_=srow[:]),
                            e.drain(),
                            e.tensor_tensor(out=mrow[:], in0=srow[:], in1=Rb1[:],
                                            op=ALU.subtract),
                            e.tensor_copy(out=Rb2[:], in_=mrow[:]),
                            e.drain(),
                        )[-1], waits=[(sS, t_ks), (sV, t_ld2), kgw])
                        t_aug = None
                        for jj in range(4):
                            t_stk = ten(lambda e, jj=jj: (
                                e.matmul(ps[7][0:4, :], rc["c_eye4"][0:1, 0:4],
                                         Rb0[0:1, jj * 512:(jj + 1) * 512],
                                         start=True, stop=False),
                                e.matmul(ps[7][0:4, :], rc["c_eye4"][0:1, 4:8],
                                         Rb1[0:1, jj * 512:(jj + 1) * 512],
                                         start=False, stop=False),
                                e.matmul(ps[7][0:4, :], rc["c_eye4"][0:1, 8:12],
                                         Rb2[0:1, jj * 512:(jj + 1) * 512],
                                         start=False, stop=True),
                            )[-1], waits=[(sV, t_aug if t_aug else t_rows)])
                            t_aug = dve(lambda e, jj=jj: e.tensor_copy(
                                out=Qaug[64:68, jj * 512:(jj + 1) * 512],
                                in_=ps[7][0:4, :]), waits=[(sT, t_stk)])
                        for j in range(4):
                            for cc in range(16):
                                bank = (0, 1, 2, 6, 7)[cc % 5]
                                if cc < 4 * j:
                                    t_mm = ten(lambda e, cc=cc, j=j, bank=bank:
                                        e.matmul(ps[bank][:],
                                                 Kaug[0:65, cc * 128:(cc + 1) * 128],
                                                 Qaug[0:65, j * 512:(j + 1) * 512],
                                                 start=True, stop=True),
                                        waits=[(sV, t_aug), (sS, sS.n)])
                                    act(lambda e, cc=cc, bank=bank: e.activation(
                                        out=ATj[:, cc * 512:(cc + 1) * 512],
                                        in_=ps[bank][:], func=AF.Exp, scale=0.125),
                                        waits=[(sT, t_mm)])
                                elif cc >= 4 * (j + 1):
                                    t_mm = ten(lambda e, cc=cc, j=j, bank=bank:
                                        e.matmul(ps[bank][:],
                                                 Kaug[0:68, cc * 128:(cc + 1) * 128],
                                                 Qaug[0:68, j * 512:(j + 1) * 512],
                                                 start=True, stop=True),
                                        waits=[(sV, t_aug), (sS, sS.n)])
                                    act(lambda e, cc=cc, bank=bank: e.activation(
                                        out=ATj[:, cc * 512:(cc + 1) * 512],
                                        in_=ps[bank][:], func=AF.Exp, scale=0.125,
                                        bias=Bh[:, cc:cc + 1]),
                                        waits=[(sT, t_mm)])
                                else:
                                    for ssub in range(4):
                                        jq = 4 * j + ssub
                                        q0 = j * 512 + ssub * 128
                                        if jq < cc:
                                            t_mm = ten(lambda e, cc=cc, q0=q0, bank=bank:
                                                e.matmul(ps[bank][:, 0:128],
                                                         Kaug[0:68, cc * 128:(cc + 1) * 128],
                                                         Qaug[0:68, q0:q0 + 128],
                                                         start=True, stop=True),
                                                waits=[(sV, t_aug), (sS, sS.n)])
                                            act(lambda e, cc=cc, q0=q0, bank=bank, j=j:
                                                e.activation(
                                                    out=ATj[:, cc * 512 + (q0 - j * 512):
                                                            cc * 512 + (q0 - j * 512) + 128],
                                                    in_=ps[bank][:, 0:128],
                                                    func=AF.Exp, scale=0.125,
                                                    bias=Bh[:, cc:cc + 1]),
                                                waits=[(sT, t_mm)])
                                        elif jq > cc:
                                            t_mm = ten(lambda e, cc=cc, q0=q0, bank=bank:
                                                e.matmul(ps[bank][:, 0:128],
                                                         Kaug[0:65, cc * 128:(cc + 1) * 128],
                                                         Qaug[0:65, q0:q0 + 128],
                                                         start=True, stop=True),
                                                waits=[(sV, t_aug), (sS, sS.n)])
                                            act(lambda e, cc=cc, q0=q0, bank=bank, j=j:
                                                e.activation(
                                                    out=ATj[:, cc * 512 + (q0 - j * 512):
                                                            cc * 512 + (q0 - j * 512) + 128],
                                                    in_=ps[bank][:, 0:128],
                                                    func=AF.Exp, scale=0.125),
                                                waits=[(sT, t_mm)])
                                        else:
                                            t_mm = ten(lambda e, cc=cc, q0=q0:
                                                (e.matmul(ps[3][:, 0:128],
                                                          Kaug[0:68, cc * 128:(cc + 1) * 128],
                                                          Qaug[0:68, q0:q0 + 128],
                                                          start=True, stop=True),
                                                 e.matmul(ps[3][:, 128:256],
                                                          Kaug[0:65, cc * 128:(cc + 1) * 128],
                                                          Qaug[0:65, q0:q0 + 128],
                                                          start=True, stop=True),
                                                 )[-1],
                                                waits=[(sV, sV.n), (sS, sS.n)])
                                            t_ex = act(lambda e, cc=cc: (
                                                e.activation(out=dup[:],
                                                             in_=ps[3][:, 0:128],
                                                             func=AF.Exp, scale=0.125,
                                                             bias=Bh[:, cc:cc + 1]),
                                                e.activation(out=dlo[:],
                                                             in_=ps[3][:, 128:256],
                                                             func=AF.Exp, scale=0.125),
                                            )[-1], waits=[(sT, t_mm)])
                                            dve(lambda e, cc=cc, q0=q0, j=j:
                                                e.select(
                                                    out=ATj[:, cc * 512 + (q0 - j * 512):
                                                            cc * 512 + (q0 - j * 512) + 128],
                                                    mask=c["c_diagmask"][:],
                                                    on_true=dup[:], on_false=dlo[:]),
                                                waits=[(sS, t_ex)])
                            t_av = ten(lambda e, j=j: [
                                e.matmul(ps[4][0:65, :], Vsb[:, tc * 65:(tc + 1) * 65],
                                         ATj[:, tc * 512:(tc + 1) * 512],
                                         start=(tc == 0), stop=(tc == 15))
                                for tc in range(16)][-1],
                                waits=[(sS, sS.n), (sV, sV.n)])
                            nr = smr[0:1, 64:576]
                            t_nr = act(lambda e: (
                                e.activation(out=nr, in_=ps[4][64:65, :], func=AF.Ln),
                                e.activation(out=nr, in_=nr, func=AF.Exp, scale=-1.0),
                            )[-1], waits=[(sT, t_av)])
                            t_nb = ten(lambda e: e.matmul(
                                ps[5][0:64, :], c["c_ones1x128_f32"][0:1, 0:64], nr,
                                start=True, stop=True),
                                waits=[(sS, t_nr)])
                            dve(lambda e, j=j: (
                                e.tensor_copy(out=Tt1[:, 0:512], in_=ps[4][0:64, :]),
                                e.tensor_tensor(out=ogh[:, j * 512:(j + 1) * 512],
                                                in0=Tt1[:, 0:512], in1=ps[5][0:64, :],
                                                op=ALU.mult),
                            )[-1], waits=[(sT, t_nb)])
                        t_st = dve(lambda e: (
                            e.tensor_reduce(out=sm[0:64, 0:1], in_=ogh[:],
                                            axis=AX.X, op=ALU.add),
                            e.drain(),
                        )[-1])
                        t_st2 = act(lambda e: (
                            e.activation(out=Tt1[:], in_=ogh[:], func=AF.Square,
                                         accum_out=sm[0:64, 1:2]),
                            e.drain(),
                        )[-1], waits=[(sV, t_st)])
                        t_st3 = ten(lambda e: e.matmul(
                            ps[6][0:1, 0:2], c["c_onescol_f32"][0:64, :],
                            sm[0:64, 0:2], start=True, stop=True),
                            waits=[(sS, t_st2)])
                        t_st4 = dve(lambda e: (
                            e.tensor_scalar(out=smr[0:1, 34:36], in0=ps[6][0:1, 0:2],
                                            scalar1=1.0 / (S * HD), scalar2=None,
                                            op0=ALU.mult),
                            e.drain(),
                            e.tensor_tensor(out=smr[0:1, 36:37], in0=smr[0:1, 34:35],
                                            in1=smr[0:1, 34:35], op=ALU.mult),
                            e.drain(),
                            e.tensor_tensor(out=smr[0:1, 35:36], in0=smr[0:1, 35:36],
                                            in1=smr[0:1, 36:37], op=ALU.subtract),
                            e.drain(),
                            e.tensor_scalar(out=smr[0:1, 35:36], in0=smr[0:1, 35:36],
                                            scalar1=1e-5, scalar2=None, op0=ALU.add),
                            e.drain(),
                        )[-1], waits=[(sT, t_st3)])
                        t_st5 = act(lambda e: (
                            e.activation(out=smr[0:1, 35:36], in_=smr[0:1, 35:36],
                                         func=AF.Ln),
                            e.drain(),
                            e.activation(out=smr[0:1, 35:36], in_=smr[0:1, 35:36],
                                         func=AF.Exp, scale=-0.5),
                            e.drain(),
                        )[-1], waits=[(sV, t_st4)])
                        t_bc2 = ten(lambda e: e.matmul(
                            ps[7][0:64, 0:2], c["c_ones1x128_f32"][0:1, 0:64],
                            smr[0:1, 34:36], start=True, stop=True),
                            waits=[(sS, t_st5)])
                        t_ab2 = dve(lambda e, ch_h=ch_h, pb_h=pb_h: (
                            e.tensor_copy(out=mb2[:, 0:2], in_=ps[7][0:64, 0:2]),
                            e.drain(),
                            e.tensor_tensor(out=mb2[:, 2:3], in0=mb2[:, 1:2],
                                            in1=gbh[:, 0:1], op=ALU.mult),
                            e.drain(),
                            e.tensor_tensor(out=mb2[:, 3:4], in0=mb2[:, 0:1],
                                            in1=mb2[:, 2:3], op=ALU.mult),
                            e.drain(),
                            e.scalar_tensor_tensor(out=mb2[:, 3:4], in0=mb2[:, 3:4],
                                                   scalar=-1.0,
                                                   in1=gbh[:, 1:2],
                                                   op0=ALU.mult, op1=ALU.add),
                            e.drain(),
                            e.tensor_scalar(out=Tt1[:], in0=ogh[:],
                                            scalar1=mb2[:, 2:3], scalar2=mb2[:, 3:4],
                                            op0=ALU.mult, op1=ALU.add),
                            e.tensor_tensor(
                                out=zb[pb_h:pb_h + 64, ch_h * S:(ch_h + 1) * S],
                                in0=Tt1[:], in1=gsil[:], op=ALU.mult),
                        )[-1], waits=[(sT, t_bc2), gbw])
                P.barrier()
                with contextlib.ExitStack() as stk:
                    ps = K.ps8(stk, f"l1wo{b}")
                    wsl = [K.sb(stk, f"l1a_wo{i}{b}", [128, NCH * 128], BF16)
                           for i in range(8)]
                    Bout = [K.sb(stk, f"l1a_bo2{i}{b}", [128, 8 * 512], F32)
                            for i in range(2)]
                    bd = Boundary(K, stk, f"l1a_bd{b}", ps[4:8])
                    for mo in range(8):
                        wow = P.dma("gpsimd",
                              wsl[mo][:].rearrange("p (k j) -> p k j", k=NCH),
                              g["l1a_wot"][:, mo * 128:(mo + 1) * 128]
                              .rearrange("(k p) j -> p k j", p=128), slot="lo")
                    for tt in range(4):
                        Bo = Bout[tt % 2]
                        t_evs = []
                        for wave in range(2):
                            K.td.flush_one()
                            K.td.flush_one()
                            def mmo(e, tt=tt, wave=wave):
                                r = None
                                for mi in range(4):
                                    mo = wave * 4 + mi
                                    for k in range(NCH):
                                        r = e.matmul(
                                            ps[mi][:],
                                            wsl[mo][:, k * 128:(k + 1) * 128],
                                            zb[:, k * S + tt * 512:
                                               k * S + (tt + 1) * 512],
                                            start=(k == 0), stop=(k == NCH - 1))
                                return r
                            bw = war.get(("wobank", b))
                            t_mm = sT.bump(1)
                            P.on("tensor", mmo,
                                 waits=[wow] + ([bw] if bw else []),
                                 inc=(sT, 1))
                            t_ev = sV.bump(1)
                            def evo(e, wave=wave, Bo=Bo):
                                r = None
                                for mi in range(4):
                                    m = wave * 4 + mi
                                    r = e.tensor_scalar(
                                        out=Bo[:, m * 512:(m + 1) * 512],
                                        in0=ps[mi][:], scalar1=vec["bo"][:, m:m + 1],
                                        scalar2=None, op0=ALU.add)
                                return r
                            bo_w = war.get(("wobout", b, tt % 2))
                            P.on("vector", evo,
                                 waits=[(sT, t_mm)] + ([bo_w] if bo_w else []),
                                 inc=(sV, 1))
                            war[("wobank", b)] = (sV, t_ev)
                            t_evs.append(t_ev)
                        Bt = [Bo[:, m * 512:(m + 1) * 512] for m in range(NCH)]
                        rg = bd.run_tt(b, tt, Bt, (sV, t_evs[-1]), resid_in,
                                       res_out, ln_out, affine=None, final=False)
                        war[("wobout", b, tt % 2)] = rg
                    K.td.flush()
                    P.barrier()
    P.barrier()



def build_program(phases, stage_ext=None):
    nc = bass.Bass("TRN2", target_bir_lowering=False)
    stack = contextlib.ExitStack()
    P = Prog(nc, stack)
    K = Kern(nc, P, stack)
    K.td = TDefer(P)
    declare_io(K, stage_ext)
    with contextlib.ExitStack() as cstk:
        load_consts(K, cstk)
        for ph in phases:
            if ph == "ln0":
                phase_ln0(K)
            elif ph == "l0a":
                phase_l0attn(K, "x0", "xln", "xres", "xln")
            elif ph == "l0f":
                phase_ffn(K, "l0f", "xres", "xln", "xres2", "xln2")
            elif ph == "l1a":
                phase_l1attn(K, "xres2", "xln2", "xres", "xln")
            elif ph == "l1f":
                phase_ffn(K, "l1f", "xres", "xln", None, None, final=True)
            else:
                raise ValueError(ph)
        P.finalize()
    stack.close()
    return nc


ALL_PHASES = ["ln0", "l0a", "l0f", "l1a", "l1f"]


def kernel(**inputs):
    maps = prep_host(inputs)
    nc = build_program(ALL_PHASES)
    res = run_bass_kernel_spmd(nc, maps, core_ids=list(range(8)))
    out = np.empty((16, S, D), np.float32)
    for cid in range(8):
        o = np.asarray(res.results[cid]["out"], np.float32)
        out[2 * cid:2 * cid + 2] = o.reshape(D, NB, S).transpose(1, 2, 0)
    return out
